# version 06
# baseline (speedup 1.0000x reference)
import contextlib
import math
import numpy as np
import ml_dtypes

import concourse.bass as bass
import concourse.mybir as mybir
from concourse.bass_utils import run_bass_kernel_spmd

F32 = mybir.dt.float32
BF16 = mybir.dt.bfloat16
U8 = mybir.dt.uint8
AF = mybir.ActivationFunctionType
ALU = mybir.AluOpType
AX = mybir.AxisListType

S = 2048
D = 4096
NCH = 32
NTT = 16
NTB = 4
HD = 128
SM = 1.0 / math.sqrt(HD)
EPS = 1e-5
ENGINES = ("sync", "scalar", "vector", "tensor", "gpsimd")

L0 = dict(name="l0", n_q=32, n_kv=8, window=1024, dk=1024, dv=1024, ffn="moe")
L1 = dict(name="l1", n_q=32, n_kv=32, window=None, dk=4096, dv=4096, ffn="bin")



def host_consts():
    half = 64
    inv = 10000.0 ** (-np.arange(half, dtype=np.float64) / half)
    ang = np.arange(S, dtype=np.float64)[None, :] * inv[:, None]
    cos = np.cos(ang).astype(np.float32)
    sin = np.sin(ang).astype(np.float32)
    cc = np.concatenate([cos, cos], axis=0).astype(ml_dtypes.bfloat16)
    ss = np.concatenate([-sin, sin], axis=0).astype(ml_dtypes.bfloat16)
    ccq = (np.concatenate([cos, cos], axis=0) * SM).astype(ml_dtypes.bfloat16)
    ssq = (np.concatenate([-sin, sin], axis=0) * SM).astype(ml_dtypes.bfloat16)

    ident = np.eye(128, dtype=ml_dtypes.bfloat16)
    r = np.arange(128)
    mask_diag = (r[:, None] <= r[None, :]).astype(ml_dtypes.bfloat16)
    mask_left = (r[:, None] > r[None, :]).astype(ml_dtypes.bfloat16)
    swap64 = np.zeros((128, 128), dtype=ml_dtypes.bfloat16)
    swap64[r, (r + 64) % 128] = 1.0
    consts4 = np.ascontiguousarray(
        np.stack([ident, mask_diag, mask_left, swap64], axis=1))
    return dict(cc=cc, ss=ss, ccq=ccq, ssq=ssq, consts4=consts4)


def perm_interleave():
    p = np.zeros(D, dtype=np.int64)
    for h in range(32):
        base = 128 * h
        for j in range(64):
            p[base + j] = base + 2 * j
            p[base + 64 + j] = base + 2 * j + 1
    return p



class Builder:
    def __init__(self, fused):
        self.nc = bass.Bass("TRN2", target_bir_lowering=False)
        self.es = contextlib.ExitStack()
        self.fused = fused
        self.phase_idx = 0
        self.dram = {}
        self._sem_n = 0
        self._sem_pool = []
        self._phase_sems = []
        self._nm = 0

    def sem(self, name):
        if self.fused and self._sem_pool:
            h = self._sem_pool.pop()
        else:
            self._sem_n += 1
            h = self.es.enter_context(self.nc.semaphore(f"{name}_{self._sem_n}"))
        self._phase_sems.append(h)
        return h

    def dram_in(self, name, shape, dtype):
        if name not in self.dram:
            self.dram[name] = self.nc.dram_tensor(name, shape, dtype,
                                                  kind="ExternalInput").ap()
        return self.dram[name]

    def dram_tmp(self, name, shape, dtype, out=False, inp=False):
        if name in self.dram:
            return self.dram[name]
        if self.fused:
            kind = "Internal"
        else:
            kind = "ExternalOutput" if out else ("ExternalInput" if inp else "Internal")
        self.dram[name] = self.nc.dram_tensor(name, shape, dtype, kind=kind).ap()
        return self.dram[name]

    def dram_out(self, name, shape, dtype):
        if name not in self.dram:
            self.dram[name] = self.nc.dram_tensor(name, shape, dtype,
                                                  kind="ExternalOutput").ap()
        return self.dram[name]

    def alloc_fixed(self):
        nc = self.nc
        self.big_t = nc.alloc_sbuf_tensor("big", [128, NCH, S], BF16)
        self.big = self.big_t.ap()
        self.arenaA = nc.alloc_sbuf_tensor("arenaA", [128, 49152], U8)
        self.arenaB = nc.alloc_sbuf_tensor("arenaB", [128, 14336], U8)
        self.cc = nc.alloc_sbuf_tensor("cc_sb", [128, S], BF16).ap()
        self.ss = nc.alloc_sbuf_tensor("ss_sb", [128, S], BF16).ap()
        self.ccq = nc.alloc_sbuf_tensor("ccq_sb", [128, S], BF16).ap()
        self.ssq = nc.alloc_sbuf_tensor("ssq_sb", [128, S], BF16).ap()
        self.consts4 = nc.alloc_sbuf_tensor("consts4_sb", [128, 4, 128], BF16).ap()
        self.smalls = nc.alloc_sbuf_tensor("smalls", [128, 64], F32).ap()
        self.onesrow = nc.alloc_sbuf_tensor("onesrow", [1, 128], BF16).ap()
        self.ident = self.consts4[:, 0, :]
        self.mask_diag = self.consts4[:, 1, :]
        self.mask_left = self.consts4[:, 2, :]
        self.swap64 = self.consts4[:, 3, :]
        self.eps = self.smalls[:, 0:1]
        self.bar = self.es.enter_context(nc.semaphore("bar"))
        self.block = self.es.enter_context(nc.Block())

    def uname(self, base):
        self._nm += 1
        return f"{base}_{self._nm}"

    def carve(self, arena, off_bytes, shape, dtype):
        bs = mybir.dt.size(dtype)
        assert off_bytes % bs == 0
        size = int(np.prod(shape[1:]))
        start = off_bytes // bs
        t = arena.bitcast(dtype)
        ap = t.ap()[0:int(shape[0]), start:start + size]
        if len(shape) > 2:
            names = " ".join(f"d{i}" for i in range(1, len(shape)))
            kw = {f"d{i}": int(shape[i]) for i in range(1, len(shape))}
            ap = ap.rearrange(f"p ({names}) -> p {names}", **kw)
        return ap

    def tpose(self, eng, out, in_, ident=None):
        eng.wait_ge(self.bar, 0)
        return eng.transpose(out, in_, self.ident if ident is None else ident)

    def tpose_end(self, eng):
        eng.wait_ge(self.bar, 0)

    def emit_phase(self, bodies):
        k = self.phase_idx
        bar = self.bar
        phase_sems = list(self._phase_sems)
        for ename in ENGINES:
            fn = bodies.get(ename)

            def wrapper(eng, fn=fn, k=k, ename=ename):
                if k > 0:
                    eng.wait_ge(bar, 6 * k)
                if fn is not None:
                    fn(eng)
                if ename == "sync":
                    eng.sem_inc(bar, 1)
                else:
                    eng.drain().then_inc(bar, 1)
                if ename == "vector":
                    eng.wait_ge(bar, 6 * k + 5)
                    for s in phase_sems:
                        eng.sem_clear(s)
                    eng.sem_inc(bar, 1)

            getattr(self.block, ename)(wrapper)
        self.phase_idx += 1
        self._sem_pool.extend(self._phase_sems)
        self._phase_sems = []

    def phase_consts(self):
        cc_in = self.dram_in("cc", [128, S], BF16)
        ss_in = self.dram_in("ss", [128, S], BF16)
        ccq_in = self.dram_in("ccq", [128, S], BF16)
        ssq_in = self.dram_in("ssq", [128, S], BF16)
        c4_in = self.dram_in("consts4", [128, 4, 128], BF16)
        s_cl = self.sem("cl")

        def b_sync(eng):
            eng.dma_start(out=self.cc, in_=cc_in).then_inc(s_cl, 16)
            eng.dma_start(out=self.ss, in_=ss_in).then_inc(s_cl, 16)
            eng.dma_start(out=self.ccq, in_=ccq_in).then_inc(s_cl, 16)
            eng.dma_start(out=self.ssq, in_=ssq_in).then_inc(s_cl, 16)
            eng.dma_start(out=self.consts4, in_=c4_in).then_inc(s_cl, 16)
            eng.wait_ge(s_cl, 80)

        def b_vector(eng):
            eng.memset(self.eps, EPS)
            eng.memset(self.onesrow, 1.0)

        self.emit_phase(dict(sync=b_sync, vector=b_vector))

    def phase_pa(self, src, dump_xnt=False):
        nc = self.nc
        xtile = self.carve(self.arenaA, 0, [128, 2, D], F32)
        xn = self.carve(self.arenaA, 32768, [128, D], BF16)
        sqtrash = self.carve(self.arenaA, 40960, [128, D], BF16)
        stats = self.carve(self.arenaB, 0, [128, 2, 8], F32)

        s_ld = [self.sem("pa_ld"), self.sem("pa_ld")]
        s_sq = self.sem("pa_sq")
        s_var = self.sem("pa_var")
        s_rs = self.sem("pa_rs")
        s_nm = self.sem("pa_nm")
        s_tp = self.sem("pa_tp")
        s_ev = self.sem("pa_ev")
        s_st = self.sem("pa_st")

        with nc.psum_tensor(self.uname("pa_tpA"), [128, 4, 128], BF16) as tpA, \
             nc.psum_tensor(self.uname("pa_tpB"), [128, 4, 128], BF16) as tpB:
            tp = [tpA.ap(), tpB.ap()]

            def b_sync(eng):
                for tt in range(NTT):
                    if tt >= 2:
                        eng.wait_ge(s_nm, tt - 1)
                    eng.dma_start(out=xtile[:, tt % 2, :],
                                  in_=src[tt * 128:(tt + 1) * 128, :]
                                  ).then_inc(s_ld[tt % 2], 16)

            def b_scalar(eng):
                for tt in range(NTT):
                    eng.wait_ge(s_ld[tt % 2], 16 * (tt // 2 + 1))
                    eng.activation(sqtrash[:, :], xtile[:, tt % 2, :], AF.Square,
                                   accum_out=stats[:, tt % 2, 1:2])
                    eng.drain().then_inc(s_sq, 1)
                    eng.wait_ge(s_var, tt + 1)
                    eng.activation(stats[:, tt % 2, 4:5], stats[:, tt % 2, 3:4],
                                   AF.Ln, bias=self.eps)
                    eng.activation(stats[:, tt % 2, 4:5], stats[:, tt % 2, 4:5],
                                   AF.Exp, scale=-0.5)
                    eng.drain().then_inc(s_rs, 1)
                if dump_xnt:
                    dst = self.dram_tmp("xnt_s", [D, S], BF16, out=True)
                    eng.wait_ge(s_ev, NTT * 8)
                    for ic in range(NCH):
                        eng.dma_start(out=dst[ic * 128:(ic + 1) * 128, :],
                                      in_=self.big[:, ic, :]).then_inc(s_st, 16)
                    eng.wait_ge(s_st, 16 * NCH)

            def b_vector(eng):
                for tt in range(NTT):
                    eng.wait_ge(s_ld[tt % 2], 16 * (tt // 2 + 1))
                    eng.tensor_reduce(out=stats[:, tt % 2, 0:1],
                                      in_=xtile[:, tt % 2, :], axis=AX.X, op=ALU.add)
                    eng.wait_ge(s_sq, tt + 1)
                    eng.drain()
                    eng.tensor_scalar(out=stats[:, tt % 2, 2:3],
                                      in0=stats[:, tt % 2, 0:1],
                                      scalar1=1.0 / D, scalar2=None, op0=ALU.mult)
                    eng.tensor_scalar(out=stats[:, tt % 2, 3:4],
                                      in0=stats[:, tt % 2, 1:2],
                                      scalar1=1.0 / D, scalar2=None, op0=ALU.mult)
                    eng.drain()
                    eng.scalar_tensor_tensor(out=stats[:, tt % 2, 5:6],
                                             in0=stats[:, tt % 2, 2:3], scalar=-1.0,
                                             in1=stats[:, tt % 2, 2:3],
                                             op0=ALU.mult, op1=ALU.mult)
                    eng.drain()
                    eng.tensor_add(stats[:, tt % 2, 3:4], stats[:, tt % 2, 3:4],
                                   stats[:, tt % 2, 5:6])
                    eng.drain().then_inc(s_var, 1)
                    eng.wait_ge(s_rs, tt + 1)
                    if tt >= 1:
                        eng.wait_ge(s_tp, 8 * tt)
                    eng.tensor_scalar(out=xn[:, :], in0=xtile[:, tt % 2, :],
                                      scalar1=stats[:, tt % 2, 2:3],
                                      scalar2=stats[:, tt % 2, 4:5],
                                      op0=ALU.subtract, op1=ALU.mult)
                    eng.drain().then_inc(s_nm, 1)
                    for p in range(8):
                        gidx = tt * 8 + p
                        eng.wait_ge(s_tp, gidx + 1)
                        eng.tensor_copy(
                            self.big[:, p * 4:p * 4 + 4, tt * 128:(tt + 1) * 128],
                            tp[p % 2][:, :, :])
                        eng.drain().then_inc(s_ev, 1)

            def b_tensor(eng):
                for tt in range(NTT):
                    eng.wait_ge(s_nm, tt + 1)
                    for p in range(8):
                        gidx = tt * 8 + p
                        if gidx >= 2:
                            eng.wait_ge(s_ev, gidx - 1)
                        for j in range(4):
                            ic = p * 4 + j
                            ii = self.tpose(eng, tp[p % 2][:, j, :],
                                            xn[:, ic * 128:(ic + 1) * 128])
                        ii.then_inc(s_tp, 1)
                        self.tpose_end(eng)

            self.emit_phase(dict(sync=b_sync, scalar=b_scalar, vector=b_vector,
                                 tensor=b_tensor))

    def phase_pb(self, L, load_xnt=False, only=None):
        nc = self.nc
        ln = L["name"]
        wq = self.dram_in(f"{ln}_wq", [D, D], F32)
        wk = self.dram_in(f"{ln}_wk", [D, L["dk"]], F32)
        wv = self.dram_in(f"{ln}_wv", [D, L["dv"]], F32)
        qt_s = self.dram_tmp("qt_s", [D, S], BF16, out=True)
        kt_s = self.dram_tmp("kt_s", [D, S], BF16, out=True)
        v_s = self.dram_tmp("v_s", [S, D], BF16, out=True)

        GW = 256
        wbuf = self.carve(self.arenaA, 0, [128, 2, NCH, GW], BF16)
        qk_sb = self.carve(self.arenaA, 32768, [128, 2, S], BF16)
        rope_t = self.carve(self.arenaA, 40960, [128, 2, 512], BF16)
        rope_sw = self.carve(self.arenaA, 43008, [128, 2, 512], BF16)
        rope_m = self.carve(self.arenaA, 45056, [128, 2, 512], BF16)
        v_sb = self.carve(self.arenaA, 47104, [128, 2, GW], BF16)
        stage = self.carve(self.arenaB, 0, [128, 4, GW], F32)

        s_wl = [self.sem("pb_wl") for _ in range(4)]
        s_wc = self.sem("pb_wc")
        s_wcf = self.sem("pb_wcf")
        s_peg = self.sem("pb_peg")
        s_pet = self.sem("pb_pet")
        s_pesw = self.sem("pb_pesw")
        s_ev1 = self.sem("pb_ev1")
        s_ev2 = self.sem("pb_ev2")
        s_rp = self.sem("pb_rp")
        s_rst = [self.sem("pb_rst"), self.sem("pb_rst")]
        s_vst = [self.sem("pb_vst"), self.sem("pb_vst")]
        s_ldb = self.sem("pb_ldb")

        projs = [(wq, D, "fm", qt_s), (wk, L["dk"], "fm", kt_s),
                 (wv, L["dv"], "tm", v_s)]
        if only == "v":
            projs = projs[2:]
        elif only == "q":
            projs = projs[:1]
        elif only == "qk":
            projs = projs[:2]
        NSUB = GW // 128

        groups = []
        chunks = []
        for pi, (w, dout, kind, dst) in enumerate(projs):
            for g in range(dout // GW):
                groups.append((pi, g))
                for ic in range(NCH):
                    chunks.append((len(groups) - 1, pi, g, ic))

        tiles = []
        for gi, (pi, g) in enumerate(groups):
            kind = projs[pi][2]
            if kind == "fm":
                for sub in range(NSUB):
                    for tb in range(NTB):
                        tiles.append(dict(gi=gi, kind="fm", pi=pi, g=g,
                                          sub=sub, tb=tb))
            else:
                for tt in range(NTT):
                    tiles.append(dict(gi=gi, kind="tm", pi=pi, g=g, tt=tt))

        fm_tiles = [ti for ti, t in enumerate(tiles) if t["kind"] == "fm"]
        fmidx = {ti: fi for fi, ti in enumerate(fm_tiles)}
        nfm = len(fm_tiles)
        rowidx = {}
        nrow = -1
        row_last_fi = {}
        for ti in fm_tiles:
            t = tiles[ti]
            if t["tb"] == 0:
                nrow += 1
            rowidx[ti] = nrow
            if t["tb"] == NTB - 1:
                row_last_fi[nrow] = fmidx[ti]
        row_info = {}
        for ti in fm_tiles:
            t = tiles[ti]
            if t["tb"] == NTB - 1:
                r = rowidx[ti]
                row_info[r] = (t["pi"], t["g"] * NSUB + t["sub"])
        vrow_of = {}
        nv = -1
        for ti, t in enumerate(tiles):
            if t["kind"] == "tm":
                nv += 1
                vrow_of[ti] = nv

        def b_sync(eng):
            if load_xnt and not self.fused:
                srcb = self.dram_tmp("xnt_s", [D, S], BF16, inp=True)
                for ic in range(NCH):
                    eng.dma_start(out=self.big[:, ic, :],
                                  in_=srcb[ic * 128:(ic + 1) * 128, :]
                                  ).then_inc(s_ldb, 16)
            for j, (gi, pi, g, ic) in enumerate(chunks):
                w = projs[pi][0]
                if j >= 4:
                    eng.wait_ge(s_wcf, j - 3)
                eng.dma_start(out=stage[:, j % 4, :],
                              in_=w[ic * 128:(ic + 1) * 128, g * GW:(g + 1) * GW]
                              ).then_inc(s_wl[j % 4], 16)

        def b_gpsimd(eng):
            for j, (gi, pi, g, ic) in enumerate(chunks):
                if gi >= 2 and ic == 0:
                    eng.wait_ge(s_peg, gi - 1)
                eng.wait_ge(s_wl[j % 4], 16 * (j // 4 + 1))
                eng.tensor_copy(wbuf[:, gi % 2, ic, :],
                                stage[:, j % 4, :]).then_inc(s_wcf, 1)
                if ic == NCH - 1:
                    eng.drain().then_inc(s_wc, 1)

        with nc.psum_tensor(self.uname("pb_ps"), [128, 4, 512], F32) as ps_t, \
             nc.psum_tensor(self.uname("pb_psw"), [128, 2, 512], F32) as psw_t:
            psum = ps_t.ap()
            psw = psw_t.ap()

            def pe_swap(eng, fi):
                eng.wait_ge(s_ev1, ev1_count_of_fm[fi])
                if fi >= 2:
                    eng.wait_ge(s_ev2, fi - 1)
                eng.matmul(psw[:, fi % 2, :], self.swap64, rope_t[:, fi % 2, :],
                           start=True, stop=True).then_inc(s_pesw, 1)

            def b_tensor(eng):
                if load_xnt and not self.fused:
                    eng.wait_ge(s_ldb, 16 * NCH)
                for ti, t in enumerate(tiles):
                    if t["kind"] == "tm" and ti > 0 and tiles[ti - 1]["kind"] == "fm":
                        pe_swap(eng, nfm - 1)
                    eng.wait_ge(s_wc, t["gi"] + 1)
                    if ti >= 4:
                        eng.wait_ge(s_ev1, ti - 3)
                    if t["kind"] == "fm":
                        ps = psum[:, ti % 4, :]
                        for ic in range(NCH):
                            ii = eng.matmul(
                                ps,
                                wbuf[:, t["gi"] % 2, ic,
                                     t["sub"] * 128:(t["sub"] + 1) * 128],
                                self.big[:, ic, t["tb"] * 512:(t["tb"] + 1) * 512],
                                start=(ic == 0), stop=(ic == NCH - 1))
                    else:
                        ps = psum[:, ti % 4, 0:GW]
                        tt = t["tt"]
                        for ic in range(NCH):
                            ii = eng.matmul(
                                ps,
                                self.big[:, ic, tt * 128:(tt + 1) * 128],
                                wbuf[:, t["gi"] % 2, ic, :],
                                start=(ic == 0), stop=(ic == NCH - 1))
                    ii.then_inc(s_pet, 1)
                    if t["kind"] == "fm" and fmidx[ti] >= 1:
                        pe_swap(eng, fmidx[ti] - 1)
                    if ti + 1 == len(tiles) or tiles[ti + 1]["gi"] != t["gi"]:
                        eng.drain().then_inc(s_peg, 1)
                if tiles[-1]["kind"] == "fm":
                    pe_swap(eng, nfm - 1)

            ev1_count_of_fm = {}
            for ti in fm_tiles:
                ev1_count_of_fm[fmidx[ti]] = ti + 1

            def act_evac2(eng, fi):
                eng.wait_ge(s_pesw, fi + 1)
                if fi >= 2:
                    eng.wait_ge(s_rp, fi - 1)
                eng.activation(rope_sw[:, fi % 2, :], psw[:, fi % 2, :], AF.Copy)
                eng.drain().then_inc(s_ev2, 1)

            def b_scalar(eng):
                nrow_st = [0, 0]
                nv_st = [0, 0]
                pending_rows = []
                for ti, t in enumerate(tiles):
                    if t["kind"] == "tm" and ti > 0 and tiles[ti - 1]["kind"] == "fm":
                        act_evac2(eng, nfm - 1)
                        flush_rows(eng, nfm, nrow_st, pending_rows)
                    eng.wait_ge(s_pet, ti + 1)
                    ps = psum[:, ti % 4, :] if t["kind"] == "fm" else psum[:, ti % 4, 0:GW]
                    if t["kind"] == "fm":
                        fi = fmidx[ti]
                        if fi >= 2:
                            eng.wait_ge(s_rp, fi - 1)
                        eng.activation(rope_t[:, fi % 2, :], ps, AF.Copy)
                        eng.drain().then_inc(s_ev1, 1)
                        if fi >= 1:
                            act_evac2(eng, fi - 1)
                        if t["tb"] == NTB - 1:
                            pending_rows.append(rowidx[ti])
                        flush_rows(eng, fi, nrow_st, pending_rows)
                    else:
                        vr = vrow_of[ti]
                        if vr >= 2:
                            eng.wait_ge(s_vst[vr % 2], 16 * (vr // 2))
                        eng.activation(v_sb[:, vr % 2, :], ps, AF.Copy)
                        eng.drain().then_inc(s_ev1, 1)
                        tt = t["tt"]
                        dst = projs[t["pi"]][3]
                        eng.dma_start(
                            out=dst[tt * 128:(tt + 1) * 128,
                                    t["g"] * GW:(t["g"] + 1) * GW],
                            in_=v_sb[:, vr % 2, :]).then_inc(s_vst[vr % 2], 16)
                        nv_st[vr % 2] += 1
                if tiles[-1]["kind"] == "fm":
                    act_evac2(eng, nfm - 1)
                    flush_rows(eng, nfm, nrow_st, pending_rows)
                eng.wait_ge(s_rst[0], 16 * nrow_st[0])
                eng.wait_ge(s_rst[1], 16 * nrow_st[1])
                eng.wait_ge(s_vst[0], 16 * nv_st[0])
                eng.wait_ge(s_vst[1], 16 * nv_st[1])

            def flush_rows(eng, ropes_emitted, nrow_st, pending_rows):
                while pending_rows and row_last_fi[pending_rows[0]] + 1 <= ropes_emitted:
                    r = pending_rows.pop(0)
                    eng.wait_ge(s_rp, row_last_fi[r] + 1)
                    pi, dchunk = row_info[r]
                    dst = projs[pi][3]
                    eng.dma_start(out=dst[dchunk * 128:(dchunk + 1) * 128, :],
                                  in_=qk_sb[:, r % 2, :]).then_inc(s_rst[r % 2], 16)
                    nrow_st[r % 2] += 1

            def dve_rope(eng, fi):
                ti = fm_tiles[fi]
                t = tiles[ti]
                r = rowidx[ti]
                tb = t["tb"]
                eng.wait_ge(s_ev2, fi + 1)
                if tb == 0 and r >= 2:
                    eng.wait_ge(s_rst[r % 2], 16 * (r // 2))
                if t["pi"] == 0:
                    cslice = self.ccq[:, tb * 512:(tb + 1) * 512]
                    sslice = self.ssq[:, tb * 512:(tb + 1) * 512]
                else:
                    cslice = self.cc[:, tb * 512:(tb + 1) * 512]
                    sslice = self.ss[:, tb * 512:(tb + 1) * 512]
                eng.tensor_mul(rope_m[:, 0, :], rope_t[:, fi % 2, :], cslice)
                eng.tensor_mul(rope_m[:, 1, :], rope_sw[:, fi % 2, :], sslice)
                eng.drain()
                eng.tensor_add(qk_sb[:, r % 2, tb * 512:(tb + 1) * 512],
                               rope_m[:, 0, :], rope_m[:, 1, :])
                eng.drain().then_inc(s_rp, 1)

            def b_vector(eng):
                for fi in range(1, nfm):
                    dve_rope(eng, fi - 1)
                if nfm:
                    dve_rope(eng, nfm - 1)

            self.emit_phase(dict(sync=b_sync, gpsimd=b_gpsimd, tensor=b_tensor,
                                 scalar=b_scalar, vector=b_vector))

    def phase_pf_bin(self, L, load_xnt=False):
        nc = self.nc
        ln = L["name"]
        wr = self.dram_in(f"{ln}_wreal", [D, D], F32)
        bias = self.dram_in(f"{ln}_bias", [1, D], F32)
        delta_s = self.dram_tmp("delta_s", [S, D], F32, out=True)

        wtb = self.carve(self.arenaA, 0, [128, 2, NCH, 256], BF16)
        sgn_tmp = self.carve(self.arenaA, 32768, [128, D], BF16)
        biasb = self.carve(self.arenaA, 40960, [1, D], BF16)
        stage = self.carve(self.arenaB, 0, [128, 2, 1024], F32)
        dsb = self.carve(self.arenaB, 8192, [128, 2, 256], F32)
        al = self.smalls

        s_wld = [self.sem("pf_wld"), self.sem("pf_wld")]
        s_bld = self.sem("pf_bld")
        s_alq = self.sem("pf_alq")
        s_sc = self.sem("pf_sc")
        s_sg = self.sem("pf_sg")
        s_tpk = self.sem("pf_tpk")
        s_te = self.sem("pf_te")
        s_mm = self.sem("pf_mm")
        s_de = self.sem("pf_de")
        s_dst = [self.sem("pf_dst"), self.sem("pf_dst")]
        s_ldb = self.sem("pf_ldb")

        NOT_ = NCH

        with nc.psum_tensor(self.uname("pf_tpA"), [128, 4, 128], BF16) as tpA, \
             nc.psum_tensor(self.uname("pf_tpB"), [128, 4, 128], BF16) as tpB, \
             nc.psum_tensor(self.uname("pf_mmp"), [128, 2, 512], F32) as mm_t:
            tp = [tpA.ap(), tpB.ap()]
            mmp = mm_t.ap()

            def b_sync(eng):
                if load_xnt and not self.fused:
                    srcb = self.dram_tmp("xnt_s", [D, S], BF16, inp=True)
                    for ic in range(NCH):
                        eng.dma_start(out=self.big[:, ic, :],
                                      in_=srcb[ic * 128:(ic + 1) * 128, :]
                                      ).then_inc(s_ldb, 16)
                for j in range(NOT_ * 4):
                    ot, q = divmod(j, 4)
                    if j >= 2:
                        eng.wait_ge(s_sg, j - 1)
                        eng.wait_ge(s_alq, j - 1)
                    eng.dma_start(out=stage[:, j % 2, :],
                                  in_=wr[ot * 128:(ot + 1) * 128,
                                         q * 1024:(q + 1) * 1024]
                                  ).then_inc(s_wld[j % 2], 16)

            def b_gpsimd(eng):
                eng.dma_start(out=biasb, in_=bias).then_inc(s_bld, 16)
                nst = [0, 0]
                for mi in range(16 * NTT):
                    band, tt = divmod(mi, NTT)
                    eng.wait_ge(s_de, mi + 1)
                    eng.dma_start(out=delta_s[tt * 128:(tt + 1) * 128,
                                              band * 256:(band + 1) * 256],
                                  in_=dsb[:, mi % 2, :]).then_inc(s_dst[mi % 2], 16)
                    nst[mi % 2] += 1
                eng.wait_ge(s_dst[0], 16 * nst[0])
                eng.wait_ge(s_dst[1], 16 * nst[1])

            def b_vector(eng):
                for ot in range(NOT_):
                    for q in range(4):
                        j = ot * 4 + q
                        eng.wait_ge(s_wld[j % 2], 16 * (j // 2 + 1))
                        eng.tensor_reduce(out=al[:, 8 + q:9 + q],
                                          in_=stage[:, j % 2, :], axis=AX.X,
                                          op=ALU.add, apply_absolute_value=True
                                          ).then_inc(s_alq, 1)
                    eng.drain()
                    eng.tensor_reduce(out=al[:, 12:13], in_=al[:, 8:12],
                                      axis=AX.X, op=ALU.add)
                    eng.drain()
                    eng.tensor_scalar(out=al[:, 12:13], in0=al[:, 12:13],
                                      scalar1=1.0 / D, scalar2=None, op0=ALU.mult)
                    eng.drain()
                    eng.wait_ge(s_sg, 4 * (ot + 1))
                    eng.tensor_scalar(out=sgn_tmp[:, :], in0=sgn_tmp[:, :],
                                      scalar1=al[:, 12:13], scalar2=None,
                                      op0=ALU.mult)
                    eng.drain().then_inc(s_sc, 1)
                    if ot % 2 == 1:
                        band = ot // 2
                        for j in range(NTT):
                            mi = band * NTT + j
                            eng.wait_ge(s_mm, mi + 1)
                            if mi >= 2:
                                eng.wait_ge(s_dst[mi % 2], 16 * (mi // 2))
                            eng.tensor_copy(dsb[:, mi % 2, :],
                                            mmp[:, mi % 2, 0:256])
                            eng.drain().then_inc(s_de, 1)

            def b_scalar(eng):
                for ot in range(NOT_):
                    for q in range(4):
                        j = ot * 4 + q
                        eng.wait_ge(s_wld[j % 2], 16 * (j // 2 + 1))
                        if q == 0 and ot >= 1:
                            eng.wait_ge(s_tpk, 8 * ot)
                        eng.activation(sgn_tmp[:, q * 1024:(q + 1) * 1024],
                                       stage[:, j % 2, :], AF.Sign)
                        eng.drain().then_inc(s_sg, 1)
                    band, half = ot // 2, ot % 2
                    if band >= 2 and half == 0:
                        eng.wait_ge(s_mm, 16 * (band - 1))
                    for pk in range(8):
                        g = ot * 8 + pk
                        eng.wait_ge(s_tpk, g + 1)
                        eng.activation(wtb[:, band % 2, pk * 4:pk * 4 + 4,
                                           half * 128:(half + 1) * 128],
                                       tp[pk % 2][:, :, :], AF.Copy)
                        eng.drain().then_inc(s_te, 1)

            def b_tensor(eng):
                if load_xnt and not self.fused:
                    eng.wait_ge(s_ldb, 16 * NCH)
                eng.wait_ge(s_bld, 16)
                for ot in range(NOT_):
                    eng.wait_ge(s_sc, ot + 1)
                    for pk in range(8):
                        g = ot * 8 + pk
                        if g >= 2:
                            eng.wait_ge(s_te, g - 1)
                        for j in range(4):
                            ic = pk * 4 + j
                            ii = self.tpose(eng, tp[pk % 2][:, j, :],
                                            sgn_tmp[:, ic * 128:(ic + 1) * 128])
                        ii.then_inc(s_tpk, 1)
                        self.tpose_end(eng)
                    if ot % 2 == 1:
                        band = ot // 2
                        eng.wait_ge(s_te, 8 * (ot + 1))
                        for tt in range(NTT):
                            mi = band * NTT + tt
                            if mi >= 2:
                                eng.wait_ge(s_de, mi - 1)
                            p = mmp[:, mi % 2, 0:256]
                            for ic in range(NCH):
                                eng.matmul(p,
                                           self.big[:, ic, tt * 128:(tt + 1) * 128],
                                           wtb[:, band % 2, ic, :],
                                           start=(ic == 0), stop=False)
                            eng.matmul(p, self.onesrow,
                                       biasb[:, band * 256:(band + 1) * 256],
                                       start=False, stop=True).then_inc(s_mm, 1)

            self.emit_phase(dict(sync=b_sync, gpsimd=b_gpsimd, vector=b_vector,
                                 scalar=b_scalar, tensor=b_tensor))

    def phase_pf_moe(self, L, load_xnt=False):
        nc = self.nc
        ln = L["name"]
        A_in = self.dram_in(f"{ln}_A", [4, 16, D], F32)
        B_in = self.dram_in(f"{ln}_B", [4, D, 16], F32)
        Wr_in = self.dram_in(f"{ln}_Wr", [4, D], F32)
        delta_s = self.dram_tmp("delta_s", [S, D], F32, out=True)
        SCALE = 4.0

        AT = self.carve(self.arenaA, 0, [128, NCH, 64], BF16)
        BT = self.carve(self.arenaA, 4096, [64, NCH * 128], BF16)
        BT3 = self.carve(self.arenaA, 4096, [64, NCH, 128], BF16)
        WrT = self.carve(self.arenaA, 12288, [128, NCH, 4], BF16)
        A_sb = self.carve(self.arenaA, 12544, [64, D], BF16)
        wr_sb = self.carve(self.arenaA, 20736, [4, D], BF16)
        Bst = self.carve(self.arenaA, 28928, [128, 2, NCH, 32], BF16)
        mid_sb = self.carve(self.arenaA, 33024, [128, 2, 64], BF16)
        midT_sb = self.carve(self.arenaA, 33280, [64, 2, 128], BF16)
        gsc = self.carve(self.arenaA, 34304, [128, 2, 16], F32)
        dsb = self.carve(self.arenaB, 0, [128, 2, 512], F32)

        s_ldc = self.sem("pf0_ldc")
        s_stp = self.sem("pf0_stp")
        s_ste = self.sem("pf0_ste")
        s_rt = self.sem("pf0_rt")
        s_gt = self.sem("pf0_gt")
        s_gs = self.sem("pf0_gs")
        s_mi = self.sem("pf0_mi")
        s_me = self.sem("pf0_me")
        s_mt = self.sem("pf0_mt")
        s_mte = self.sem("pf0_mte")
        s_dm = self.sem("pf0_dm")
        s_dev = self.sem("pf0_dev")
        s_dst = [self.sem("pf0_dst"), self.sem("pf0_dst")]
        s_ldb = self.sem("pf0_ldb")

        NOG = 8
        with nc.psum_tensor(self.uname("pf0_rp"), [128, 4], F32) as rp_t, \
             nc.psum_tensor(self.uname("pf0_mid"), [128, 64], F32) as mid_t, \
             nc.psum_tensor(self.uname("pf0_tp"), [128, 128], BF16) as tp_t, \
             nc.psum_tensor(self.uname("pf0_tp2"), [128, 128], BF16) as tp2_t, \
             nc.psum_tensor(self.uname("pf0_dp"), [128, 2, 512], F32) as dp_t:
            rp = rp_t.ap()
            midp = mid_t.ap()
            tpb = [tp_t.ap(), tp2_t.ap()]
            dp = dp_t.ap()

            def b_gpsimd(eng):
                eng.dma_start(out=A_sb[:, :],
                              in_=A_in.rearrange("e r d -> (e r) d")
                              ).then_inc(s_ldc, 16)
                eng.dma_start(out=wr_sb[:, :], in_=Wr_in).then_inc(s_ldc, 16)
                for e in range(4):
                    eng.dma_start(out=Bst[:, e // 2, :, (e % 2) * 16:(e % 2) * 16 + 16],
                                  in_=B_in[e].rearrange("(ic p) r -> p ic r", p=128)
                                  ).then_inc(s_ldc, 16)

            def b_sync(eng):
                if load_xnt and not self.fused:
                    srcb = self.dram_tmp("xnt_s", [D, S], BF16, inp=True)
                    for ic in range(NCH):
                        eng.dma_start(out=self.big[:, ic, :],
                                      in_=srcb[ic * 128:(ic + 1) * 128, :]
                                      ).then_inc(s_ldb, 16)
                nst = [0, 0]
                for mi in range(NTT * NOG):
                    tt, og = divmod(mi, NOG)
                    eng.wait_ge(s_dev, mi + 1)
                    eng.dma_start(out=delta_s[tt * 128:(tt + 1) * 128,
                                              og * 512:(og + 1) * 512],
                                  in_=dsb[:, mi % 2, :]).then_inc(s_dst[mi % 2], 16)
                    nst[mi % 2] += 1
                eng.wait_ge(s_dst[0], 16 * nst[0])
                eng.wait_ge(s_dst[1], 16 * nst[1])

            def b_tensor(eng):
                if load_xnt and not self.fused:
                    eng.wait_ge(s_ldb, 16 * NCH)
                eng.wait_ge(s_ldc, 96)
                g = 0
                for ic in range(NCH):
                    if g >= 2:
                        eng.wait_ge(s_ste, g - 1)
                    ii = self.tpose(eng, tpb[g % 2][:, 0:64],
                                    A_sb[0:64, ic * 128:(ic + 1) * 128],
                                    ident=self.ident[0:64, 0:64])
                    ii.then_inc(s_stp, 1)
                    self.tpose_end(eng)
                    g += 1
                for ic in range(NCH):
                    if g >= 2:
                        eng.wait_ge(s_ste, g - 1)
                    ii = self.tpose(eng, tpb[g % 2][:, 0:4],
                                    wr_sb[0:4, ic * 128:(ic + 1) * 128],
                                    ident=self.ident[0:4, 0:4])
                    ii.then_inc(s_stp, 1)
                    self.tpose_end(eng)
                    g += 1
                for pair in range(2):
                    for ic in range(NCH):
                        if g >= 2:
                            eng.wait_ge(s_ste, g - 1)
                        ii = self.tpose(eng, tpb[g % 2][0:32, :],
                                        Bst[:, pair, ic, :])
                        ii.then_inc(s_stp, 1)
                        self.tpose_end(eng)
                        g += 1
                eng.wait_ge(s_ste, 128)
                for tt in range(NTT):
                    if tt >= 1:
                        eng.wait_ge(s_gt, tt)
                    for ic in range(NCH):
                        ii = eng.matmul(rp[:, :],
                                        self.big[:, ic, tt * 128:(tt + 1) * 128],
                                        WrT[:, ic, :],
                                        start=(ic == 0), stop=(ic == NCH - 1))
                    ii.then_inc(s_rt, 1)
                    if tt >= 1:
                        eng.wait_ge(s_me, tt)
                    for ic in range(NCH):
                        ii = eng.matmul(midp[:, :],
                                        self.big[:, ic, tt * 128:(tt + 1) * 128],
                                        AT[:, ic, :],
                                        start=(ic == 0), stop=(ic == NCH - 1))
                    ii.then_inc(s_mi, 1)
                    eng.wait_ge(s_me, tt + 1)
                    if tt >= 1:
                        eng.wait_ge(s_mte, tt)
                    ii = self.tpose(eng, tpb[1][0:64, :], mid_sb[:, tt % 2, :])
                    ii.then_inc(s_mt, 1)
                    self.tpose_end(eng)
                    eng.wait_ge(s_mte, tt + 1)
                    for og in range(NOG):
                        mi = tt * NOG + og
                        if mi >= 2:
                            eng.wait_ge(s_dev, mi - 1)
                        eng.matmul(dp[:, mi % 2, :], midT_sb[0:64, tt % 2, :],
                                   BT[0:64, og * 512:(og + 1) * 512],
                                   start=True, stop=True).then_inc(s_dm, 1)

            def b_scalar(eng):
                for g in range(128):
                    eng.wait_ge(s_stp, g + 1)
                    if g < 32:
                        eng.activation(AT[:, g, :], tpb[g % 2][0:128, 0:64],
                                       AF.Copy)
                    elif g < 64:
                        eng.activation(WrT[:, g - 32, :], tpb[g % 2][0:128, 0:4],
                                       AF.Copy)
                    else:
                        pair = (g - 64) // 32
                        ic = (g - 64) % 32
                        eng.activation(BT3[pair * 32:(pair + 1) * 32, ic, :],
                                       tpb[g % 2][0:32, 0:128], AF.Copy)
                    eng.drain().then_inc(s_ste, 1)
                for tt in range(NTT):
                    r = tt % 2
                    eng.wait_ge(s_gs, 4 * tt + 1)
                    eng.activation(gsc[:, r, 4:8], gsc[:, r, 0:4], AF.Sign)
                    eng.drain().then_inc(s_gs, 1)
                    eng.wait_ge(s_gs, 4 * tt + 3)
                    eng.activation(gsc[:, r, 8:12], gsc[:, r, 0:4], AF.Exp)
                    eng.drain().then_inc(s_gs, 1)

            def b_vector(eng):
                for tt in range(NTT):
                    r = tt % 2
                    lg = gsc[:, r, 0:4]
                    sm = self.smalls
                    eng.wait_ge(s_rt, tt + 1)
                    eng.tensor_copy(lg, rp[:, :])
                    eng.drain()
                    eng.tensor_reduce(out=sm[:, 16 + r:17 + r], in_=lg, axis=AX.X,
                                      op=ALU.max, negate=True)
                    eng.drain()
                    eng.tensor_scalar(out=lg, in0=lg, scalar1=sm[:, 16 + r:17 + r],
                                      scalar2=None, op0=ALU.add)
                    eng.drain().then_inc(s_gs, 1)
                    eng.wait_ge(s_gs, 4 * tt + 2)
                    eng.tensor_scalar(out=gsc[:, r, 4:8], in0=gsc[:, r, 4:8],
                                      scalar1=1.0, scalar2=-1e30,
                                      op0=ALU.add, op1=ALU.mult)
                    eng.drain()
                    eng.tensor_add(gsc[:, r, 12:16], lg, gsc[:, r, 4:8])
                    eng.drain()
                    eng.tensor_reduce(out=sm[:, 18 + r:19 + r], in_=gsc[:, r, 12:16],
                                      axis=AX.X, op=ALU.max)
                    eng.drain()
                    eng.tensor_scalar(out=gsc[:, r, 12:16], in0=lg,
                                      scalar1=sm[:, 18 + r:19 + r], scalar2=None,
                                      op0=ALU.is_ge)
                    eng.drain().then_inc(s_gs, 1)
                    eng.wait_ge(s_gs, 4 * tt + 4)
                    eng.tensor_mul(gsc[:, r, 4:8], gsc[:, r, 8:12],
                                   gsc[:, r, 12:16])
                    eng.drain()
                    eng.tensor_reduce(out=sm[:, 20 + r:21 + r], in_=gsc[:, r, 4:8],
                                      axis=AX.X, op=ALU.add)
                    eng.drain()
                    eng.reciprocal(sm[:, 22 + r:23 + r], sm[:, 20 + r:21 + r])
                    eng.drain()
                    eng.tensor_scalar(out=gsc[:, r, 8:12], in0=gsc[:, r, 4:8],
                                      scalar1=sm[:, 22 + r:23 + r], scalar2=SCALE,
                                      op0=ALU.mult, op1=ALU.mult)
                    eng.drain().then_inc(s_gt, 1)
                    eng.wait_ge(s_mi, tt + 1)
                    if tt >= 1:
                        eng.wait_ge(s_mt, tt)
                    for e in range(4):
                        eng.tensor_scalar(out=mid_sb[:, r, e * 16:(e + 1) * 16],
                                          in0=midp[:, e * 16:(e + 1) * 16],
                                          scalar1=gsc[:, r, 8 + e:9 + e],
                                          scalar2=None, op0=ALU.mult)
                    eng.drain().then_inc(s_me, 1)
                    eng.wait_ge(s_mt, tt + 1)
                    eng.tensor_copy(midT_sb[0:64, r, :], tpb[1][0:64, :])
                    eng.drain().then_inc(s_mte, 1)
                    for og in range(NOG):
                        mi = tt * NOG + og
                        eng.wait_ge(s_dm, mi + 1)
                        if mi >= 2:
                            eng.wait_ge(s_dst[mi % 2], 16 * (mi // 2))
                        eng.tensor_copy(dsb[:, mi % 2, :], dp[:, mi % 2, :])
                        eng.drain().then_inc(s_dev, 1)

            self.emit_phase(dict(sync=b_sync, gpsimd=b_gpsimd, tensor=b_tensor,
                                 scalar=b_scalar, vector=b_vector))


    def phase_pc(self, L, dump_o=False):
        nc = self.nc
        qt_s = self.dram_tmp("qt_s", [D, S], BF16, inp=True)
        kt_s = self.dram_tmp("kt_s", [D, S], BF16, inp=True)
        v_s = self.dram_tmp("v_s", [S, D], BF16, inp=True)
        n_q = L["n_q"]
        G = n_q // L["n_kv"]
        win = L["window"]
        NT = n_q * NTT

        qh = self.carve(self.arenaA, 0, [128, 2, S], BF16)
        kh = self.carve(self.arenaA, 8192, [128, 2, S], BF16)
        vh = self.carve(self.arenaA, 16384, [128, 2, 16, 132], BF16)
        probs = self.carve(self.arenaA, 24832, [128, 2, S], BF16)
        pt_sb = self.carve(self.arenaA, 33024, [128, 2, 16, 128], BF16)
        sm = self.smalls

        def geom(i):
            if win is None:
                lo = 0
            else:
                lo = max(0, i - 8)
            b0 = lo // 4
            nb = i // 4 - b0 + 1
            sbl = list(range(lo, i + 1))
            return lo, b0, nb, sbl

        packs_cum = [0]
        mask_cum = [0]
        sc_cum = [0]
        ex_cum = [0]
        nb_of = {}
        for t in range(NT):
            i = t % 16
            lo, b0, nb, sbl = geom(i)
            nb_of[t] = nb
            packs_cum.append(packs_cum[-1] + (len(sbl) + 3) // 4)
            sc_cum.append(sc_cum[-1] + nb)
            ex_cum.append(ex_cum[-1] + nb)
            nm = 1 + (1 if (win is not None and i >= 8) else 0)
            mask_cum.append(mask_cum[-1] + nm)
        def masked_blocks(i):
            lo, b0, nb, sbl = geom(i)
            mb = {sbl.index(i)}
            if win is not None and i >= 8:
                mb.add(sbl.index(i - 8))
            return mb

        def pack_engine(i, pk):
            lo, b0, nb, sbl = geom(i)
            mb = masked_blocks(i)
            if any((j in mb) for j in range(pk * 4, min(pk * 4 + 4, len(sbl)))):
                return "B"
            return "A" if pk % 2 == 1 else "B"

        packsA_cum = [0]
        packsB_cum = [0]
        packsA_cum_g = [0]
        packsB_cum_g = [0]
        for t in range(NT):
            i = t % 16
            lo, b0, nb, sbl = geom(i)
            npk = (len(sbl) + 3) // 4
            a = sum(1 for pk in range(npk) if pack_engine(i, pk) == "A")
            b = npk - a
            packsA_cum.append(packsA_cum[-1] + a)
            packsB_cum.append(packsB_cum[-1] + b)
            for pk in range(npk):
                isA = pack_engine(i, pk) == "A"
                packsA_cum_g.append(packsA_cum_g[-1] + (1 if isA else 0))
                packsB_cum_g.append(packsB_cum_g[-1] + (0 if isA else 1))

        s_q = [self.sem("pc_q"), self.sem("pc_q")]
        s_k = [self.sem("pc_k"), self.sem("pc_k")]
        s_hd = self.sem("pc_hd")
        s_kv = self.sem("pc_kv")
        s_sc = self.sem("pc_sc")
        s_mx = self.sem("pc_mx")
        s_ex = self.sem("pc_ex")
        s_ptk = self.sem("pc_ptk")
        s_pteA = self.sem("pc_pteA")
        s_pteB = self.sem("pc_pteB")
        s_ptwA = self.sem("pc_ptwA")
        s_ptwB = self.sem("pc_ptwB")
        s_ptm = self.sem("pc_ptm")
        s_pv = self.sem("pc_pv")
        s_oev = self.sem("pc_oev")
        s_ost = self.sem("pc_ost")
        s_vset = self.sem("pc_vset")

        def kv_of(h):
            return h // G

        def kvslot(h):
            return (kv_of(h) % 2) if win is not None else (h % 2)

        qrounds = {h: 16 * (h // 2 + 1) for h in range(n_q)}
        kvrounds = {}
        for h in range(n_q):
            kvh = kv_of(h) if win is not None else h
            kvrounds[h] = 16 * 17 * (kvh // 2 + 1)

        with nc.psum_tensor(self.uname("pc_sc"), [128, 4, 512], F32) as sc_t, \
             nc.psum_tensor(self.uname("pc_ptA"), [128, 4, 128], BF16) as ptA, \
             nc.psum_tensor(self.uname("pc_ptB"), [128, 4, 128], BF16) as ptB, \
             nc.psum_tensor(self.uname("pc_oA"), [128, 132], F32) as oA, \
             nc.psum_tensor(self.uname("pc_oB"), [128, 132], F32) as oB:
            scp = sc_t.ap().rearrange("p a b -> p (a b)")
            ptp = [ptA.ap(), ptB.ap()]
            op = [oA.ap(), oB.ap()]

            def b_sync(eng):
                for h in range(n_q):
                    if h >= 2:
                        eng.wait_ge(s_hd, h - 1)
                    eng.dma_start(out=qh[:, h % 2, :],
                                  in_=qt_s[h * 128:(h + 1) * 128, :]
                                  ).then_inc(s_q[h % 2], 16)
                    if win is None or h % G == 0:
                        kvh = kv_of(h)
                        slot = kvslot(h)
                        if win is not None and kvh >= 2:
                            eng.wait_ge(s_kv, kvh - 1)
                        krow = (kvh if win is not None else h) * 128
                        dv0 = kvh * 128 if win is not None else h * 128
                        eng.dma_start(out=kh[:, slot, :],
                                      in_=kt_s[krow:krow + 128, :]
                                      ).then_inc(s_k[slot], 16)
                        eng.wait_ge(s_vset, 1)
                        for sb in range(16):
                            eng.dma_start(out=vh[:, slot, sb, 0:128],
                                          in_=v_s[sb * 128:(sb + 1) * 128,
                                                  dv0:dv0 + 128]
                                          ).then_inc(s_k[slot], 16)

            def b_gpsimd(eng):
                eng.memset(vh[:, 0, :, 128:129], 1.0)
                eng.memset(vh[:, 1, :, 128:129], 1.0)
                eng.drain().then_inc(s_vset, 1)

            def b_tensor(eng):
                gpk = 0
                for t in range(NT + 2):
                    if t < NT:
                        h, i = divmod(t, 16)
                        lo, b0, nb, sbl = geom(i)
                        if i == 0:
                            eng.wait_ge(s_q[h % 2], qrounds[h])
                            eng.wait_ge(s_k[kvslot(h)], kvrounds[h])
                        for b in range(nb):
                            if t >= 1:
                                eng.wait_ge(s_ex, ex_cum[t - 1] + min(b + 1, nb_of[t - 1]))
                            eng.matmul(
                                scp[:, b * 512:(b + 1) * 512],
                                qh[:, h % 2, i * 128:(i + 1) * 128],
                                kh[:, kvslot(h), (b0 + b) * 512:(b0 + b + 1) * 512],
                                start=True, stop=True).then_inc(s_sc, 1)
                    u = t - 1
                    if 0 <= u < NT:
                        h1, i1 = divmod(u, 16)
                        lo1, b01, nb1, sbl1 = geom(i1)
                        npk = (len(sbl1) + 3) // 4
                        eng.wait_ge(s_ex, u + 1)
                        for pk in range(npk):
                            g = packs_cum[u] + pk
                            if g >= 2:
                                eng.wait_ge(s_ptwA, packsA_cum_g[g - 1])
                                eng.wait_ge(s_ptwB, packsB_cum_g[g - 1])
                            blocks = sbl1[pk * 4:pk * 4 + 4]
                            needed_exp = max((sb - b01 * 4) // 4 for sb in blocks)
                            eng.wait_ge(s_ex, ex_cum[u] + needed_exp + 1)
                            for j, sb in enumerate(blocks):
                                lb = sb - b01 * 4
                                ii = self.tpose(
                                    eng, ptp[g % 2][:, j, :],
                                    probs[:, u % 2, lb * 128:(lb + 1) * 128])
                            ii.then_inc(s_ptk, 1)
                            self.tpose_end(eng)
                        eng.wait_ge(s_pteA, packsA_cum[u + 1])
                        eng.wait_ge(s_pteB, packsB_cum[u + 1])
                        if u >= 2:
                            eng.wait_ge(s_oev, u - 1)
                        for j, sb in enumerate(sbl1):
                            ii = eng.matmul(op[u % 2][:, 0:129],
                                            pt_sb[:, u % 2, j, :],
                                            vh[:, kvslot(h1), sb, 0:129],
                                            start=(j == 0),
                                            stop=(j == len(sbl1) - 1))
                        ii.then_inc(s_pv, 1)
                        if i1 == 15:
                            eng.drain().then_inc(s_hd, 1)
                            if win is not None and (h1 % G) == G - 1:
                                eng.nop().then_inc(s_kv, 1)

            def b_vector(eng):
                for t in range(NT + 2):
                    if t < NT:
                        h, i = divmod(t, 16)
                        lo, b0, nb, sbl = geom(i)
                        for b in range(nb):
                            eng.wait_ge(s_sc, sc_cum[t] + b + 1)
                            eng.tensor_reduce(out=sm[:, 32 + b:33 + b],
                                              in_=scp[:, b * 512:(b + 1) * 512],
                                              axis=AX.X, op=ALU.max, negate=True)
                        eng.drain()
                        eng.tensor_reduce(out=sm[:, 26 + t % 2:27 + t % 2],
                                          in_=sm[:, 32:32 + nb], axis=AX.X,
                                          op=ALU.min)
                        eng.drain().then_inc(s_mx, 1)
                    u2 = t - 2
                    if 0 <= u2 < NT:
                        eng.wait_ge(s_pv, u2 + 1)
                        eng.tensor_copy(sm[:, 28 + t % 2:29 + t % 2],
                                        op[u2 % 2][:, 128:129])
                    u = t - 1
                    if 0 <= u < NT:
                        h1, i1 = divmod(u, 16)
                        lo1, b01, nb1, sbl1 = geom(i1)
                        npk = (len(sbl1) + 3) // 4
                        mb1 = masked_blocks(i1)
                        nmine = 0
                        for pk in range(npk):
                            if pack_engine(i1, pk) != "B":
                                continue
                            g = packs_cum[u] + pk
                            eng.wait_ge(s_ptk, g + 1)
                            cntb = len(sbl1[pk * 4:pk * 4 + 4])
                            pkmb = [j for j in range(pk * 4, pk * 4 + cntb)
                                    if j in mb1]
                            if not pkmb:
                                eng.tensor_copy(
                                    pt_sb[:, u % 2, pk * 4:pk * 4 + cntb, :],
                                    ptp[g % 2][:, 0:cntb, :]).then_inc(s_ptwB, 1)
                            else:
                                ii = None
                                for j in range(pk * 4, pk * 4 + cntb):
                                    jj = j - pk * 4
                                    if j in pkmb:
                                        mask = (self.mask_diag
                                                if sbl1[j] == i1 else self.mask_left)
                                        ii = eng.tensor_mul(
                                            pt_sb[:, u % 2, j, :],
                                            ptp[g % 2][:, jj, :], mask)
                                    else:
                                        ii = eng.tensor_copy(
                                            pt_sb[:, u % 2, j, :],
                                            ptp[g % 2][:, jj, :])
                                ii.then_inc(s_ptwB, 1)
                            nmine += 1
                            if nmine == 1:
                                eng.drain().then_inc(s_pteB, 1)
                        if nmine > 1:
                            eng.drain().then_inc(s_pteB, nmine - 1)

                    if 0 <= u2 < NT:
                        h2, i2 = divmod(u2, 16)
                        eng.reciprocal(sm[:, 30 + t % 2:31 + t % 2],
                                       sm[:, 28 + t % 2:29 + t % 2])
                        eng.drain()
                        eng.tensor_scalar(out=self.big[:, h2, i2 * 128:(i2 + 1) * 128],
                                          in0=op[u2 % 2][:, 0:128],
                                          scalar1=sm[:, 30 + t % 2:31 + t % 2],
                                          scalar2=None, op0=ALU.mult)
                        eng.drain().then_inc(s_oev, 1)

            def b_scalar(eng):
                for t in range(NT + 1):
                    if t < NT:
                        h, i = divmod(t, 16)
                        lo, b0, nb, sbl = geom(i)
                        eng.wait_ge(s_mx, t + 1)
                        eng.activation(probs[:, t % 2, 0:nb * 512],
                                       scp[:, 0:nb * 512], AF.Exp,
                                       bias=sm[:, 26 + t % 2:27 + t % 2])
                        eng.drain().then_inc(s_ex, nb)
                    u = t - 1
                    if 0 <= u < NT:
                        h1, i1 = divmod(u, 16)
                        lo1, b01, nb1, sbl1 = geom(i1)
                        npk = (len(sbl1) + 3) // 4
                        nmine = 0
                        for pk in range(npk):
                            if pack_engine(i1, pk) != "A":
                                continue
                            g = packs_cum[u] + pk
                            eng.wait_ge(s_ptk, g + 1)
                            cntb = len(sbl1[pk * 4:pk * 4 + 4])
                            eng.activation(pt_sb[:, u % 2, pk * 4:pk * 4 + cntb, :],
                                           ptp[g % 2][:, 0:cntb, :], AF.Copy
                                           ).then_inc(s_ptwA, 1)
                            nmine += 1
                            if nmine == 1:
                                eng.drain().then_inc(s_pteA, 1)
                        if nmine > 1:
                            eng.drain().then_inc(s_pteA, nmine - 1)
                if dump_o and not self.fused:
                    o_s = self.dram_tmp("o_s", [D, S], BF16, out=True)
                    eng.wait_ge(s_oev, NT)
                    for ic in range(NCH):
                        eng.dma_start(out=o_s[ic * 128:(ic + 1) * 128, :],
                                      in_=self.big[:, ic, :]).then_inc(s_ost, 16)
                    eng.wait_ge(s_ost, 16 * NCH)

            self.emit_phase(dict(sync=b_sync, gpsimd=b_gpsimd, tensor=b_tensor,
                                 vector=b_vector, scalar=b_scalar))


    def phase_pd(self, L, resid_name, out_name, out_kind, load_o=False):
        nc = self.nc
        ln = L["name"]
        wo = self.dram_in(f"{ln}_wo", [D, D], F32)
        delta_s = self.dram_tmp("delta_s", [S, D], F32, inp=True)
        resid = (self.dram_in(resid_name, [S, D], F32) if resid_name == "xin"
                 else self.dram_tmp(resid_name, [S, D], F32))
        if out_kind == "out":
            yout = self.dram_out(out_name, [S, D], F32)
        else:
            yout = self.dram_tmp(out_name, [S, D], F32, out=True)

        GW = 256
        wbuf = self.carve(self.arenaA, 0, [128, 2, NCH, GW], BF16)
        xd = self.carve(self.arenaA, 32768, [128, 4, 2, GW], F32)
        y_sb = self.carve(self.arenaA, 40960, [128, 2, GW], F32)
        ytmp = self.carve(self.arenaA, 43008, [128, GW], F32)
        stage = self.carve(self.arenaB, 0, [128, 4, GW], F32)

        s_wl = [self.sem("pd_wl") for _ in range(4)]
        s_wc = self.sem("pd_wc")
        s_wcf = self.sem("pd_wcf")
        s_peg = self.sem("pd_peg")
        s_xd = [self.sem("pd_xd") for _ in range(4)]
        s_mm = self.sem("pd_mm")
        s_ye = self.sem("pd_ye")
        s_yst = [self.sem("pd_yst"), self.sem("pd_yst")]
        s_tp = self.sem("pd_tp")
        s_pte = self.sem("pd_pte")
        s_ldo = self.sem("pd_ldo")

        NG = D // GW

        with nc.psum_tensor(self.uname("pd_tpA"), [128, 4, 128], BF16) as tpA, \
             nc.psum_tensor(self.uname("pd_tpB"), [128, 4, 128], BF16) as tpB, \
             nc.psum_tensor(self.uname("pd_mm"), [128, 4, 512], F32) as mm_t:
            tp = [tpA.ap(), tpB.ap()]
            mmp = mm_t.ap()

            def b_sync(eng):
                if load_o and not self.fused:
                    srcb = self.dram_tmp("o_s", [D, S], BF16, inp=True)
                    for ic in range(NCH):
                        eng.dma_start(out=self.big[:, ic, :],
                                      in_=srcb[ic * 128:(ic + 1) * 128, :]
                                      ).then_inc(s_ldo, 16)
                for og in range(NG):
                    for ic in range(NCH):
                        j = og * NCH + ic
                        if j >= 4:
                            eng.wait_ge(s_wcf, j - 3)
                        eng.dma_start(out=stage[:, j % 4, :],
                                      in_=wo[ic * 128:(ic + 1) * 128,
                                             og * GW:(og + 1) * GW]
                                      ).then_inc(s_wl[j % 4], 16)
                    for tt in range(NTT):
                        mi = og * NTT + tt
                        if mi >= 4:
                            eng.wait_ge(s_ye, mi - 3)
                        eng.dma_start(out=xd[:, mi % 4, 0, :],
                                      in_=resid[tt * 128:(tt + 1) * 128,
                                                og * GW:(og + 1) * GW]
                                      ).then_inc(s_xd[mi % 4], 16)
                        eng.dma_start(out=xd[:, mi % 4, 1, :],
                                      in_=delta_s[tt * 128:(tt + 1) * 128,
                                                  og * GW:(og + 1) * GW]
                                      ).then_inc(s_xd[mi % 4], 16)

            def b_gpsimd(eng):
                for j in range(NG * NCH):
                    gi, ic = divmod(j, NCH)
                    if gi >= 2 and ic == 0:
                        eng.wait_ge(s_peg, gi - 1)
                    eng.wait_ge(s_wl[j % 4], 16 * (j // 4 + 1))
                    eng.tensor_copy(wbuf[:, gi % 2, ic, :],
                                    stage[:, j % 4, :]).then_inc(s_wcf, 1)
                    if ic == NCH - 1:
                        eng.drain().then_inc(s_wc, 1)

            def b_tensor(eng):
                if load_o and not self.fused:
                    eng.wait_ge(s_ldo, 16 * NCH)
                for tt in range(NTT):
                    for p in range(8):
                        g = tt * 8 + p
                        if g >= 2:
                            eng.wait_ge(s_pte, g - 1)
                        for j in range(4):
                            ic = p * 4 + j
                            ii = self.tpose(eng, tp[p % 2][:, j, :],
                                            self.big[:, ic, tt * 128:(tt + 1) * 128])
                        ii.then_inc(s_tp, 1)
                        self.tpose_end(eng)
                for og in range(NG):
                    eng.wait_ge(s_wc, og + 1)
                    eng.wait_ge(s_pte, NTT * 8)
                    for tt in range(NTT):
                        mi = og * NTT + tt
                        if mi >= 4:
                            eng.wait_ge(s_ye, mi - 3)
                        p = mmp[:, mi % 4, 0:GW]
                        for ic in range(NCH):
                            ii = eng.matmul(p,
                                            self.big[:, ic, tt * 128:(tt + 1) * 128],
                                            wbuf[:, og % 2, ic, :],
                                            start=(ic == 0), stop=(ic == NCH - 1))
                        ii.then_inc(s_mm, 1)
                    eng.drain().then_inc(s_peg, 1)

            def b_scalar(eng):
                for tt in range(NTT):
                    for p in range(8):
                        g = tt * 8 + p
                        eng.wait_ge(s_tp, g + 1)
                        eng.activation(
                            self.big[:, p * 4:p * 4 + 4, tt * 128:(tt + 1) * 128],
                            tp[p % 2][:, :, :], AF.Copy)
                        eng.drain().then_inc(s_pte, 1)
                nst = [0, 0]
                for mi in range(NG * NTT):
                    og, tt = divmod(mi, NTT)
                    eng.wait_ge(s_ye, mi + 1)
                    eng.dma_start(out=yout[tt * 128:(tt + 1) * 128,
                                           og * GW:(og + 1) * GW],
                                  in_=y_sb[:, mi % 2, :]).then_inc(s_yst[mi % 2], 16)
                    nst[mi % 2] += 1
                eng.wait_ge(s_yst[0], 16 * nst[0])
                eng.wait_ge(s_yst[1], 16 * nst[1])

            def b_vector(eng):
                for mi in range(NG * NTT):
                    eng.wait_ge(s_mm, mi + 1)
                    eng.wait_ge(s_xd[mi % 4], 32 * (mi // 4 + 1))
                    if mi >= 2:
                        eng.wait_ge(s_yst[mi % 2], 16 * (mi // 2))
                    eng.scalar_tensor_tensor(out=ytmp[:, :], in0=mmp[:, mi % 4, 0:GW],
                                             scalar=1.0, in1=xd[:, mi % 4, 0, :],
                                             op0=ALU.mult, op1=ALU.add)
                    eng.drain()
                    eng.tensor_add(y_sb[:, mi % 2, :], ytmp[:, :],
                                   xd[:, mi % 4, 1, :])
                    eng.drain().then_inc(s_ye, 1)

            self.emit_phase(dict(sync=b_sync, gpsimd=b_gpsimd, tensor=b_tensor,
                                 scalar=b_scalar, vector=b_vector))



def build_prog(phases, fused):
    B = Builder(fused)
    B.alloc_fixed()
    with B.es:
        B.phase_consts()
        for (pname, L, kw) in phases:
            getattr(B, "phase_" + pname)(**kw) if L is None else \
                getattr(B, "phase_" + pname)(L, **kw)
    return B


_prog_cache = {}


def _layer_prog(layer):
    key = layer["name"]
    if key in _prog_cache:
        return _prog_cache[key]
    B = Builder(fused=True)
    B.alloc_fixed()
    with B.es:
        B.phase_consts()
        src = B.dram_in("xin", [S, D], F32)
        B.phase_pa(src)
        B.phase_pb(layer)
        if layer["ffn"] == "moe":
            B.phase_pf_moe(layer)
        else:
            B.phase_pf_bin(layer)
        B.phase_pc(layer)
        B.phase_pd(layer, "xin", "yout", "out")
    _prog_cache[key] = B.nc
    return B.nc


def _full_prog():
    if "full" in _prog_cache:
        return _prog_cache["full"]
    B = Builder(fused=True)
    B.alloc_fixed()
    with B.es:
        B.phase_consts()
        src = B.dram_in("xin", [S, D], F32)
        B.phase_pa(src)
        B.phase_pb(L0)
        B.phase_pf_moe(L0)
        B.phase_pc(L0)
        B.phase_pd(L0, "xin", "resid_s", "tmpout")
        B.phase_pa(B.dram["resid_s"])
        B.phase_pb(L1)
        B.phase_pf_bin(L1)
        B.phase_pc(L1)
        B.phase_pd(L1, "resid_s", "yout", "out")
    _prog_cache["full"] = B.nc
    return B.nc


def _weight_maps(inputs):
    consts = host_consts()
    perm = perm_interleave()
    f32c = lambda a: np.ascontiguousarray(np.asarray(a), dtype=np.float32)
    w0 = {
        **consts,
        "l0_wq": f32c(inputs["L0_attn_wq"]),
        "l0_wk": f32c(inputs["L0_attn_wk"]),
        "l0_wv": f32c(inputs["L0_attn_wv"]),
        "l0_wo": f32c(inputs["L0_attn_wo"]),
        "l0_A": f32c(inputs["L0_ffn_A"]),
        "l0_B": f32c(inputs["L0_ffn_B"]),
        "l0_Wr": f32c(inputs["L0_ffn_Wr"]),
    }
    w1 = {
        **consts,
        "l1_wq": f32c(np.asarray(inputs["L1_attn_wq"])[:, perm]),
        "l1_wk": f32c(np.asarray(inputs["L1_attn_wk"])[:, perm]),
        "l1_wv": f32c(inputs["L1_attn_wv"]),
        "l1_wo": f32c(inputs["L1_attn_wo"]),
        "l1_wreal": f32c(inputs["L1_ffn_weight_real"]),
        "l1_bias": f32c(np.asarray(inputs["L1_ffn_bias"]).reshape(1, D)),
    }
    return w0, w1


def kernel(**inputs):
    x = np.ascontiguousarray(np.asarray(inputs["L0_attn_x"], dtype=np.float32))
    w0, w1 = _weight_maps(inputs)
    w = dict(w0)
    w.update({k: v for k, v in w1.items()})
    p = _full_prog()
    im = [dict(w, xin=np.ascontiguousarray(x[c])) for c in range(8)]
    r = run_bass_kernel_spmd(p, im, core_ids=list(range(8)))
    return np.stack([r.results[c]["yout"] for c in range(8)], axis=0
                    ).astype(np.float32)


# version 07
# speedup vs baseline: 1.1623x; 1.1623x over previous
import contextlib
import math
import numpy as np
import ml_dtypes

import concourse.bass as bass
import concourse.mybir as mybir
from concourse.bass_utils import run_bass_kernel_spmd

F32 = mybir.dt.float32
BF16 = mybir.dt.bfloat16
U8 = mybir.dt.uint8
AF = mybir.ActivationFunctionType
ALU = mybir.AluOpType
AX = mybir.AxisListType

S = 2048
D = 4096
NCH = 32
NTT = 16
NTB = 4
HD = 128
SM = 1.0 / math.sqrt(HD)
EPS = 1e-5
ENGINES = ("sync", "scalar", "vector", "tensor", "gpsimd")

L0 = dict(name="l0", n_q=32, n_kv=8, window=1024, dk=1024, dv=1024, ffn="moe")
L1 = dict(name="l1", n_q=32, n_kv=32, window=None, dk=4096, dv=4096, ffn="bin")



def host_consts():
    half = 64
    inv = 10000.0 ** (-np.arange(half, dtype=np.float64) / half)
    ang = np.arange(S, dtype=np.float64)[None, :] * inv[:, None]
    cos = np.cos(ang).astype(np.float32)
    sin = np.sin(ang).astype(np.float32)
    cc = np.concatenate([cos, cos], axis=0).astype(ml_dtypes.bfloat16)
    ss = np.concatenate([-sin, sin], axis=0).astype(ml_dtypes.bfloat16)
    ccq = (np.concatenate([cos, cos], axis=0) * SM).astype(ml_dtypes.bfloat16)
    ssq = (np.concatenate([-sin, sin], axis=0) * SM).astype(ml_dtypes.bfloat16)

    ident = np.eye(128, dtype=ml_dtypes.bfloat16)
    r = np.arange(128)
    mask_diag = (r[:, None] <= r[None, :]).astype(ml_dtypes.bfloat16)
    mask_left = (r[:, None] > r[None, :]).astype(ml_dtypes.bfloat16)
    swap64 = np.zeros((128, 128), dtype=ml_dtypes.bfloat16)
    swap64[r, (r + 64) % 128] = 1.0
    consts4 = np.ascontiguousarray(
        np.stack([ident, mask_diag, mask_left, swap64], axis=1))
    return dict(cc=cc, ss=ss, ccq=ccq, ssq=ssq, consts4=consts4)


def perm_interleave():
    p = np.zeros(D, dtype=np.int64)
    for h in range(32):
        base = 128 * h
        for j in range(64):
            p[base + j] = base + 2 * j
            p[base + 64 + j] = base + 2 * j + 1
    return p



class Builder:
    def __init__(self, fused):
        self.nc = bass.Bass("TRN2", target_bir_lowering=False)
        self.es = contextlib.ExitStack()
        self.fused = fused
        self.phase_idx = 0
        self.dram = {}
        self._sem_n = 0
        self._sem_pool = []
        self._phase_sems = []
        self._nm = 0

    def sem(self, name):
        if self.fused and self._sem_pool:
            h = self._sem_pool.pop()
        else:
            self._sem_n += 1
            h = self.es.enter_context(self.nc.semaphore(f"{name}_{self._sem_n}"))
        self._phase_sems.append(h)
        return h

    def dram_in(self, name, shape, dtype):
        if name not in self.dram:
            self.dram[name] = self.nc.dram_tensor(name, shape, dtype,
                                                  kind="ExternalInput").ap()
        return self.dram[name]

    def dram_tmp(self, name, shape, dtype, out=False, inp=False):
        if name in self.dram:
            return self.dram[name]
        if self.fused:
            kind = "Internal"
        else:
            kind = "ExternalOutput" if out else ("ExternalInput" if inp else "Internal")
        self.dram[name] = self.nc.dram_tensor(name, shape, dtype, kind=kind).ap()
        return self.dram[name]

    def dram_out(self, name, shape, dtype):
        if name not in self.dram:
            self.dram[name] = self.nc.dram_tensor(name, shape, dtype,
                                                  kind="ExternalOutput").ap()
        return self.dram[name]

    def alloc_fixed(self):
        nc = self.nc
        self.big_t = nc.alloc_sbuf_tensor("big", [128, NCH, S], BF16)
        self.big = self.big_t.ap()
        self.arenaA = nc.alloc_sbuf_tensor("arenaA", [128, 49152], U8)
        self.arenaB = nc.alloc_sbuf_tensor("arenaB", [128, 14336], U8)
        self.cc = nc.alloc_sbuf_tensor("cc_sb", [128, S], BF16).ap()
        self.ss = nc.alloc_sbuf_tensor("ss_sb", [128, S], BF16).ap()
        self.ccq = nc.alloc_sbuf_tensor("ccq_sb", [128, S], BF16).ap()
        self.ssq = nc.alloc_sbuf_tensor("ssq_sb", [128, S], BF16).ap()
        self.consts4 = nc.alloc_sbuf_tensor("consts4_sb", [128, 4, 128], BF16).ap()
        self.smalls = nc.alloc_sbuf_tensor("smalls", [128, 64], F32).ap()
        self.onesrow = nc.alloc_sbuf_tensor("onesrow", [1, 128], BF16).ap()
        self.ident = self.consts4[:, 0, :]
        self.mask_diag = self.consts4[:, 1, :]
        self.mask_left = self.consts4[:, 2, :]
        self.swap64 = self.consts4[:, 3, :]
        self.eps = self.smalls[:, 0:1]
        self.bar = self.es.enter_context(nc.semaphore("bar"))
        self.block = self.es.enter_context(nc.Block())

    def uname(self, base):
        self._nm += 1
        return f"{base}_{self._nm}"

    def carve(self, arena, off_bytes, shape, dtype):
        bs = mybir.dt.size(dtype)
        assert off_bytes % bs == 0
        size = int(np.prod(shape[1:]))
        start = off_bytes // bs
        t = arena.bitcast(dtype)
        ap = t.ap()[0:int(shape[0]), start:start + size]
        if len(shape) > 2:
            names = " ".join(f"d{i}" for i in range(1, len(shape)))
            kw = {f"d{i}": int(shape[i]) for i in range(1, len(shape))}
            ap = ap.rearrange(f"p ({names}) -> p {names}", **kw)
        return ap

    def tpose(self, eng, out, in_, ident=None):
        eng.wait_ge(self.bar, 0)
        return eng.transpose(out, in_, self.ident if ident is None else ident)

    def tpose_end(self, eng):
        eng.wait_ge(self.bar, 0)

    def emit_phase(self, bodies):
        k = self.phase_idx
        bar = self.bar
        phase_sems = list(self._phase_sems)
        for ename in ENGINES:
            fn = bodies.get(ename)

            def wrapper(eng, fn=fn, k=k, ename=ename):
                if k > 0:
                    eng.wait_ge(bar, 6 * k)
                if fn is not None:
                    fn(eng)
                if ename == "sync":
                    eng.sem_inc(bar, 1)
                else:
                    eng.drain().then_inc(bar, 1)
                if ename == "vector":
                    eng.wait_ge(bar, 6 * k + 5)
                    for s in phase_sems:
                        eng.sem_clear(s)
                    eng.sem_inc(bar, 1)

            getattr(self.block, ename)(wrapper)
        self.phase_idx += 1
        self._sem_pool.extend(self._phase_sems)
        self._phase_sems = []

    def phase_consts(self):
        cc_in = self.dram_in("cc", [128, S], BF16)
        ss_in = self.dram_in("ss", [128, S], BF16)
        ccq_in = self.dram_in("ccq", [128, S], BF16)
        ssq_in = self.dram_in("ssq", [128, S], BF16)
        c4_in = self.dram_in("consts4", [128, 4, 128], BF16)
        s_cl = self.sem("cl")

        def b_sync(eng):
            eng.dma_start(out=self.cc, in_=cc_in).then_inc(s_cl, 16)
            eng.dma_start(out=self.ss, in_=ss_in).then_inc(s_cl, 16)
            eng.dma_start(out=self.ccq, in_=ccq_in).then_inc(s_cl, 16)
            eng.dma_start(out=self.ssq, in_=ssq_in).then_inc(s_cl, 16)
            eng.dma_start(out=self.consts4, in_=c4_in).then_inc(s_cl, 16)
            eng.wait_ge(s_cl, 80)

        def b_vector(eng):
            eng.memset(self.eps, EPS)
            eng.memset(self.onesrow, 1.0)

        self.emit_phase(dict(sync=b_sync, vector=b_vector))

    def phase_pa(self, src, dump_xnt=False):
        nc = self.nc
        xtile = self.carve(self.arenaA, 0, [128, 2, D], F32)
        xn = self.carve(self.arenaA, 32768, [128, D], BF16)
        sqtrash = self.carve(self.arenaA, 40960, [128, D], BF16)
        stats = self.carve(self.arenaB, 0, [128, 2, 8], F32)

        s_ld = [self.sem("pa_ld"), self.sem("pa_ld")]
        s_sq = self.sem("pa_sq")
        s_var = self.sem("pa_var")
        s_rs = self.sem("pa_rs")
        s_nm = self.sem("pa_nm")
        s_tp = self.sem("pa_tp")
        s_ev = self.sem("pa_ev")
        s_st = self.sem("pa_st")

        with nc.psum_tensor(self.uname("pa_tpA"), [128, 4, 128], BF16) as tpA, \
             nc.psum_tensor(self.uname("pa_tpB"), [128, 4, 128], BF16) as tpB:
            tp = [tpA.ap(), tpB.ap()]

            def b_sync(eng):
                for tt in range(NTT):
                    if tt >= 2:
                        eng.wait_ge(s_nm, tt - 1)
                    eng.dma_start(out=xtile[:, tt % 2, :],
                                  in_=src[tt * 128:(tt + 1) * 128, :]
                                  ).then_inc(s_ld[tt % 2], 16)

            def b_scalar(eng):
                for tt in range(NTT):
                    eng.wait_ge(s_ld[tt % 2], 16 * (tt // 2 + 1))
                    eng.activation(sqtrash[:, :], xtile[:, tt % 2, :], AF.Square,
                                   accum_out=stats[:, tt % 2, 1:2])
                    eng.drain().then_inc(s_sq, 1)
                    eng.wait_ge(s_var, tt + 1)
                    eng.activation(stats[:, tt % 2, 4:5], stats[:, tt % 2, 3:4],
                                   AF.Ln, bias=self.eps)
                    eng.activation(stats[:, tt % 2, 4:5], stats[:, tt % 2, 4:5],
                                   AF.Exp, scale=-0.5)
                    eng.drain().then_inc(s_rs, 1)
                if dump_xnt:
                    dst = self.dram_tmp("xnt_s", [D, S], BF16, out=True)
                    eng.wait_ge(s_ev, NTT * 8)
                    for ic in range(NCH):
                        eng.dma_start(out=dst[ic * 128:(ic + 1) * 128, :],
                                      in_=self.big[:, ic, :]).then_inc(s_st, 16)
                    eng.wait_ge(s_st, 16 * NCH)

            def b_vector(eng):
                for tt in range(NTT):
                    eng.wait_ge(s_ld[tt % 2], 16 * (tt // 2 + 1))
                    eng.tensor_reduce(out=stats[:, tt % 2, 0:1],
                                      in_=xtile[:, tt % 2, :], axis=AX.X, op=ALU.add)
                    eng.wait_ge(s_sq, tt + 1)
                    eng.drain()
                    eng.tensor_scalar(out=stats[:, tt % 2, 2:3],
                                      in0=stats[:, tt % 2, 0:1],
                                      scalar1=1.0 / D, scalar2=None, op0=ALU.mult)
                    eng.tensor_scalar(out=stats[:, tt % 2, 3:4],
                                      in0=stats[:, tt % 2, 1:2],
                                      scalar1=1.0 / D, scalar2=None, op0=ALU.mult)
                    eng.drain()
                    eng.scalar_tensor_tensor(out=stats[:, tt % 2, 5:6],
                                             in0=stats[:, tt % 2, 2:3], scalar=-1.0,
                                             in1=stats[:, tt % 2, 2:3],
                                             op0=ALU.mult, op1=ALU.mult)
                    eng.drain()
                    eng.tensor_add(stats[:, tt % 2, 3:4], stats[:, tt % 2, 3:4],
                                   stats[:, tt % 2, 5:6])
                    eng.drain().then_inc(s_var, 1)
                    eng.wait_ge(s_rs, tt + 1)
                    if tt >= 1:
                        eng.wait_ge(s_tp, 8 * tt)
                    eng.tensor_scalar(out=xn[:, :], in0=xtile[:, tt % 2, :],
                                      scalar1=stats[:, tt % 2, 2:3],
                                      scalar2=stats[:, tt % 2, 4:5],
                                      op0=ALU.subtract, op1=ALU.mult)
                    eng.drain().then_inc(s_nm, 1)
                    for p in range(8):
                        gidx = tt * 8 + p
                        eng.wait_ge(s_tp, gidx + 1)
                        eng.tensor_copy(
                            self.big[:, p * 4:p * 4 + 4, tt * 128:(tt + 1) * 128],
                            tp[p % 2][:, :, :])
                        eng.drain().then_inc(s_ev, 1)

            def b_tensor(eng):
                for tt in range(NTT):
                    eng.wait_ge(s_nm, tt + 1)
                    for p in range(8):
                        gidx = tt * 8 + p
                        if gidx >= 2:
                            eng.wait_ge(s_ev, gidx - 1)
                        for j in range(4):
                            ic = p * 4 + j
                            ii = self.tpose(eng, tp[p % 2][:, j, :],
                                            xn[:, ic * 128:(ic + 1) * 128])
                        ii.then_inc(s_tp, 1)
                        self.tpose_end(eng)

            self.emit_phase(dict(sync=b_sync, scalar=b_scalar, vector=b_vector,
                                 tensor=b_tensor))

    def phase_pb(self, L, load_xnt=False, only=None):
        nc = self.nc
        ln = L["name"]
        wq = self.dram_in(f"{ln}_wq", [D, D], F32)
        wk = self.dram_in(f"{ln}_wk", [D, L["dk"]], F32)
        wv = self.dram_in(f"{ln}_wv", [D, L["dv"]], F32)
        qt_s = self.dram_tmp("qt_s", [D, S], BF16, out=True)
        kt_s = self.dram_tmp("kt_s", [D, S], BF16, out=True)
        v_s = self.dram_tmp("v_s", [S, D], BF16, out=True)

        GW = 256
        wbuf = self.carve(self.arenaA, 0, [128, 2, NCH, GW], BF16)
        qk_sb = self.carve(self.arenaA, 32768, [128, 2, S], BF16)
        rope_t = self.carve(self.arenaA, 40960, [128, 2, 512], BF16)
        rope_sw = self.carve(self.arenaA, 43008, [128, 2, 512], BF16)
        rope_m = self.carve(self.arenaA, 45056, [128, 2, 512], BF16)
        v_sb = self.carve(self.arenaA, 47104, [128, 2, GW], BF16)
        stage = self.carve(self.arenaB, 0, [128, 4, GW], F32)

        s_wl = [self.sem("pb_wl") for _ in range(4)]
        s_wc = self.sem("pb_wc")
        s_wcf = self.sem("pb_wcf")
        s_peg = self.sem("pb_peg")
        s_pet = self.sem("pb_pet")
        s_pesw = self.sem("pb_pesw")
        s_ev1 = self.sem("pb_ev1")
        s_ev2 = self.sem("pb_ev2")
        s_rp = self.sem("pb_rp")
        s_rst = [self.sem("pb_rst"), self.sem("pb_rst")]
        s_vst = [self.sem("pb_vst"), self.sem("pb_vst")]
        s_ldb = self.sem("pb_ldb")

        projs = [(wq, D, "fm", qt_s), (wk, L["dk"], "fm", kt_s),
                 (wv, L["dv"], "tm", v_s)]
        if only == "v":
            projs = projs[2:]
        elif only == "q":
            projs = projs[:1]
        elif only == "qk":
            projs = projs[:2]
        NSUB = GW // 128

        groups = []
        chunks = []
        for pi, (w, dout, kind, dst) in enumerate(projs):
            for g in range(dout // GW):
                groups.append((pi, g))
                for ic in range(NCH):
                    chunks.append((len(groups) - 1, pi, g, ic))

        tiles = []
        for gi, (pi, g) in enumerate(groups):
            kind = projs[pi][2]
            if kind == "fm":
                for sub in range(NSUB):
                    for tb in range(NTB):
                        tiles.append(dict(gi=gi, kind="fm", pi=pi, g=g,
                                          sub=sub, tb=tb))
            else:
                for tt in range(NTT):
                    tiles.append(dict(gi=gi, kind="tm", pi=pi, g=g, tt=tt))

        fm_tiles = [ti for ti, t in enumerate(tiles) if t["kind"] == "fm"]
        fmidx = {ti: fi for fi, ti in enumerate(fm_tiles)}
        nfm = len(fm_tiles)
        rowidx = {}
        nrow = -1
        row_last_fi = {}
        for ti in fm_tiles:
            t = tiles[ti]
            if t["tb"] == 0:
                nrow += 1
            rowidx[ti] = nrow
            if t["tb"] == NTB - 1:
                row_last_fi[nrow] = fmidx[ti]
        row_info = {}
        for ti in fm_tiles:
            t = tiles[ti]
            if t["tb"] == NTB - 1:
                r = rowidx[ti]
                row_info[r] = (t["pi"], t["g"] * NSUB + t["sub"])
        vrow_of = {}
        nv = -1
        for ti, t in enumerate(tiles):
            if t["kind"] == "tm":
                nv += 1
                vrow_of[ti] = nv

        def b_sync(eng):
            if load_xnt and not self.fused:
                srcb = self.dram_tmp("xnt_s", [D, S], BF16, inp=True)
                for ic in range(NCH):
                    eng.dma_start(out=self.big[:, ic, :],
                                  in_=srcb[ic * 128:(ic + 1) * 128, :]
                                  ).then_inc(s_ldb, 16)
            for j, (gi, pi, g, ic) in enumerate(chunks):
                w = projs[pi][0]
                if j >= 4:
                    eng.wait_ge(s_wcf, j - 3)
                eng.dma_start(out=stage[:, j % 4, :],
                              in_=w[ic * 128:(ic + 1) * 128, g * GW:(g + 1) * GW]
                              ).then_inc(s_wl[j % 4], 16)

        def b_gpsimd(eng):
            for j, (gi, pi, g, ic) in enumerate(chunks):
                if gi >= 2 and ic == 0:
                    eng.wait_ge(s_peg, gi - 1)
                eng.wait_ge(s_wl[j % 4], 16 * (j // 4 + 1))
                eng.tensor_copy(wbuf[:, gi % 2, ic, :],
                                stage[:, j % 4, :]).then_inc(s_wcf, 1)
                if ic == NCH - 1:
                    eng.drain().then_inc(s_wc, 1)

        with nc.psum_tensor(self.uname("pb_ps"), [128, 4, 512], F32) as ps_t, \
             nc.psum_tensor(self.uname("pb_psw"), [128, 2, 512], F32) as psw_t:
            psum = ps_t.ap()
            psw = psw_t.ap()

            def pe_swap(eng, fi):
                eng.wait_ge(s_ev1, ev1_count_of_fm[fi])
                if fi >= 2:
                    eng.wait_ge(s_ev2, fi - 1)
                eng.matmul(psw[:, fi % 2, :], self.swap64, rope_t[:, fi % 2, :],
                           start=True, stop=True).then_inc(s_pesw, 1)

            def b_tensor(eng):
                if load_xnt and not self.fused:
                    eng.wait_ge(s_ldb, 16 * NCH)
                for ti, t in enumerate(tiles):
                    if t["kind"] == "tm" and ti > 0 and tiles[ti - 1]["kind"] == "fm":
                        pe_swap(eng, nfm - 1)
                    eng.wait_ge(s_wc, t["gi"] + 1)
                    if ti >= 4:
                        eng.wait_ge(s_ev1, ti - 3)
                    if t["kind"] == "fm":
                        ps = psum[:, ti % 4, :]
                        for ic in range(NCH):
                            ii = eng.matmul(
                                ps,
                                wbuf[:, t["gi"] % 2, ic,
                                     t["sub"] * 128:(t["sub"] + 1) * 128],
                                self.big[:, ic, t["tb"] * 512:(t["tb"] + 1) * 512],
                                start=(ic == 0), stop=(ic == NCH - 1))
                    else:
                        ps = psum[:, ti % 4, 0:GW]
                        tt = t["tt"]
                        for ic in range(NCH):
                            ii = eng.matmul(
                                ps,
                                self.big[:, ic, tt * 128:(tt + 1) * 128],
                                wbuf[:, t["gi"] % 2, ic, :],
                                start=(ic == 0), stop=(ic == NCH - 1))
                    ii.then_inc(s_pet, 1)
                    if t["kind"] == "fm" and fmidx[ti] >= 1:
                        pe_swap(eng, fmidx[ti] - 1)
                    if ti + 1 == len(tiles) or tiles[ti + 1]["gi"] != t["gi"]:
                        eng.drain().then_inc(s_peg, 1)
                if tiles[-1]["kind"] == "fm":
                    pe_swap(eng, nfm - 1)

            ev1_count_of_fm = {}
            for ti in fm_tiles:
                ev1_count_of_fm[fmidx[ti]] = ti + 1

            def act_evac2(eng, fi):
                eng.wait_ge(s_pesw, fi + 1)
                if fi >= 2:
                    eng.wait_ge(s_rp, fi - 1)
                eng.activation(rope_sw[:, fi % 2, :], psw[:, fi % 2, :], AF.Copy)
                eng.drain().then_inc(s_ev2, 1)

            def b_scalar(eng):
                nrow_st = [0, 0]
                nv_st = [0, 0]
                pending_rows = []
                for ti, t in enumerate(tiles):
                    if t["kind"] == "tm" and ti > 0 and tiles[ti - 1]["kind"] == "fm":
                        act_evac2(eng, nfm - 1)
                        flush_rows(eng, nfm, nrow_st, pending_rows)
                    eng.wait_ge(s_pet, ti + 1)
                    ps = psum[:, ti % 4, :] if t["kind"] == "fm" else psum[:, ti % 4, 0:GW]
                    if t["kind"] == "fm":
                        fi = fmidx[ti]
                        if fi >= 2:
                            eng.wait_ge(s_rp, fi - 1)
                        eng.activation(rope_t[:, fi % 2, :], ps, AF.Copy)
                        eng.drain().then_inc(s_ev1, 1)
                        if fi >= 1:
                            act_evac2(eng, fi - 1)
                        if t["tb"] == NTB - 1:
                            pending_rows.append(rowidx[ti])
                        flush_rows(eng, fi, nrow_st, pending_rows)
                    else:
                        vr = vrow_of[ti]
                        if vr >= 2:
                            eng.wait_ge(s_vst[vr % 2], 16 * (vr // 2))
                        eng.activation(v_sb[:, vr % 2, :], ps, AF.Copy)
                        eng.drain().then_inc(s_ev1, 1)
                        tt = t["tt"]
                        dst = projs[t["pi"]][3]
                        eng.dma_start(
                            out=dst[tt * 128:(tt + 1) * 128,
                                    t["g"] * GW:(t["g"] + 1) * GW],
                            in_=v_sb[:, vr % 2, :]).then_inc(s_vst[vr % 2], 16)
                        nv_st[vr % 2] += 1
                if tiles[-1]["kind"] == "fm":
                    act_evac2(eng, nfm - 1)
                    flush_rows(eng, nfm, nrow_st, pending_rows)
                eng.wait_ge(s_rst[0], 16 * nrow_st[0])
                eng.wait_ge(s_rst[1], 16 * nrow_st[1])
                eng.wait_ge(s_vst[0], 16 * nv_st[0])
                eng.wait_ge(s_vst[1], 16 * nv_st[1])

            def flush_rows(eng, ropes_emitted, nrow_st, pending_rows):
                while pending_rows and row_last_fi[pending_rows[0]] + 1 <= ropes_emitted:
                    r = pending_rows.pop(0)
                    eng.wait_ge(s_rp, row_last_fi[r] + 1)
                    pi, dchunk = row_info[r]
                    dst = projs[pi][3]
                    eng.dma_start(out=dst[dchunk * 128:(dchunk + 1) * 128, :],
                                  in_=qk_sb[:, r % 2, :]).then_inc(s_rst[r % 2], 16)
                    nrow_st[r % 2] += 1

            def dve_rope(eng, fi):
                ti = fm_tiles[fi]
                t = tiles[ti]
                r = rowidx[ti]
                tb = t["tb"]
                eng.wait_ge(s_ev2, fi + 1)
                if tb == 0 and r >= 2:
                    eng.wait_ge(s_rst[r % 2], 16 * (r // 2))
                if t["pi"] == 0:
                    cslice = self.ccq[:, tb * 512:(tb + 1) * 512]
                    sslice = self.ssq[:, tb * 512:(tb + 1) * 512]
                else:
                    cslice = self.cc[:, tb * 512:(tb + 1) * 512]
                    sslice = self.ss[:, tb * 512:(tb + 1) * 512]
                eng.tensor_mul(rope_m[:, 0, :], rope_t[:, fi % 2, :], cslice)
                eng.tensor_mul(rope_m[:, 1, :], rope_sw[:, fi % 2, :], sslice)
                eng.drain()
                eng.tensor_add(qk_sb[:, r % 2, tb * 512:(tb + 1) * 512],
                               rope_m[:, 0, :], rope_m[:, 1, :])
                eng.drain().then_inc(s_rp, 1)

            def b_vector(eng):
                for fi in range(1, nfm):
                    dve_rope(eng, fi - 1)
                if nfm:
                    dve_rope(eng, nfm - 1)

            self.emit_phase(dict(sync=b_sync, gpsimd=b_gpsimd, tensor=b_tensor,
                                 scalar=b_scalar, vector=b_vector))

    def phase_pf_bin(self, L, load_xnt=False):
        nc = self.nc
        ln = L["name"]
        wr = self.dram_in(f"{ln}_wreal", [D, D], F32)
        bias = self.dram_in(f"{ln}_bias", [1, D], F32)
        delta_s = self.dram_tmp("delta_s", [S, D], F32, out=True)

        wtb = self.carve(self.arenaA, 0, [128, 2, NCH, 256], BF16)
        sgn_tmp = self.carve(self.arenaA, 32768, [128, D], BF16)
        biasb = self.carve(self.arenaA, 40960, [1, D], BF16)
        stage = self.carve(self.arenaB, 0, [128, 2, 1024], F32)
        dsb = self.carve(self.arenaB, 8192, [128, 2, 256], F32)
        al = self.smalls

        s_wld = [self.sem("pf_wld"), self.sem("pf_wld")]
        s_bld = self.sem("pf_bld")
        s_alq = self.sem("pf_alq")
        s_sc = self.sem("pf_sc")
        s_sg = self.sem("pf_sg")
        s_tpk = self.sem("pf_tpk")
        s_te = self.sem("pf_te")
        s_mm = self.sem("pf_mm")
        s_de = self.sem("pf_de")
        s_dst = [self.sem("pf_dst"), self.sem("pf_dst")]
        s_ldb = self.sem("pf_ldb")

        NOT_ = NCH

        with nc.psum_tensor(self.uname("pf_tpA"), [128, 4, 128], BF16) as tpA, \
             nc.psum_tensor(self.uname("pf_tpB"), [128, 4, 128], BF16) as tpB, \
             nc.psum_tensor(self.uname("pf_mmp"), [128, 2, 512], F32) as mm_t:
            tp = [tpA.ap(), tpB.ap()]
            mmp = mm_t.ap()

            def b_sync(eng):
                if load_xnt and not self.fused:
                    srcb = self.dram_tmp("xnt_s", [D, S], BF16, inp=True)
                    for ic in range(NCH):
                        eng.dma_start(out=self.big[:, ic, :],
                                      in_=srcb[ic * 128:(ic + 1) * 128, :]
                                      ).then_inc(s_ldb, 16)
                for j in range(NOT_ * 4):
                    ot, q = divmod(j, 4)
                    if j >= 2:
                        eng.wait_ge(s_sg, j - 1)
                        eng.wait_ge(s_alq, j - 1)
                    eng.dma_start(out=stage[:, j % 2, :],
                                  in_=wr[ot * 128:(ot + 1) * 128,
                                         q * 1024:(q + 1) * 1024]
                                  ).then_inc(s_wld[j % 2], 16)

            def b_gpsimd(eng):
                eng.dma_start(out=biasb, in_=bias).then_inc(s_bld, 16)
                nst = [0, 0]
                for mi in range(16 * NTT):
                    band, tt = divmod(mi, NTT)
                    eng.wait_ge(s_de, mi + 1)
                    eng.dma_start(out=delta_s[tt * 128:(tt + 1) * 128,
                                              band * 256:(band + 1) * 256],
                                  in_=dsb[:, mi % 2, :]).then_inc(s_dst[mi % 2], 16)
                    nst[mi % 2] += 1
                eng.wait_ge(s_dst[0], 16 * nst[0])
                eng.wait_ge(s_dst[1], 16 * nst[1])

            def b_vector(eng):
                for ot in range(NOT_):
                    for q in range(4):
                        j = ot * 4 + q
                        eng.wait_ge(s_wld[j % 2], 16 * (j // 2 + 1))
                        eng.tensor_reduce(out=al[:, 8 + q:9 + q],
                                          in_=stage[:, j % 2, :], axis=AX.X,
                                          op=ALU.add, apply_absolute_value=True
                                          ).then_inc(s_alq, 1)
                    eng.drain()
                    eng.tensor_reduce(out=al[:, 12:13], in_=al[:, 8:12],
                                      axis=AX.X, op=ALU.add)
                    eng.drain()
                    eng.tensor_scalar(out=al[:, 12:13], in0=al[:, 12:13],
                                      scalar1=1.0 / D, scalar2=None, op0=ALU.mult)
                    eng.drain()
                    eng.wait_ge(s_sg, 4 * (ot + 1))
                    eng.tensor_scalar(out=sgn_tmp[:, :], in0=sgn_tmp[:, :],
                                      scalar1=al[:, 12:13], scalar2=None,
                                      op0=ALU.mult)
                    eng.drain().then_inc(s_sc, 1)
                    if ot % 2 == 1:
                        band = ot // 2
                        for j in range(NTT):
                            mi = band * NTT + j
                            eng.wait_ge(s_mm, mi + 1)
                            if mi >= 2:
                                eng.wait_ge(s_dst[mi % 2], 16 * (mi // 2))
                            eng.tensor_copy(dsb[:, mi % 2, :],
                                            mmp[:, mi % 2, 0:256])
                            eng.drain().then_inc(s_de, 1)

            def b_scalar(eng):
                for ot in range(NOT_):
                    for q in range(4):
                        j = ot * 4 + q
                        eng.wait_ge(s_wld[j % 2], 16 * (j // 2 + 1))
                        if q == 0 and ot >= 1:
                            eng.wait_ge(s_tpk, 8 * ot)
                        eng.activation(sgn_tmp[:, q * 1024:(q + 1) * 1024],
                                       stage[:, j % 2, :], AF.Sign)
                        eng.drain().then_inc(s_sg, 1)
                    band, half = ot // 2, ot % 2
                    if band >= 2 and half == 0:
                        eng.wait_ge(s_mm, 16 * (band - 1))
                    for pk in range(8):
                        g = ot * 8 + pk
                        eng.wait_ge(s_tpk, g + 1)
                        eng.activation(wtb[:, band % 2, pk * 4:pk * 4 + 4,
                                           half * 128:(half + 1) * 128],
                                       tp[pk % 2][:, :, :], AF.Copy)
                        eng.drain().then_inc(s_te, 1)

            def b_tensor(eng):
                if load_xnt and not self.fused:
                    eng.wait_ge(s_ldb, 16 * NCH)
                eng.wait_ge(s_bld, 16)
                for ot in range(NOT_):
                    eng.wait_ge(s_sc, ot + 1)
                    for pk in range(8):
                        g = ot * 8 + pk
                        if g >= 2:
                            eng.wait_ge(s_te, g - 1)
                        for j in range(4):
                            ic = pk * 4 + j
                            ii = self.tpose(eng, tp[pk % 2][:, j, :],
                                            sgn_tmp[:, ic * 128:(ic + 1) * 128])
                        ii.then_inc(s_tpk, 1)
                        self.tpose_end(eng)
                    if ot % 2 == 1:
                        band = ot // 2
                        eng.wait_ge(s_te, 8 * (ot + 1))
                        for tt in range(NTT):
                            mi = band * NTT + tt
                            if mi >= 2:
                                eng.wait_ge(s_de, mi - 1)
                            p = mmp[:, mi % 2, 0:256]
                            for ic in range(NCH):
                                eng.matmul(p,
                                           self.big[:, ic, tt * 128:(tt + 1) * 128],
                                           wtb[:, band % 2, ic, :],
                                           start=(ic == 0), stop=False)
                            eng.matmul(p, self.onesrow,
                                       biasb[:, band * 256:(band + 1) * 256],
                                       start=False, stop=True).then_inc(s_mm, 1)

            self.emit_phase(dict(sync=b_sync, gpsimd=b_gpsimd, vector=b_vector,
                                 scalar=b_scalar, tensor=b_tensor))

    def phase_pf_moe(self, L, load_xnt=False):
        nc = self.nc
        ln = L["name"]
        A_in = self.dram_in(f"{ln}_A", [4, 16, D], F32)
        B_in = self.dram_in(f"{ln}_B", [4, D, 16], F32)
        Wr_in = self.dram_in(f"{ln}_Wr", [4, D], F32)
        delta_s = self.dram_tmp("delta_s", [S, D], F32, out=True)
        SCALE = 4.0

        AT = self.carve(self.arenaA, 0, [128, NCH, 64], BF16)
        BT = self.carve(self.arenaA, 4096, [64, NCH * 128], BF16)
        BT3 = self.carve(self.arenaA, 4096, [64, NCH, 128], BF16)
        WrT = self.carve(self.arenaA, 12288, [128, NCH, 4], BF16)
        A_sb = self.carve(self.arenaA, 12544, [64, D], BF16)
        wr_sb = self.carve(self.arenaA, 20736, [4, D], BF16)
        Bst = self.carve(self.arenaA, 28928, [128, 2, NCH, 32], BF16)
        mid_sb = self.carve(self.arenaA, 33024, [128, 2, 64], BF16)
        midT_sb = self.carve(self.arenaA, 33280, [64, 2, 128], BF16)
        gsc = self.carve(self.arenaA, 34304, [128, 2, 16], F32)
        dsb = self.carve(self.arenaB, 0, [128, 2, 512], F32)

        s_ldc = self.sem("pf0_ldc")
        s_stp = self.sem("pf0_stp")
        s_ste = self.sem("pf0_ste")
        s_rt = self.sem("pf0_rt")
        s_gt = self.sem("pf0_gt")
        s_gs = self.sem("pf0_gs")
        s_mi = self.sem("pf0_mi")
        s_me = self.sem("pf0_me")
        s_mt = self.sem("pf0_mt")
        s_mte = self.sem("pf0_mte")
        s_dm = self.sem("pf0_dm")
        s_dev = self.sem("pf0_dev")
        s_dst = [self.sem("pf0_dst"), self.sem("pf0_dst")]
        s_ldb = self.sem("pf0_ldb")

        NOG = 8
        with nc.psum_tensor(self.uname("pf0_rp"), [128, 4], F32) as rp_t, \
             nc.psum_tensor(self.uname("pf0_mid"), [128, 64], F32) as mid_t, \
             nc.psum_tensor(self.uname("pf0_tp"), [128, 128], BF16) as tp_t, \
             nc.psum_tensor(self.uname("pf0_tp2"), [128, 128], BF16) as tp2_t, \
             nc.psum_tensor(self.uname("pf0_dp"), [128, 2, 512], F32) as dp_t:
            rp = rp_t.ap()
            midp = mid_t.ap()
            tpb = [tp_t.ap(), tp2_t.ap()]
            dp = dp_t.ap()

            def b_gpsimd(eng):
                eng.dma_start(out=A_sb[:, :],
                              in_=A_in.rearrange("e r d -> (e r) d")
                              ).then_inc(s_ldc, 16)
                eng.dma_start(out=wr_sb[:, :], in_=Wr_in).then_inc(s_ldc, 16)
                for e in range(4):
                    eng.dma_start(out=Bst[:, e // 2, :, (e % 2) * 16:(e % 2) * 16 + 16],
                                  in_=B_in[e].rearrange("(ic p) r -> p ic r", p=128)
                                  ).then_inc(s_ldc, 16)

            def b_sync(eng):
                if load_xnt and not self.fused:
                    srcb = self.dram_tmp("xnt_s", [D, S], BF16, inp=True)
                    for ic in range(NCH):
                        eng.dma_start(out=self.big[:, ic, :],
                                      in_=srcb[ic * 128:(ic + 1) * 128, :]
                                      ).then_inc(s_ldb, 16)
                nst = [0, 0]
                for mi in range(NTT * NOG):
                    tt, og = divmod(mi, NOG)
                    eng.wait_ge(s_dev, mi + 1)
                    eng.dma_start(out=delta_s[tt * 128:(tt + 1) * 128,
                                              og * 512:(og + 1) * 512],
                                  in_=dsb[:, mi % 2, :]).then_inc(s_dst[mi % 2], 16)
                    nst[mi % 2] += 1
                eng.wait_ge(s_dst[0], 16 * nst[0])
                eng.wait_ge(s_dst[1], 16 * nst[1])

            def b_tensor(eng):
                if load_xnt and not self.fused:
                    eng.wait_ge(s_ldb, 16 * NCH)
                eng.wait_ge(s_ldc, 96)
                g = 0
                for ic in range(NCH):
                    if g >= 2:
                        eng.wait_ge(s_ste, g - 1)
                    ii = self.tpose(eng, tpb[g % 2][:, 0:64],
                                    A_sb[0:64, ic * 128:(ic + 1) * 128],
                                    ident=self.ident[0:64, 0:64])
                    ii.then_inc(s_stp, 1)
                    self.tpose_end(eng)
                    g += 1
                for ic in range(NCH):
                    if g >= 2:
                        eng.wait_ge(s_ste, g - 1)
                    ii = self.tpose(eng, tpb[g % 2][:, 0:4],
                                    wr_sb[0:4, ic * 128:(ic + 1) * 128],
                                    ident=self.ident[0:4, 0:4])
                    ii.then_inc(s_stp, 1)
                    self.tpose_end(eng)
                    g += 1
                for pair in range(2):
                    for ic in range(NCH):
                        if g >= 2:
                            eng.wait_ge(s_ste, g - 1)
                        ii = self.tpose(eng, tpb[g % 2][0:32, :],
                                        Bst[:, pair, ic, :])
                        ii.then_inc(s_stp, 1)
                        self.tpose_end(eng)
                        g += 1
                eng.wait_ge(s_ste, 128)
                for tt in range(NTT):
                    if tt >= 1:
                        eng.wait_ge(s_gt, tt)
                    for ic in range(NCH):
                        ii = eng.matmul(rp[:, :],
                                        self.big[:, ic, tt * 128:(tt + 1) * 128],
                                        WrT[:, ic, :],
                                        start=(ic == 0), stop=(ic == NCH - 1))
                    ii.then_inc(s_rt, 1)
                    if tt >= 1:
                        eng.wait_ge(s_me, tt)
                    for ic in range(NCH):
                        ii = eng.matmul(midp[:, :],
                                        self.big[:, ic, tt * 128:(tt + 1) * 128],
                                        AT[:, ic, :],
                                        start=(ic == 0), stop=(ic == NCH - 1))
                    ii.then_inc(s_mi, 1)
                    eng.wait_ge(s_me, tt + 1)
                    if tt >= 1:
                        eng.wait_ge(s_mte, tt)
                    ii = self.tpose(eng, tpb[1][0:64, :], mid_sb[:, tt % 2, :])
                    ii.then_inc(s_mt, 1)
                    self.tpose_end(eng)
                    eng.wait_ge(s_mte, tt + 1)
                    for og in range(NOG):
                        mi = tt * NOG + og
                        if mi >= 2:
                            eng.wait_ge(s_dev, mi - 1)
                        eng.matmul(dp[:, mi % 2, :], midT_sb[0:64, tt % 2, :],
                                   BT[0:64, og * 512:(og + 1) * 512],
                                   start=True, stop=True).then_inc(s_dm, 1)

            def b_scalar(eng):
                for g in range(128):
                    eng.wait_ge(s_stp, g + 1)
                    if g < 32:
                        eng.activation(AT[:, g, :], tpb[g % 2][0:128, 0:64],
                                       AF.Copy)
                    elif g < 64:
                        eng.activation(WrT[:, g - 32, :], tpb[g % 2][0:128, 0:4],
                                       AF.Copy)
                    else:
                        pair = (g - 64) // 32
                        ic = (g - 64) % 32
                        eng.activation(BT3[pair * 32:(pair + 1) * 32, ic, :],
                                       tpb[g % 2][0:32, 0:128], AF.Copy)
                    eng.drain().then_inc(s_ste, 1)
                for tt in range(NTT):
                    r = tt % 2
                    eng.wait_ge(s_gs, 4 * tt + 1)
                    eng.activation(gsc[:, r, 4:8], gsc[:, r, 0:4], AF.Sign)
                    eng.drain().then_inc(s_gs, 1)
                    eng.wait_ge(s_gs, 4 * tt + 3)
                    eng.activation(gsc[:, r, 8:12], gsc[:, r, 0:4], AF.Exp)
                    eng.drain().then_inc(s_gs, 1)

            def b_vector(eng):
                for tt in range(NTT):
                    r = tt % 2
                    lg = gsc[:, r, 0:4]
                    sm = self.smalls
                    eng.wait_ge(s_rt, tt + 1)
                    eng.tensor_copy(lg, rp[:, :])
                    eng.drain()
                    eng.tensor_reduce(out=sm[:, 16 + r:17 + r], in_=lg, axis=AX.X,
                                      op=ALU.max, negate=True)
                    eng.drain()
                    eng.tensor_scalar(out=lg, in0=lg, scalar1=sm[:, 16 + r:17 + r],
                                      scalar2=None, op0=ALU.add)
                    eng.drain().then_inc(s_gs, 1)
                    eng.wait_ge(s_gs, 4 * tt + 2)
                    eng.tensor_scalar(out=gsc[:, r, 4:8], in0=gsc[:, r, 4:8],
                                      scalar1=1.0, scalar2=-1e30,
                                      op0=ALU.add, op1=ALU.mult)
                    eng.drain()
                    eng.tensor_add(gsc[:, r, 12:16], lg, gsc[:, r, 4:8])
                    eng.drain()
                    eng.tensor_reduce(out=sm[:, 18 + r:19 + r], in_=gsc[:, r, 12:16],
                                      axis=AX.X, op=ALU.max)
                    eng.drain()
                    eng.tensor_scalar(out=gsc[:, r, 12:16], in0=lg,
                                      scalar1=sm[:, 18 + r:19 + r], scalar2=None,
                                      op0=ALU.is_ge)
                    eng.drain().then_inc(s_gs, 1)
                    eng.wait_ge(s_gs, 4 * tt + 4)
                    eng.tensor_mul(gsc[:, r, 4:8], gsc[:, r, 8:12],
                                   gsc[:, r, 12:16])
                    eng.drain()
                    eng.tensor_reduce(out=sm[:, 20 + r:21 + r], in_=gsc[:, r, 4:8],
                                      axis=AX.X, op=ALU.add)
                    eng.drain()
                    eng.reciprocal(sm[:, 22 + r:23 + r], sm[:, 20 + r:21 + r])
                    eng.drain()
                    eng.tensor_scalar(out=gsc[:, r, 8:12], in0=gsc[:, r, 4:8],
                                      scalar1=sm[:, 22 + r:23 + r], scalar2=SCALE,
                                      op0=ALU.mult, op1=ALU.mult)
                    eng.drain().then_inc(s_gt, 1)
                    eng.wait_ge(s_mi, tt + 1)
                    if tt >= 1:
                        eng.wait_ge(s_mt, tt)
                    for e in range(4):
                        eng.tensor_scalar(out=mid_sb[:, r, e * 16:(e + 1) * 16],
                                          in0=midp[:, e * 16:(e + 1) * 16],
                                          scalar1=gsc[:, r, 8 + e:9 + e],
                                          scalar2=None, op0=ALU.mult)
                    eng.drain().then_inc(s_me, 1)
                    eng.wait_ge(s_mt, tt + 1)
                    eng.tensor_copy(midT_sb[0:64, r, :], tpb[1][0:64, :])
                    eng.drain().then_inc(s_mte, 1)
                    for og in range(NOG):
                        mi = tt * NOG + og
                        eng.wait_ge(s_dm, mi + 1)
                        if mi >= 2:
                            eng.wait_ge(s_dst[mi % 2], 16 * (mi // 2))
                        eng.tensor_copy(dsb[:, mi % 2, :], dp[:, mi % 2, :])
                        eng.drain().then_inc(s_dev, 1)

            self.emit_phase(dict(sync=b_sync, gpsimd=b_gpsimd, tensor=b_tensor,
                                 scalar=b_scalar, vector=b_vector))


    def phase_pc(self, L, dump_o=False):
        nc = self.nc
        qt_s = self.dram_tmp("qt_s", [D, S], BF16, inp=True)
        kt_s = self.dram_tmp("kt_s", [D, S], BF16, inp=True)
        v_s = self.dram_tmp("v_s", [S, D], BF16, inp=True)
        n_q = L["n_q"]
        G = n_q // L["n_kv"]
        win = L["window"]
        NT = n_q * NTT

        qh = self.carve(self.arenaA, 0, [128, 2, S], BF16)
        kh = self.carve(self.arenaA, 8192, [128, 2, S], BF16)
        vh = self.carve(self.arenaA, 16384, [128, 2, 16, 132], BF16)
        probs = self.carve(self.arenaA, 24832, [128, 2, S], BF16)
        pt_sb = self.carve(self.arenaA, 33024, [128, 2, 16, 128], BF16)
        sm = self.smalls

        def geom(i):
            if win is None:
                lo = 0
            else:
                lo = max(0, i - 8)
            b0 = lo // 4
            nb = i // 4 - b0 + 1
            sbl = list(range(lo, i + 1))
            return lo, b0, nb, sbl

        packs_cum = [0]
        mask_cum = [0]
        sc_cum = [0]
        ex_cum = [0]
        nb_of = {}
        for t in range(NT):
            i = t % 16
            lo, b0, nb, sbl = geom(i)
            nb_of[t] = nb
            packs_cum.append(packs_cum[-1] + (len(sbl) + 3) // 4)
            sc_cum.append(sc_cum[-1] + nb)
            ex_cum.append(ex_cum[-1] + nb)
            nm = 1 + (1 if (win is not None and i >= 8) else 0)
            mask_cum.append(mask_cum[-1] + nm)
        def masked_blocks(i):
            lo, b0, nb, sbl = geom(i)
            mb = {sbl.index(i)}
            if win is not None and i >= 8:
                mb.add(sbl.index(i - 8))
            return mb

        def pack_engine(i, pk):
            lo, b0, nb, sbl = geom(i)
            mb = masked_blocks(i)
            if any((j in mb) for j in range(pk * 4, min(pk * 4 + 4, len(sbl)))):
                return "B"
            return "A" if pk % 2 == 1 else "B"

        packsA_cum = [0]
        packsB_cum = [0]
        packsA_cum_g = [0]
        packsB_cum_g = [0]
        for t in range(NT):
            i = t % 16
            lo, b0, nb, sbl = geom(i)
            npk = (len(sbl) + 3) // 4
            a = sum(1 for pk in range(npk) if pack_engine(i, pk) == "A")
            b = npk - a
            packsA_cum.append(packsA_cum[-1] + a)
            packsB_cum.append(packsB_cum[-1] + b)
            for pk in range(npk):
                isA = pack_engine(i, pk) == "A"
                packsA_cum_g.append(packsA_cum_g[-1] + (1 if isA else 0))
                packsB_cum_g.append(packsB_cum_g[-1] + (0 if isA else 1))

        s_q = [self.sem("pc_q"), self.sem("pc_q")]
        s_k = [self.sem("pc_k"), self.sem("pc_k")]
        s_hd = self.sem("pc_hd")
        s_kv = self.sem("pc_kv")
        s_sc = self.sem("pc_sc")
        s_mx = self.sem("pc_mx")
        s_ex = self.sem("pc_ex")
        s_ptk = self.sem("pc_ptk")
        s_pteA = self.sem("pc_pteA")
        s_pteB = self.sem("pc_pteB")
        s_ptwA = self.sem("pc_ptwA")
        s_ptwB = self.sem("pc_ptwB")
        s_ptm = self.sem("pc_ptm")
        s_pv = self.sem("pc_pv")
        s_oev = self.sem("pc_oev")
        s_ost = self.sem("pc_ost")
        s_vset = self.sem("pc_vset")

        def kv_of(h):
            return h // G

        def kvslot(h):
            return (kv_of(h) % 2) if win is not None else (h % 2)

        qrounds = {h: 16 * (h // 2 + 1) for h in range(n_q)}
        kvrounds = {}
        for h in range(n_q):
            kvh = kv_of(h) if win is not None else h
            kvrounds[h] = 16 * 17 * (kvh // 2 + 1)

        with nc.psum_tensor(self.uname("pc_sc"), [128, 4, 512], F32) as sc_t, \
             nc.psum_tensor(self.uname("pc_ptA"), [128, 4, 128], BF16) as ptA, \
             nc.psum_tensor(self.uname("pc_ptB"), [128, 4, 128], BF16) as ptB, \
             nc.psum_tensor(self.uname("pc_oA"), [128, 132], F32) as oA, \
             nc.psum_tensor(self.uname("pc_oB"), [128, 132], F32) as oB:
            scp = sc_t.ap().rearrange("p a b -> p (a b)")
            ptp = [ptA.ap(), ptB.ap()]
            op = [oA.ap(), oB.ap()]

            def b_sync(eng):
                for h in range(n_q):
                    if h >= 2:
                        eng.wait_ge(s_hd, h - 1)
                    eng.dma_start(out=qh[:, h % 2, :],
                                  in_=qt_s[h * 128:(h + 1) * 128, :]
                                  ).then_inc(s_q[h % 2], 16)
                    if win is None or h % G == 0:
                        kvh = kv_of(h)
                        slot = kvslot(h)
                        if win is not None and kvh >= 2:
                            eng.wait_ge(s_kv, kvh - 1)
                        krow = (kvh if win is not None else h) * 128
                        dv0 = kvh * 128 if win is not None else h * 128
                        eng.dma_start(out=kh[:, slot, :],
                                      in_=kt_s[krow:krow + 128, :]
                                      ).then_inc(s_k[slot], 16)
                        eng.wait_ge(s_vset, 1)
                        for sb in range(16):
                            eng.dma_start(out=vh[:, slot, sb, 0:128],
                                          in_=v_s[sb * 128:(sb + 1) * 128,
                                                  dv0:dv0 + 128]
                                          ).then_inc(s_k[slot], 16)

            def b_gpsimd(eng):
                eng.memset(vh[:, 0, :, 128:129], 1.0)
                eng.memset(vh[:, 1, :, 128:129], 1.0)
                eng.drain().then_inc(s_vset, 1)

            def b_tensor(eng):
                gpk = 0
                for t in range(NT + 2):
                    if t < NT:
                        h, i = divmod(t, 16)
                        lo, b0, nb, sbl = geom(i)
                        if i == 0:
                            eng.wait_ge(s_q[h % 2], qrounds[h])
                            eng.wait_ge(s_k[kvslot(h)], kvrounds[h])
                        for b in range(nb):
                            if t >= 1:
                                eng.wait_ge(s_ex, ex_cum[t - 1] + min(b + 1, nb_of[t - 1]))
                            eng.matmul(
                                scp[:, b * 512:(b + 1) * 512],
                                qh[:, h % 2, i * 128:(i + 1) * 128],
                                kh[:, kvslot(h), (b0 + b) * 512:(b0 + b + 1) * 512],
                                start=True, stop=True).then_inc(s_sc, 1)
                    u = t - 1
                    if 0 <= u < NT:
                        h1, i1 = divmod(u, 16)
                        lo1, b01, nb1, sbl1 = geom(i1)
                        npk = (len(sbl1) + 3) // 4
                        eng.wait_ge(s_ex, u + 1)
                        for pk in range(npk):
                            g = packs_cum[u] + pk
                            if g >= 2:
                                eng.wait_ge(s_ptwA, packsA_cum_g[g - 1])
                                eng.wait_ge(s_ptwB, packsB_cum_g[g - 1])
                            blocks = sbl1[pk * 4:pk * 4 + 4]
                            needed_exp = max((sb - b01 * 4) // 4 for sb in blocks)
                            eng.wait_ge(s_ex, ex_cum[u] + needed_exp + 1)
                            for j, sb in enumerate(blocks):
                                lb = sb - b01 * 4
                                ii = self.tpose(
                                    eng, ptp[g % 2][:, j, :],
                                    probs[:, u % 2, lb * 128:(lb + 1) * 128])
                            ii.then_inc(s_ptk, 1)
                            self.tpose_end(eng)
                        eng.wait_ge(s_pteA, packsA_cum[u + 1])
                        eng.wait_ge(s_pteB, packsB_cum[u + 1])
                        if u >= 2:
                            eng.wait_ge(s_oev, u - 1)
                        for j, sb in enumerate(sbl1):
                            ii = eng.matmul(op[u % 2][:, 0:129],
                                            pt_sb[:, u % 2, j, :],
                                            vh[:, kvslot(h1), sb, 0:129],
                                            start=(j == 0),
                                            stop=(j == len(sbl1) - 1))
                        ii.then_inc(s_pv, 1)
                        if i1 == 15:
                            eng.drain().then_inc(s_hd, 1)
                            if win is not None and (h1 % G) == G - 1:
                                eng.nop().then_inc(s_kv, 1)

            def b_vector(eng):
                for t in range(NT + 2):
                    if t < NT:
                        h, i = divmod(t, 16)
                        lo, b0, nb, sbl = geom(i)
                        for b in range(nb):
                            eng.wait_ge(s_sc, sc_cum[t] + b + 1)
                            eng.tensor_reduce(out=sm[:, 32 + b:33 + b],
                                              in_=scp[:, b * 512:(b + 1) * 512],
                                              axis=AX.X, op=ALU.max, negate=True)
                        eng.drain()
                        eng.tensor_reduce(out=sm[:, 26 + t % 2:27 + t % 2],
                                          in_=sm[:, 32:32 + nb], axis=AX.X,
                                          op=ALU.min)
                        eng.drain().then_inc(s_mx, 1)
                    u2 = t - 2
                    if 0 <= u2 < NT:
                        eng.wait_ge(s_pv, u2 + 1)
                        eng.tensor_copy(sm[:, 28 + t % 2:29 + t % 2],
                                        op[u2 % 2][:, 128:129])
                    u = t - 1
                    if 0 <= u < NT:
                        h1, i1 = divmod(u, 16)
                        lo1, b01, nb1, sbl1 = geom(i1)
                        npk = (len(sbl1) + 3) // 4
                        mb1 = masked_blocks(i1)
                        nmine = 0
                        for pk in range(npk):
                            if pack_engine(i1, pk) != "B":
                                continue
                            g = packs_cum[u] + pk
                            eng.wait_ge(s_ptk, g + 1)
                            cntb = len(sbl1[pk * 4:pk * 4 + 4])
                            pkmb = [j for j in range(pk * 4, pk * 4 + cntb)
                                    if j in mb1]
                            if not pkmb:
                                eng.tensor_copy(
                                    pt_sb[:, u % 2, pk * 4:pk * 4 + cntb, :],
                                    ptp[g % 2][:, 0:cntb, :]).then_inc(s_ptwB, 1)
                            else:
                                ii = None
                                for j in range(pk * 4, pk * 4 + cntb):
                                    jj = j - pk * 4
                                    if j in pkmb:
                                        mask = (self.mask_diag
                                                if sbl1[j] == i1 else self.mask_left)
                                        ii = eng.tensor_mul(
                                            pt_sb[:, u % 2, j, :],
                                            ptp[g % 2][:, jj, :], mask)
                                    else:
                                        ii = eng.tensor_copy(
                                            pt_sb[:, u % 2, j, :],
                                            ptp[g % 2][:, jj, :])
                                ii.then_inc(s_ptwB, 1)
                            nmine += 1
                            if nmine == 1:
                                eng.drain().then_inc(s_pteB, 1)
                        if nmine > 1:
                            eng.drain().then_inc(s_pteB, nmine - 1)

                    if 0 <= u2 < NT:
                        h2, i2 = divmod(u2, 16)
                        eng.reciprocal(sm[:, 30 + t % 2:31 + t % 2],
                                       sm[:, 28 + t % 2:29 + t % 2])
                        eng.drain()
                        eng.tensor_scalar(out=self.big[:, h2, i2 * 128:(i2 + 1) * 128],
                                          in0=op[u2 % 2][:, 0:128],
                                          scalar1=sm[:, 30 + t % 2:31 + t % 2],
                                          scalar2=None, op0=ALU.mult)
                        eng.drain().then_inc(s_oev, 1)

            def b_scalar(eng):
                for t in range(NT + 1):
                    if t < NT:
                        h, i = divmod(t, 16)
                        lo, b0, nb, sbl = geom(i)
                        eng.wait_ge(s_mx, t + 1)
                        eh = (nb + 1) // 2
                        eng.activation(probs[:, t % 2, 0:eh * 512],
                                       scp[:, 0:eh * 512], AF.Exp,
                                       bias=sm[:, 26 + t % 2:27 + t % 2])
                        eng.drain().then_inc(s_ex, eh)
                        if nb > eh:
                            eng.activation(probs[:, t % 2, eh * 512:nb * 512],
                                           scp[:, eh * 512:nb * 512], AF.Exp,
                                           bias=sm[:, 26 + t % 2:27 + t % 2])
                            eng.drain().then_inc(s_ex, nb - eh)
                    u = t - 1
                    if 0 <= u < NT:
                        h1, i1 = divmod(u, 16)
                        lo1, b01, nb1, sbl1 = geom(i1)
                        npk = (len(sbl1) + 3) // 4
                        nmine = 0
                        for pk in range(npk):
                            if pack_engine(i1, pk) != "A":
                                continue
                            g = packs_cum[u] + pk
                            eng.wait_ge(s_ptk, g + 1)
                            cntb = len(sbl1[pk * 4:pk * 4 + 4])
                            eng.activation(pt_sb[:, u % 2, pk * 4:pk * 4 + cntb, :],
                                           ptp[g % 2][:, 0:cntb, :], AF.Copy
                                           ).then_inc(s_ptwA, 1)
                            nmine += 1
                            if nmine == 1:
                                eng.drain().then_inc(s_pteA, 1)
                        if nmine > 1:
                            eng.drain().then_inc(s_pteA, nmine - 1)
                if dump_o and not self.fused:
                    o_s = self.dram_tmp("o_s", [D, S], BF16, out=True)
                    eng.wait_ge(s_oev, NT)
                    for ic in range(NCH):
                        eng.dma_start(out=o_s[ic * 128:(ic + 1) * 128, :],
                                      in_=self.big[:, ic, :]).then_inc(s_ost, 16)
                    eng.wait_ge(s_ost, 16 * NCH)

            self.emit_phase(dict(sync=b_sync, gpsimd=b_gpsimd, tensor=b_tensor,
                                 vector=b_vector, scalar=b_scalar))


    def phase_pd(self, L, resid_name, out_name, out_kind, load_o=False):
        nc = self.nc
        ln = L["name"]
        wo = self.dram_in(f"{ln}_wo", [D, D], F32)
        delta_s = self.dram_tmp("delta_s", [S, D], F32, inp=True)
        resid = (self.dram_in(resid_name, [S, D], F32) if resid_name == "xin"
                 else self.dram_tmp(resid_name, [S, D], F32))
        if out_kind == "out":
            yout = self.dram_out(out_name, [S, D], F32)
        else:
            yout = self.dram_tmp(out_name, [S, D], F32, out=True)

        GW = 256
        wbuf = self.carve(self.arenaA, 0, [128, 2, NCH, GW], BF16)
        xd = self.carve(self.arenaA, 32768, [128, 4, 2, GW], F32)
        y_sb = self.carve(self.arenaA, 40960, [128, 2, GW], F32)
        ytmp = self.carve(self.arenaA, 43008, [128, GW], F32)
        stage = self.carve(self.arenaB, 0, [128, 4, GW], F32)

        s_wl = [self.sem("pd_wl") for _ in range(4)]
        s_wc = self.sem("pd_wc")
        s_wcf = self.sem("pd_wcf")
        s_peg = self.sem("pd_peg")
        s_xd = [self.sem("pd_xd") for _ in range(4)]
        s_mm = self.sem("pd_mm")
        s_ye = self.sem("pd_ye")
        s_yst = [self.sem("pd_yst"), self.sem("pd_yst")]
        s_tp = self.sem("pd_tp")
        s_pte = self.sem("pd_pte")
        s_ldo = self.sem("pd_ldo")

        NG = D // GW

        with nc.psum_tensor(self.uname("pd_tpA"), [128, 4, 128], BF16) as tpA, \
             nc.psum_tensor(self.uname("pd_tpB"), [128, 4, 128], BF16) as tpB, \
             nc.psum_tensor(self.uname("pd_mm"), [128, 4, 512], F32) as mm_t:
            tp = [tpA.ap(), tpB.ap()]
            mmp = mm_t.ap()

            def b_sync(eng):
                if load_o and not self.fused:
                    srcb = self.dram_tmp("o_s", [D, S], BF16, inp=True)
                    for ic in range(NCH):
                        eng.dma_start(out=self.big[:, ic, :],
                                      in_=srcb[ic * 128:(ic + 1) * 128, :]
                                      ).then_inc(s_ldo, 16)
                for og in range(NG):
                    for ic in range(NCH):
                        j = og * NCH + ic
                        if j >= 4:
                            eng.wait_ge(s_wcf, j - 3)
                        eng.dma_start(out=stage[:, j % 4, :],
                                      in_=wo[ic * 128:(ic + 1) * 128,
                                             og * GW:(og + 1) * GW]
                                      ).then_inc(s_wl[j % 4], 16)
                    for tt in range(NTT):
                        mi = og * NTT + tt
                        if mi >= 4:
                            eng.wait_ge(s_ye, mi - 3)
                        eng.dma_start(out=xd[:, mi % 4, 0, :],
                                      in_=resid[tt * 128:(tt + 1) * 128,
                                                og * GW:(og + 1) * GW]
                                      ).then_inc(s_xd[mi % 4], 16)
                        eng.dma_start(out=xd[:, mi % 4, 1, :],
                                      in_=delta_s[tt * 128:(tt + 1) * 128,
                                                  og * GW:(og + 1) * GW]
                                      ).then_inc(s_xd[mi % 4], 16)

            def b_gpsimd(eng):
                for j in range(NG * NCH):
                    gi, ic = divmod(j, NCH)
                    if gi >= 2 and ic == 0:
                        eng.wait_ge(s_peg, gi - 1)
                    eng.wait_ge(s_wl[j % 4], 16 * (j // 4 + 1))
                    eng.tensor_copy(wbuf[:, gi % 2, ic, :],
                                    stage[:, j % 4, :]).then_inc(s_wcf, 1)
                    if ic == NCH - 1:
                        eng.drain().then_inc(s_wc, 1)

            def b_tensor(eng):
                if load_o and not self.fused:
                    eng.wait_ge(s_ldo, 16 * NCH)
                for tt in range(NTT):
                    for p in range(8):
                        g = tt * 8 + p
                        if g >= 2:
                            eng.wait_ge(s_pte, g - 1)
                        for j in range(4):
                            ic = p * 4 + j
                            ii = self.tpose(eng, tp[p % 2][:, j, :],
                                            self.big[:, ic, tt * 128:(tt + 1) * 128])
                        ii.then_inc(s_tp, 1)
                        self.tpose_end(eng)
                for og in range(NG):
                    eng.wait_ge(s_wc, og + 1)
                    eng.wait_ge(s_pte, NTT * 8)
                    for tt in range(NTT):
                        mi = og * NTT + tt
                        if mi >= 4:
                            eng.wait_ge(s_ye, mi - 3)
                        p = mmp[:, mi % 4, 0:GW]
                        for ic in range(NCH):
                            ii = eng.matmul(p,
                                            self.big[:, ic, tt * 128:(tt + 1) * 128],
                                            wbuf[:, og % 2, ic, :],
                                            start=(ic == 0), stop=(ic == NCH - 1))
                        ii.then_inc(s_mm, 1)
                    eng.drain().then_inc(s_peg, 1)

            def b_scalar(eng):
                for tt in range(NTT):
                    for p in range(8):
                        g = tt * 8 + p
                        eng.wait_ge(s_tp, g + 1)
                        eng.activation(
                            self.big[:, p * 4:p * 4 + 4, tt * 128:(tt + 1) * 128],
                            tp[p % 2][:, :, :], AF.Copy)
                        eng.drain().then_inc(s_pte, 1)
                nst = [0, 0]
                for mi in range(NG * NTT):
                    og, tt = divmod(mi, NTT)
                    eng.wait_ge(s_ye, mi + 1)
                    eng.dma_start(out=yout[tt * 128:(tt + 1) * 128,
                                           og * GW:(og + 1) * GW],
                                  in_=y_sb[:, mi % 2, :]).then_inc(s_yst[mi % 2], 16)
                    nst[mi % 2] += 1
                eng.wait_ge(s_yst[0], 16 * nst[0])
                eng.wait_ge(s_yst[1], 16 * nst[1])

            def b_vector(eng):
                for mi in range(NG * NTT):
                    eng.wait_ge(s_mm, mi + 1)
                    eng.wait_ge(s_xd[mi % 4], 32 * (mi // 4 + 1))
                    if mi >= 2:
                        eng.wait_ge(s_yst[mi % 2], 16 * (mi // 2))
                    eng.scalar_tensor_tensor(out=ytmp[:, :], in0=mmp[:, mi % 4, 0:GW],
                                             scalar=1.0, in1=xd[:, mi % 4, 0, :],
                                             op0=ALU.mult, op1=ALU.add)
                    eng.drain()
                    eng.tensor_add(y_sb[:, mi % 2, :], ytmp[:, :],
                                   xd[:, mi % 4, 1, :])
                    eng.drain().then_inc(s_ye, 1)

            self.emit_phase(dict(sync=b_sync, gpsimd=b_gpsimd, tensor=b_tensor,
                                 scalar=b_scalar, vector=b_vector))



def build_prog(phases, fused):
    B = Builder(fused)
    B.alloc_fixed()
    with B.es:
        B.phase_consts()
        for (pname, L, kw) in phases:
            getattr(B, "phase_" + pname)(**kw) if L is None else \
                getattr(B, "phase_" + pname)(L, **kw)
    return B


_prog_cache = {}


def _layer_prog(layer):
    key = layer["name"]
    if key in _prog_cache:
        return _prog_cache[key]
    B = Builder(fused=True)
    B.alloc_fixed()
    with B.es:
        B.phase_consts()
        src = B.dram_in("xin", [S, D], F32)
        B.phase_pa(src)
        B.phase_pb(layer)
        if layer["ffn"] == "moe":
            B.phase_pf_moe(layer)
        else:
            B.phase_pf_bin(layer)
        B.phase_pc(layer)
        B.phase_pd(layer, "xin", "yout", "out")
    _prog_cache[key] = B.nc
    return B.nc


def _full_prog():
    if "full" in _prog_cache:
        return _prog_cache["full"]
    B = Builder(fused=True)
    B.alloc_fixed()
    with B.es:
        B.phase_consts()
        src = B.dram_in("xin", [S, D], F32)
        B.phase_pa(src)
        B.phase_pb(L0)
        B.phase_pf_moe(L0)
        B.phase_pc(L0)
        B.phase_pd(L0, "xin", "resid_s", "tmpout")
        B.phase_pa(B.dram["resid_s"])
        B.phase_pb(L1)
        B.phase_pf_bin(L1)
        B.phase_pc(L1)
        B.phase_pd(L1, "resid_s", "yout", "out")
    _prog_cache["full"] = B.nc
    return B.nc


def _weight_maps(inputs):
    consts = host_consts()
    perm = perm_interleave()
    f32c = lambda a: np.ascontiguousarray(np.asarray(a), dtype=np.float32)
    w0 = {
        **consts,
        "l0_wq": f32c(inputs["L0_attn_wq"]),
        "l0_wk": f32c(inputs["L0_attn_wk"]),
        "l0_wv": f32c(inputs["L0_attn_wv"]),
        "l0_wo": f32c(inputs["L0_attn_wo"]),
        "l0_A": f32c(inputs["L0_ffn_A"]),
        "l0_B": f32c(inputs["L0_ffn_B"]),
        "l0_Wr": f32c(inputs["L0_ffn_Wr"]),
    }
    w1 = {
        **consts,
        "l1_wq": f32c(np.asarray(inputs["L1_attn_wq"])[:, perm]),
        "l1_wk": f32c(np.asarray(inputs["L1_attn_wk"])[:, perm]),
        "l1_wv": f32c(inputs["L1_attn_wv"]),
        "l1_wo": f32c(inputs["L1_attn_wo"]),
        "l1_wreal": f32c(inputs["L1_ffn_weight_real"]),
        "l1_bias": f32c(np.asarray(inputs["L1_ffn_bias"]).reshape(1, D)),
    }
    return w0, w1


def kernel(**inputs):
    x = np.ascontiguousarray(np.asarray(inputs["L0_attn_x"], dtype=np.float32))
    w0, w1 = _weight_maps(inputs)
    w = dict(w0)
    w.update({k: v for k, v in w1.items()})
    p = _full_prog()
    im = [dict(w, xin=np.ascontiguousarray(x[c])) for c in range(8)]
    r = run_bass_kernel_spmd(p, im, core_ids=list(range(8)))
    return np.stack([r.results[c]["yout"] for c in range(8)], axis=0
                    ).astype(np.float32)


# version 08 (final)
# speedup vs baseline: 1.2563x; 1.0809x over previous
import contextlib
import math
import numpy as np
import ml_dtypes

import concourse.bass as bass
import concourse.mybir as mybir
from concourse.bass_utils import run_bass_kernel_spmd

F32 = mybir.dt.float32
BF16 = mybir.dt.bfloat16
U8 = mybir.dt.uint8
AF = mybir.ActivationFunctionType
ALU = mybir.AluOpType
AX = mybir.AxisListType

S = 2048
D = 4096
NCH = 32
NTT = 16
NTB = 4
HD = 128
SM = 1.0 / math.sqrt(HD)
EPS = 1e-5
ENGINES = ("sync", "scalar", "vector", "tensor", "gpsimd")

L0 = dict(name="l0", n_q=32, n_kv=8, window=1024, dk=1024, dv=1024, ffn="moe")
L1 = dict(name="l1", n_q=32, n_kv=32, window=None, dk=4096, dv=4096, ffn="bin")



def host_consts():
    half = 64
    inv = 10000.0 ** (-np.arange(half, dtype=np.float64) / half)
    ang = np.arange(S, dtype=np.float64)[None, :] * inv[:, None]
    cos = np.cos(ang).astype(np.float32)
    sin = np.sin(ang).astype(np.float32)
    cc = np.concatenate([cos, cos], axis=0).astype(ml_dtypes.bfloat16)
    ss = np.concatenate([-sin, sin], axis=0).astype(ml_dtypes.bfloat16)
    ccq = (np.concatenate([cos, cos], axis=0) * SM).astype(ml_dtypes.bfloat16)
    ssq = (np.concatenate([-sin, sin], axis=0) * SM).astype(ml_dtypes.bfloat16)

    ident = np.eye(128, dtype=ml_dtypes.bfloat16)
    r = np.arange(128)
    mask_diag = (r[:, None] <= r[None, :]).astype(ml_dtypes.bfloat16)
    mask_left = (r[:, None] > r[None, :]).astype(ml_dtypes.bfloat16)
    swap64 = np.zeros((128, 128), dtype=ml_dtypes.bfloat16)
    swap64[r, (r + 64) % 128] = 1.0
    consts4 = np.ascontiguousarray(
        np.stack([ident, mask_diag, mask_left, swap64], axis=1))
    return dict(cc=cc, ss=ss, ccq=ccq, ssq=ssq, consts4=consts4)


def perm_interleave():
    p = np.zeros(D, dtype=np.int64)
    for h in range(32):
        base = 128 * h
        for j in range(64):
            p[base + j] = base + 2 * j
            p[base + 64 + j] = base + 2 * j + 1
    return p



class Builder:
    def __init__(self, fused):
        self.nc = bass.Bass("TRN2", target_bir_lowering=False)
        self.es = contextlib.ExitStack()
        self.fused = fused
        self.phase_idx = 0
        self.dram = {}
        self._sem_n = 0
        self._sem_pool = []
        self._phase_sems = []
        self._nm = 0

    def sem(self, name):
        if self.fused and self._sem_pool:
            h = self._sem_pool.pop()
        else:
            self._sem_n += 1
            h = self.es.enter_context(self.nc.semaphore(f"{name}_{self._sem_n}"))
        self._phase_sems.append(h)
        return h

    def dram_in(self, name, shape, dtype):
        if name not in self.dram:
            self.dram[name] = self.nc.dram_tensor(name, shape, dtype,
                                                  kind="ExternalInput").ap()
        return self.dram[name]

    def dram_tmp(self, name, shape, dtype, out=False, inp=False):
        if name in self.dram:
            return self.dram[name]
        if self.fused:
            kind = "Internal"
        else:
            kind = "ExternalOutput" if out else ("ExternalInput" if inp else "Internal")
        self.dram[name] = self.nc.dram_tensor(name, shape, dtype, kind=kind).ap()
        return self.dram[name]

    def dram_out(self, name, shape, dtype):
        if name not in self.dram:
            self.dram[name] = self.nc.dram_tensor(name, shape, dtype,
                                                  kind="ExternalOutput").ap()
        return self.dram[name]

    def alloc_fixed(self):
        nc = self.nc
        self.big_t = nc.alloc_sbuf_tensor("big", [128, NCH, S], BF16)
        self.big = self.big_t.ap()
        self.arenaA = nc.alloc_sbuf_tensor("arenaA", [128, 49152], U8)
        self.arenaB = nc.alloc_sbuf_tensor("arenaB", [128, 14336], U8)
        self.cc = nc.alloc_sbuf_tensor("cc_sb", [128, S], BF16).ap()
        self.ss = nc.alloc_sbuf_tensor("ss_sb", [128, S], BF16).ap()
        self.ccq = nc.alloc_sbuf_tensor("ccq_sb", [128, S], BF16).ap()
        self.ssq = nc.alloc_sbuf_tensor("ssq_sb", [128, S], BF16).ap()
        self.consts4 = nc.alloc_sbuf_tensor("consts4_sb", [128, 4, 128], BF16).ap()
        self.smalls = nc.alloc_sbuf_tensor("smalls", [128, 64], F32).ap()
        self.onesrow = nc.alloc_sbuf_tensor("onesrow", [1, 128], BF16).ap()
        self.ident = self.consts4[:, 0, :]
        self.mask_diag = self.consts4[:, 1, :]
        self.mask_left = self.consts4[:, 2, :]
        self.swap64 = self.consts4[:, 3, :]
        self.eps = self.smalls[:, 0:1]
        self.bar = self.es.enter_context(nc.semaphore("bar"))
        self.block = self.es.enter_context(nc.Block())

    def uname(self, base):
        self._nm += 1
        return f"{base}_{self._nm}"

    def carve(self, arena, off_bytes, shape, dtype):
        bs = mybir.dt.size(dtype)
        assert off_bytes % bs == 0
        size = int(np.prod(shape[1:]))
        start = off_bytes // bs
        t = arena.bitcast(dtype)
        ap = t.ap()[0:int(shape[0]), start:start + size]
        if len(shape) > 2:
            names = " ".join(f"d{i}" for i in range(1, len(shape)))
            kw = {f"d{i}": int(shape[i]) for i in range(1, len(shape))}
            ap = ap.rearrange(f"p ({names}) -> p {names}", **kw)
        return ap

    def tpose(self, eng, out, in_, ident=None):
        eng.wait_ge(self.bar, 0)
        return eng.transpose(out, in_, self.ident if ident is None else ident)

    def tpose_end(self, eng):
        eng.wait_ge(self.bar, 0)

    def emit_phase(self, bodies):
        k = self.phase_idx
        bar = self.bar
        phase_sems = list(self._phase_sems)
        for ename in ENGINES:
            fn = bodies.get(ename)

            def wrapper(eng, fn=fn, k=k, ename=ename):
                if k > 0:
                    eng.wait_ge(bar, 6 * k)
                if fn is not None:
                    fn(eng)
                if ename == "sync":
                    eng.sem_inc(bar, 1)
                else:
                    eng.drain().then_inc(bar, 1)
                if ename == "vector":
                    eng.wait_ge(bar, 6 * k + 5)
                    for s in phase_sems:
                        eng.sem_clear(s)
                    eng.sem_inc(bar, 1)

            getattr(self.block, ename)(wrapper)
        self.phase_idx += 1
        self._sem_pool.extend(self._phase_sems)
        self._phase_sems = []

    def phase_consts(self):
        cc_in = self.dram_in("cc", [128, S], BF16)
        ss_in = self.dram_in("ss", [128, S], BF16)
        ccq_in = self.dram_in("ccq", [128, S], BF16)
        ssq_in = self.dram_in("ssq", [128, S], BF16)
        c4_in = self.dram_in("consts4", [128, 4, 128], BF16)
        s_cl = self.sem("cl")

        def b_sync(eng):
            eng.dma_start(out=self.cc, in_=cc_in).then_inc(s_cl, 16)
            eng.dma_start(out=self.ss, in_=ss_in).then_inc(s_cl, 16)
            eng.dma_start(out=self.ccq, in_=ccq_in).then_inc(s_cl, 16)
            eng.dma_start(out=self.ssq, in_=ssq_in).then_inc(s_cl, 16)
            eng.dma_start(out=self.consts4, in_=c4_in).then_inc(s_cl, 16)
            eng.wait_ge(s_cl, 80)

        def b_vector(eng):
            eng.memset(self.eps, EPS)
            eng.memset(self.onesrow, 1.0)

        self.emit_phase(dict(sync=b_sync, vector=b_vector))

    def phase_pa(self, src, dump_xnt=False):
        nc = self.nc
        xtile = self.carve(self.arenaA, 0, [128, 2, D], F32)
        xn = self.carve(self.arenaA, 32768, [128, D], BF16)
        sqtrash = self.carve(self.arenaA, 40960, [128, D], BF16)
        stats = self.carve(self.arenaB, 0, [128, 2, 8], F32)

        s_ld = [self.sem("pa_ld"), self.sem("pa_ld")]
        s_sq = self.sem("pa_sq")
        s_var = self.sem("pa_var")
        s_rs = self.sem("pa_rs")
        s_nm = self.sem("pa_nm")
        s_tp = self.sem("pa_tp")
        s_ev = self.sem("pa_ev")
        s_st = self.sem("pa_st")

        with nc.psum_tensor(self.uname("pa_tpA"), [128, 4, 128], BF16) as tpA, \
             nc.psum_tensor(self.uname("pa_tpB"), [128, 4, 128], BF16) as tpB:
            tp = [tpA.ap(), tpB.ap()]

            def b_sync(eng):
                for tt in range(NTT):
                    if tt >= 2:
                        eng.wait_ge(s_nm, tt - 1)
                    eng.dma_start(out=xtile[:, tt % 2, :],
                                  in_=src[tt * 128:(tt + 1) * 128, :]
                                  ).then_inc(s_ld[tt % 2], 16)

            def b_scalar(eng):
                for tt in range(NTT):
                    eng.wait_ge(s_ld[tt % 2], 16 * (tt // 2 + 1))
                    eng.activation(sqtrash[:, :], xtile[:, tt % 2, :], AF.Square,
                                   accum_out=stats[:, tt % 2, 1:2])
                    eng.drain().then_inc(s_sq, 1)
                    eng.wait_ge(s_var, tt + 1)
                    eng.activation(stats[:, tt % 2, 4:5], stats[:, tt % 2, 3:4],
                                   AF.Ln, bias=self.eps)
                    eng.activation(stats[:, tt % 2, 4:5], stats[:, tt % 2, 4:5],
                                   AF.Exp, scale=-0.5)
                    eng.drain().then_inc(s_rs, 1)
                if dump_xnt:
                    dst = self.dram_tmp("xnt_s", [D, S], BF16, out=True)
                    eng.wait_ge(s_ev, NTT * 8)
                    for ic in range(NCH):
                        eng.dma_start(out=dst[ic * 128:(ic + 1) * 128, :],
                                      in_=self.big[:, ic, :]).then_inc(s_st, 16)
                    eng.wait_ge(s_st, 16 * NCH)

            def b_vector(eng):
                for tt in range(NTT):
                    eng.wait_ge(s_ld[tt % 2], 16 * (tt // 2 + 1))
                    eng.tensor_reduce(out=stats[:, tt % 2, 0:1],
                                      in_=xtile[:, tt % 2, :], axis=AX.X, op=ALU.add)
                    eng.wait_ge(s_sq, tt + 1)
                    eng.drain()
                    eng.tensor_scalar(out=stats[:, tt % 2, 2:3],
                                      in0=stats[:, tt % 2, 0:1],
                                      scalar1=1.0 / D, scalar2=None, op0=ALU.mult)
                    eng.tensor_scalar(out=stats[:, tt % 2, 3:4],
                                      in0=stats[:, tt % 2, 1:2],
                                      scalar1=1.0 / D, scalar2=None, op0=ALU.mult)
                    eng.drain()
                    eng.scalar_tensor_tensor(out=stats[:, tt % 2, 5:6],
                                             in0=stats[:, tt % 2, 2:3], scalar=-1.0,
                                             in1=stats[:, tt % 2, 2:3],
                                             op0=ALU.mult, op1=ALU.mult)
                    eng.drain()
                    eng.tensor_add(stats[:, tt % 2, 3:4], stats[:, tt % 2, 3:4],
                                   stats[:, tt % 2, 5:6])
                    eng.drain().then_inc(s_var, 1)
                    eng.wait_ge(s_rs, tt + 1)
                    if tt >= 1:
                        eng.wait_ge(s_tp, 8 * tt)
                    eng.tensor_scalar(out=xn[:, :], in0=xtile[:, tt % 2, :],
                                      scalar1=stats[:, tt % 2, 2:3],
                                      scalar2=stats[:, tt % 2, 4:5],
                                      op0=ALU.subtract, op1=ALU.mult)
                    eng.drain().then_inc(s_nm, 1)
                    for p in range(8):
                        gidx = tt * 8 + p
                        eng.wait_ge(s_tp, gidx + 1)
                        eng.tensor_copy(
                            self.big[:, p * 4:p * 4 + 4, tt * 128:(tt + 1) * 128],
                            tp[p % 2][:, :, :])
                        eng.drain().then_inc(s_ev, 1)

            def b_tensor(eng):
                for tt in range(NTT):
                    eng.wait_ge(s_nm, tt + 1)
                    for p in range(8):
                        gidx = tt * 8 + p
                        if gidx >= 2:
                            eng.wait_ge(s_ev, gidx - 1)
                        for j in range(4):
                            ic = p * 4 + j
                            ii = self.tpose(eng, tp[p % 2][:, j, :],
                                            xn[:, ic * 128:(ic + 1) * 128])
                        ii.then_inc(s_tp, 1)
                        self.tpose_end(eng)

            self.emit_phase(dict(sync=b_sync, scalar=b_scalar, vector=b_vector,
                                 tensor=b_tensor))

    def phase_pb(self, L, load_xnt=False, only=None):
        nc = self.nc
        ln = L["name"]
        wq = self.dram_in(f"{ln}_wq", [D, D], F32)
        wk = self.dram_in(f"{ln}_wk", [D, L["dk"]], F32)
        wv = self.dram_in(f"{ln}_wv", [D, L["dv"]], F32)
        qt_s = self.dram_tmp("qt_s", [D, S], BF16, out=True)
        kt_s = self.dram_tmp("kt_s", [D, S], BF16, out=True)
        v_s = self.dram_tmp("v_s", [S, D], BF16, out=True)

        GW = 256
        wbuf = self.carve(self.arenaA, 0, [128, 2, NCH, GW], BF16)
        qk_sb = self.carve(self.arenaA, 32768, [128, 2, S], BF16)
        rope_t = self.carve(self.arenaA, 40960, [128, 2, 512], BF16)
        rope_sw = self.carve(self.arenaA, 43008, [128, 2, 512], BF16)
        rope_m = self.carve(self.arenaA, 45056, [128, 2, 512], BF16)
        v_sb = self.carve(self.arenaA, 47104, [128, 2, GW], BF16)
        stage = self.carve(self.arenaB, 0, [128, 4, GW], F32)

        s_wl = [self.sem("pb_wl") for _ in range(4)]
        s_wc = self.sem("pb_wc")
        s_wcf = self.sem("pb_wcf")
        s_peg = self.sem("pb_peg")
        s_pet = self.sem("pb_pet")
        s_pesw = self.sem("pb_pesw")
        s_ev1 = self.sem("pb_ev1")
        s_ev2 = self.sem("pb_ev2")
        s_rp = self.sem("pb_rp")
        s_rst = [self.sem("pb_rst"), self.sem("pb_rst")]
        s_vst = [self.sem("pb_vst"), self.sem("pb_vst")]
        s_ldb = self.sem("pb_ldb")

        projs = [(wq, D, "fm", qt_s), (wk, L["dk"], "fm", kt_s),
                 (wv, L["dv"], "tm", v_s)]
        if only == "v":
            projs = projs[2:]
        elif only == "q":
            projs = projs[:1]
        elif only == "qk":
            projs = projs[:2]
        NSUB = GW // 128

        groups = []
        chunks = []
        for pi, (w, dout, kind, dst) in enumerate(projs):
            for g in range(dout // GW):
                groups.append((pi, g))
                for ic in range(NCH):
                    chunks.append((len(groups) - 1, pi, g, ic))

        tiles = []
        for gi, (pi, g) in enumerate(groups):
            kind = projs[pi][2]
            if kind == "fm":
                for sub in range(NSUB):
                    for tb in range(NTB):
                        tiles.append(dict(gi=gi, kind="fm", pi=pi, g=g,
                                          sub=sub, tb=tb))
            else:
                for tt in range(NTT):
                    tiles.append(dict(gi=gi, kind="tm", pi=pi, g=g, tt=tt))

        fm_tiles = [ti for ti, t in enumerate(tiles) if t["kind"] == "fm"]
        fmidx = {ti: fi for fi, ti in enumerate(fm_tiles)}
        nfm = len(fm_tiles)
        rowidx = {}
        nrow = -1
        row_last_fi = {}
        for ti in fm_tiles:
            t = tiles[ti]
            if t["tb"] == 0:
                nrow += 1
            rowidx[ti] = nrow
            if t["tb"] == NTB - 1:
                row_last_fi[nrow] = fmidx[ti]
        row_info = {}
        for ti in fm_tiles:
            t = tiles[ti]
            if t["tb"] == NTB - 1:
                r = rowidx[ti]
                row_info[r] = (t["pi"], t["g"] * NSUB + t["sub"])
        vrow_of = {}
        nv = -1
        for ti, t in enumerate(tiles):
            if t["kind"] == "tm":
                nv += 1
                vrow_of[ti] = nv

        def b_sync(eng):
            if load_xnt and not self.fused:
                srcb = self.dram_tmp("xnt_s", [D, S], BF16, inp=True)
                for ic in range(NCH):
                    eng.dma_start(out=self.big[:, ic, :],
                                  in_=srcb[ic * 128:(ic + 1) * 128, :]
                                  ).then_inc(s_ldb, 16)
            for j, (gi, pi, g, ic) in enumerate(chunks):
                w = projs[pi][0]
                if j >= 4:
                    eng.wait_ge(s_wcf, j - 3)
                eng.dma_start(out=stage[:, j % 4, :],
                              in_=w[ic * 128:(ic + 1) * 128, g * GW:(g + 1) * GW]
                              ).then_inc(s_wl[j % 4], 16)

        def b_gpsimd(eng):
            for j, (gi, pi, g, ic) in enumerate(chunks):
                if gi >= 2 and ic == 0:
                    eng.wait_ge(s_peg, gi - 1)
                eng.wait_ge(s_wl[j % 4], 16 * (j // 4 + 1))
                eng.tensor_copy(wbuf[:, gi % 2, ic, :],
                                stage[:, j % 4, :]).then_inc(s_wcf, 1)
                if ic == NCH - 1:
                    eng.drain().then_inc(s_wc, 1)

        with nc.psum_tensor(self.uname("pb_ps"), [128, 4, 512], F32) as ps_t, \
             nc.psum_tensor(self.uname("pb_psw"), [128, 2, 512], F32) as psw_t:
            psum = ps_t.ap()
            psw = psw_t.ap()

            def pe_swap(eng, fi):
                eng.wait_ge(s_ev1, ev1_count_of_fm[fi])
                if fi >= 2:
                    eng.wait_ge(s_ev2, fi - 1)
                eng.matmul(psw[:, fi % 2, :], self.swap64, rope_t[:, fi % 2, :],
                           start=True, stop=True).then_inc(s_pesw, 1)

            def b_tensor(eng):
                if load_xnt and not self.fused:
                    eng.wait_ge(s_ldb, 16 * NCH)
                for ti, t in enumerate(tiles):
                    if t["kind"] == "tm" and ti > 0 and tiles[ti - 1]["kind"] == "fm":
                        pe_swap(eng, nfm - 1)
                    eng.wait_ge(s_wc, t["gi"] + 1)
                    if ti >= 4:
                        eng.wait_ge(s_ev1, ti - 3)
                    if t["kind"] == "fm":
                        ps = psum[:, ti % 4, :]
                        for ic in range(NCH):
                            ii = eng.matmul(
                                ps,
                                wbuf[:, t["gi"] % 2, ic,
                                     t["sub"] * 128:(t["sub"] + 1) * 128],
                                self.big[:, ic, t["tb"] * 512:(t["tb"] + 1) * 512],
                                start=(ic == 0), stop=(ic == NCH - 1))
                    else:
                        ps = psum[:, ti % 4, 0:GW]
                        tt = t["tt"]
                        for ic in range(NCH):
                            ii = eng.matmul(
                                ps,
                                self.big[:, ic, tt * 128:(tt + 1) * 128],
                                wbuf[:, t["gi"] % 2, ic, :],
                                start=(ic == 0), stop=(ic == NCH - 1))
                    ii.then_inc(s_pet, 1)
                    if t["kind"] == "fm" and fmidx[ti] >= 1:
                        pe_swap(eng, fmidx[ti] - 1)
                    if ti + 1 == len(tiles) or tiles[ti + 1]["gi"] != t["gi"]:
                        eng.drain().then_inc(s_peg, 1)
                if tiles[-1]["kind"] == "fm":
                    pe_swap(eng, nfm - 1)

            ev1_count_of_fm = {}
            for ti in fm_tiles:
                ev1_count_of_fm[fmidx[ti]] = ti + 1

            def act_evac2(eng, fi):
                eng.wait_ge(s_pesw, fi + 1)
                if fi >= 2:
                    eng.wait_ge(s_rp, fi - 1)
                eng.activation(rope_sw[:, fi % 2, :], psw[:, fi % 2, :], AF.Copy)
                eng.drain().then_inc(s_ev2, 1)

            def b_scalar(eng):
                nrow_st = [0, 0]
                nv_st = [0, 0]
                pending_rows = []
                for ti, t in enumerate(tiles):
                    if t["kind"] == "tm" and ti > 0 and tiles[ti - 1]["kind"] == "fm":
                        act_evac2(eng, nfm - 1)
                        flush_rows(eng, nfm, nrow_st, pending_rows)
                    eng.wait_ge(s_pet, ti + 1)
                    ps = psum[:, ti % 4, :] if t["kind"] == "fm" else psum[:, ti % 4, 0:GW]
                    if t["kind"] == "fm":
                        fi = fmidx[ti]
                        if fi >= 2:
                            eng.wait_ge(s_rp, fi - 1)
                        eng.activation(rope_t[:, fi % 2, :], ps, AF.Copy)
                        eng.drain().then_inc(s_ev1, 1)
                        if fi >= 1:
                            act_evac2(eng, fi - 1)
                        if t["tb"] == NTB - 1:
                            pending_rows.append(rowidx[ti])
                        flush_rows(eng, fi, nrow_st, pending_rows)
                    else:
                        vr = vrow_of[ti]
                        if vr >= 2:
                            eng.wait_ge(s_vst[vr % 2], 16 * (vr // 2))
                        eng.activation(v_sb[:, vr % 2, :], ps, AF.Copy)
                        eng.drain().then_inc(s_ev1, 1)
                        tt = t["tt"]
                        dst = projs[t["pi"]][3]
                        eng.dma_start(
                            out=dst[tt * 128:(tt + 1) * 128,
                                    t["g"] * GW:(t["g"] + 1) * GW],
                            in_=v_sb[:, vr % 2, :]).then_inc(s_vst[vr % 2], 16)
                        nv_st[vr % 2] += 1
                if tiles[-1]["kind"] == "fm":
                    act_evac2(eng, nfm - 1)
                    flush_rows(eng, nfm, nrow_st, pending_rows)
                eng.wait_ge(s_rst[0], 16 * nrow_st[0])
                eng.wait_ge(s_rst[1], 16 * nrow_st[1])
                eng.wait_ge(s_vst[0], 16 * nv_st[0])
                eng.wait_ge(s_vst[1], 16 * nv_st[1])

            def flush_rows(eng, ropes_emitted, nrow_st, pending_rows):
                while pending_rows and row_last_fi[pending_rows[0]] + 1 <= ropes_emitted:
                    r = pending_rows.pop(0)
                    eng.wait_ge(s_rp, row_last_fi[r] + 1)
                    pi, dchunk = row_info[r]
                    dst = projs[pi][3]
                    eng.dma_start(out=dst[dchunk * 128:(dchunk + 1) * 128, :],
                                  in_=qk_sb[:, r % 2, :]).then_inc(s_rst[r % 2], 16)
                    nrow_st[r % 2] += 1

            def dve_rope(eng, fi):
                ti = fm_tiles[fi]
                t = tiles[ti]
                r = rowidx[ti]
                tb = t["tb"]
                eng.wait_ge(s_ev2, fi + 1)
                if tb == 0 and r >= 2:
                    eng.wait_ge(s_rst[r % 2], 16 * (r // 2))
                if t["pi"] == 0:
                    cslice = self.ccq[:, tb * 512:(tb + 1) * 512]
                    sslice = self.ssq[:, tb * 512:(tb + 1) * 512]
                else:
                    cslice = self.cc[:, tb * 512:(tb + 1) * 512]
                    sslice = self.ss[:, tb * 512:(tb + 1) * 512]
                eng.tensor_mul(rope_m[:, 0, :], rope_t[:, fi % 2, :], cslice)
                eng.tensor_mul(rope_m[:, 1, :], rope_sw[:, fi % 2, :], sslice)
                eng.drain()
                eng.tensor_add(qk_sb[:, r % 2, tb * 512:(tb + 1) * 512],
                               rope_m[:, 0, :], rope_m[:, 1, :])
                eng.drain().then_inc(s_rp, 1)

            def b_vector(eng):
                for fi in range(1, nfm):
                    dve_rope(eng, fi - 1)
                if nfm:
                    dve_rope(eng, nfm - 1)

            self.emit_phase(dict(sync=b_sync, gpsimd=b_gpsimd, tensor=b_tensor,
                                 scalar=b_scalar, vector=b_vector))

    def phase_pf_bin(self, L, load_xnt=False):
        nc = self.nc
        ln = L["name"]
        wr = self.dram_in(f"{ln}_wreal", [D, D], F32)
        bias = self.dram_in(f"{ln}_bias", [1, D], F32)
        delta_s = self.dram_tmp("delta_s", [S, D], F32, out=True)

        wtb = self.carve(self.arenaA, 0, [128, 2, NCH, 256], BF16)
        sgn_tmp = self.carve(self.arenaA, 32768, [128, D], BF16)
        biasb = self.carve(self.arenaA, 40960, [1, D], BF16)
        stage = self.carve(self.arenaB, 0, [128, 2, 1024], F32)
        dsb = self.carve(self.arenaB, 8192, [128, 2, 256], F32)
        al = self.smalls

        s_wld = [self.sem("pf_wld"), self.sem("pf_wld")]
        s_bld = self.sem("pf_bld")
        s_alq = self.sem("pf_alq")
        s_sc = self.sem("pf_sc")
        s_sg = self.sem("pf_sg")
        s_tpk = self.sem("pf_tpk")
        s_te = self.sem("pf_te")
        s_mm = self.sem("pf_mm")
        s_de = self.sem("pf_de")
        s_dst = [self.sem("pf_dst"), self.sem("pf_dst")]
        s_ldb = self.sem("pf_ldb")

        NOT_ = NCH

        with nc.psum_tensor(self.uname("pf_tpA"), [128, 4, 128], BF16) as tpA, \
             nc.psum_tensor(self.uname("pf_tpB"), [128, 4, 128], BF16) as tpB, \
             nc.psum_tensor(self.uname("pf_mmp"), [128, 2, 512], F32) as mm_t:
            tp = [tpA.ap(), tpB.ap()]
            mmp = mm_t.ap()

            def b_sync(eng):
                if load_xnt and not self.fused:
                    srcb = self.dram_tmp("xnt_s", [D, S], BF16, inp=True)
                    for ic in range(NCH):
                        eng.dma_start(out=self.big[:, ic, :],
                                      in_=srcb[ic * 128:(ic + 1) * 128, :]
                                      ).then_inc(s_ldb, 16)
                for j in range(NOT_ * 4):
                    ot, q = divmod(j, 4)
                    if j >= 2:
                        eng.wait_ge(s_sg, j - 1)
                        eng.wait_ge(s_alq, j - 1)
                    eng.dma_start(out=stage[:, j % 2, :],
                                  in_=wr[ot * 128:(ot + 1) * 128,
                                         q * 1024:(q + 1) * 1024]
                                  ).then_inc(s_wld[j % 2], 16)

            def b_gpsimd(eng):
                eng.dma_start(out=biasb, in_=bias).then_inc(s_bld, 16)
                nst = [0, 0]
                for mi in range(16 * NTT):
                    band, tt = divmod(mi, NTT)
                    eng.wait_ge(s_de, mi + 1)
                    eng.dma_start(out=delta_s[tt * 128:(tt + 1) * 128,
                                              band * 256:(band + 1) * 256],
                                  in_=dsb[:, mi % 2, :]).then_inc(s_dst[mi % 2], 16)
                    nst[mi % 2] += 1
                eng.wait_ge(s_dst[0], 16 * nst[0])
                eng.wait_ge(s_dst[1], 16 * nst[1])

            def b_vector(eng):
                for ot in range(NOT_):
                    for q in range(4):
                        j = ot * 4 + q
                        eng.wait_ge(s_wld[j % 2], 16 * (j // 2 + 1))
                        eng.tensor_reduce(out=al[:, 8 + q:9 + q],
                                          in_=stage[:, j % 2, :], axis=AX.X,
                                          op=ALU.add, apply_absolute_value=True
                                          ).then_inc(s_alq, 1)
                    eng.drain()
                    eng.tensor_reduce(out=al[:, 12:13], in_=al[:, 8:12],
                                      axis=AX.X, op=ALU.add)
                    eng.drain()
                    eng.tensor_scalar(out=al[:, 12:13], in0=al[:, 12:13],
                                      scalar1=1.0 / D, scalar2=None, op0=ALU.mult)
                    eng.drain()
                    eng.wait_ge(s_sg, 4 * (ot + 1))
                    eng.tensor_scalar(out=sgn_tmp[:, :], in0=sgn_tmp[:, :],
                                      scalar1=al[:, 12:13], scalar2=None,
                                      op0=ALU.mult)
                    eng.drain().then_inc(s_sc, 1)
                    if ot % 2 == 1:
                        band = ot // 2
                        for j in range(NTT):
                            mi = band * NTT + j
                            eng.wait_ge(s_mm, mi + 1)
                            if mi >= 2:
                                eng.wait_ge(s_dst[mi % 2], 16 * (mi // 2))
                            eng.tensor_copy(dsb[:, mi % 2, :],
                                            mmp[:, mi % 2, 0:256])
                            eng.drain().then_inc(s_de, 1)

            def b_scalar(eng):
                for ot in range(NOT_):
                    for q in range(4):
                        j = ot * 4 + q
                        eng.wait_ge(s_wld[j % 2], 16 * (j // 2 + 1))
                        if q == 0 and ot >= 1:
                            eng.wait_ge(s_tpk, 8 * ot)
                        eng.activation(sgn_tmp[:, q * 1024:(q + 1) * 1024],
                                       stage[:, j % 2, :], AF.Sign)
                        eng.drain().then_inc(s_sg, 1)
                    band, half = ot // 2, ot % 2
                    if band >= 2 and half == 0:
                        eng.wait_ge(s_mm, 16 * (band - 1))
                    for pk in range(8):
                        g = ot * 8 + pk
                        eng.wait_ge(s_tpk, g + 1)
                        eng.activation(wtb[:, band % 2, pk * 4:pk * 4 + 4,
                                           half * 128:(half + 1) * 128],
                                       tp[pk % 2][:, :, :], AF.Copy)
                        eng.drain().then_inc(s_te, 1)

            def b_tensor(eng):
                if load_xnt and not self.fused:
                    eng.wait_ge(s_ldb, 16 * NCH)
                eng.wait_ge(s_bld, 16)
                for ot in range(NOT_):
                    eng.wait_ge(s_sc, ot + 1)
                    for pk in range(8):
                        g = ot * 8 + pk
                        if g >= 2:
                            eng.wait_ge(s_te, g - 1)
                        for j in range(4):
                            ic = pk * 4 + j
                            ii = self.tpose(eng, tp[pk % 2][:, j, :],
                                            sgn_tmp[:, ic * 128:(ic + 1) * 128])
                        ii.then_inc(s_tpk, 1)
                        self.tpose_end(eng)
                    if ot % 2 == 1:
                        band = ot // 2
                        eng.wait_ge(s_te, 8 * (ot + 1))
                        for tt in range(NTT):
                            mi = band * NTT + tt
                            if mi >= 2:
                                eng.wait_ge(s_de, mi - 1)
                            p = mmp[:, mi % 2, 0:256]
                            for ic in range(NCH):
                                eng.matmul(p,
                                           self.big[:, ic, tt * 128:(tt + 1) * 128],
                                           wtb[:, band % 2, ic, :],
                                           start=(ic == 0), stop=False)
                            eng.matmul(p, self.onesrow,
                                       biasb[:, band * 256:(band + 1) * 256],
                                       start=False, stop=True).then_inc(s_mm, 1)

            self.emit_phase(dict(sync=b_sync, gpsimd=b_gpsimd, vector=b_vector,
                                 scalar=b_scalar, tensor=b_tensor))

    def phase_pf_moe(self, L, load_xnt=False):
        nc = self.nc
        ln = L["name"]
        A_in = self.dram_in(f"{ln}_A", [4, 16, D], F32)
        B_in = self.dram_in(f"{ln}_B", [4, D, 16], F32)
        Wr_in = self.dram_in(f"{ln}_Wr", [4, D], F32)
        delta_s = self.dram_tmp("delta_s", [S, D], F32, out=True)
        SCALE = 4.0

        AT = self.carve(self.arenaA, 0, [128, NCH, 64], BF16)
        BT = self.carve(self.arenaA, 4096, [64, NCH * 128], BF16)
        BT3 = self.carve(self.arenaA, 4096, [64, NCH, 128], BF16)
        WrT = self.carve(self.arenaA, 12288, [128, NCH, 4], BF16)
        A_sb = self.carve(self.arenaA, 12544, [64, D], BF16)
        wr_sb = self.carve(self.arenaA, 20736, [4, D], BF16)
        Bst = self.carve(self.arenaA, 28928, [128, 2, NCH, 32], BF16)
        mid_sb = self.carve(self.arenaA, 33024, [128, 2, 64], BF16)
        midT_sb = self.carve(self.arenaA, 33280, [64, 2, 128], BF16)
        gsc = self.carve(self.arenaA, 34304, [128, 2, 16], F32)
        dsb = self.carve(self.arenaB, 0, [128, 2, 512], F32)

        s_ldc = self.sem("pf0_ldc")
        s_stp = self.sem("pf0_stp")
        s_ste = self.sem("pf0_ste")
        s_rt = self.sem("pf0_rt")
        s_gt = self.sem("pf0_gt")
        s_gs = self.sem("pf0_gs")
        s_mi = self.sem("pf0_mi")
        s_me = self.sem("pf0_me")
        s_mt = self.sem("pf0_mt")
        s_mte = self.sem("pf0_mte")
        s_dm = self.sem("pf0_dm")
        s_dev = self.sem("pf0_dev")
        s_dst = [self.sem("pf0_dst"), self.sem("pf0_dst")]
        s_ldb = self.sem("pf0_ldb")

        NOG = 8
        with nc.psum_tensor(self.uname("pf0_rp"), [128, 4], F32) as rp_t, \
             nc.psum_tensor(self.uname("pf0_mid"), [128, 64], F32) as mid_t, \
             nc.psum_tensor(self.uname("pf0_tp"), [128, 128], BF16) as tp_t, \
             nc.psum_tensor(self.uname("pf0_tp2"), [128, 128], BF16) as tp2_t, \
             nc.psum_tensor(self.uname("pf0_dp"), [128, 2, 512], F32) as dp_t:
            rp = rp_t.ap()
            midp = mid_t.ap()
            tpb = [tp_t.ap(), tp2_t.ap()]
            dp = dp_t.ap()

            def b_gpsimd(eng):
                eng.dma_start(out=A_sb[:, :],
                              in_=A_in.rearrange("e r d -> (e r) d")
                              ).then_inc(s_ldc, 16)
                eng.dma_start(out=wr_sb[:, :], in_=Wr_in).then_inc(s_ldc, 16)
                for e in range(4):
                    eng.dma_start(out=Bst[:, e // 2, :, (e % 2) * 16:(e % 2) * 16 + 16],
                                  in_=B_in[e].rearrange("(ic p) r -> p ic r", p=128)
                                  ).then_inc(s_ldc, 16)

            def b_sync(eng):
                if load_xnt and not self.fused:
                    srcb = self.dram_tmp("xnt_s", [D, S], BF16, inp=True)
                    for ic in range(NCH):
                        eng.dma_start(out=self.big[:, ic, :],
                                      in_=srcb[ic * 128:(ic + 1) * 128, :]
                                      ).then_inc(s_ldb, 16)
                nst = [0, 0]
                for mi in range(NTT * NOG):
                    tt, og = divmod(mi, NOG)
                    eng.wait_ge(s_dev, mi + 1)
                    eng.dma_start(out=delta_s[tt * 128:(tt + 1) * 128,
                                              og * 512:(og + 1) * 512],
                                  in_=dsb[:, mi % 2, :]).then_inc(s_dst[mi % 2], 16)
                    nst[mi % 2] += 1
                eng.wait_ge(s_dst[0], 16 * nst[0])
                eng.wait_ge(s_dst[1], 16 * nst[1])

            def b_tensor(eng):
                if load_xnt and not self.fused:
                    eng.wait_ge(s_ldb, 16 * NCH)
                eng.wait_ge(s_ldc, 96)
                g = 0
                for ic in range(NCH):
                    if g >= 2:
                        eng.wait_ge(s_ste, g - 1)
                    ii = self.tpose(eng, tpb[g % 2][:, 0:64],
                                    A_sb[0:64, ic * 128:(ic + 1) * 128],
                                    ident=self.ident[0:64, 0:64])
                    ii.then_inc(s_stp, 1)
                    self.tpose_end(eng)
                    g += 1
                for ic in range(NCH):
                    if g >= 2:
                        eng.wait_ge(s_ste, g - 1)
                    ii = self.tpose(eng, tpb[g % 2][:, 0:4],
                                    wr_sb[0:4, ic * 128:(ic + 1) * 128],
                                    ident=self.ident[0:4, 0:4])
                    ii.then_inc(s_stp, 1)
                    self.tpose_end(eng)
                    g += 1
                for pair in range(2):
                    for ic in range(NCH):
                        if g >= 2:
                            eng.wait_ge(s_ste, g - 1)
                        ii = self.tpose(eng, tpb[g % 2][0:32, :],
                                        Bst[:, pair, ic, :])
                        ii.then_inc(s_stp, 1)
                        self.tpose_end(eng)
                        g += 1
                eng.wait_ge(s_ste, 128)
                for tt in range(NTT):
                    if tt >= 1:
                        eng.wait_ge(s_gt, tt)
                    for ic in range(NCH):
                        ii = eng.matmul(rp[:, :],
                                        self.big[:, ic, tt * 128:(tt + 1) * 128],
                                        WrT[:, ic, :],
                                        start=(ic == 0), stop=(ic == NCH - 1))
                    ii.then_inc(s_rt, 1)
                    if tt >= 1:
                        eng.wait_ge(s_me, tt)
                    for ic in range(NCH):
                        ii = eng.matmul(midp[:, :],
                                        self.big[:, ic, tt * 128:(tt + 1) * 128],
                                        AT[:, ic, :],
                                        start=(ic == 0), stop=(ic == NCH - 1))
                    ii.then_inc(s_mi, 1)
                    eng.wait_ge(s_me, tt + 1)
                    if tt >= 1:
                        eng.wait_ge(s_mte, tt)
                    ii = self.tpose(eng, tpb[1][0:64, :], mid_sb[:, tt % 2, :])
                    ii.then_inc(s_mt, 1)
                    self.tpose_end(eng)
                    eng.wait_ge(s_mte, tt + 1)
                    for og in range(NOG):
                        mi = tt * NOG + og
                        if mi >= 2:
                            eng.wait_ge(s_dev, mi - 1)
                        eng.matmul(dp[:, mi % 2, :], midT_sb[0:64, tt % 2, :],
                                   BT[0:64, og * 512:(og + 1) * 512],
                                   start=True, stop=True).then_inc(s_dm, 1)

            def b_scalar(eng):
                for g in range(128):
                    eng.wait_ge(s_stp, g + 1)
                    if g < 32:
                        eng.activation(AT[:, g, :], tpb[g % 2][0:128, 0:64],
                                       AF.Copy)
                    elif g < 64:
                        eng.activation(WrT[:, g - 32, :], tpb[g % 2][0:128, 0:4],
                                       AF.Copy)
                    else:
                        pair = (g - 64) // 32
                        ic = (g - 64) % 32
                        eng.activation(BT3[pair * 32:(pair + 1) * 32, ic, :],
                                       tpb[g % 2][0:32, 0:128], AF.Copy)
                    eng.drain().then_inc(s_ste, 1)
                for tt in range(NTT):
                    r = tt % 2
                    eng.wait_ge(s_gs, 4 * tt + 1)
                    eng.activation(gsc[:, r, 4:8], gsc[:, r, 0:4], AF.Sign)
                    eng.drain().then_inc(s_gs, 1)
                    eng.wait_ge(s_gs, 4 * tt + 3)
                    eng.activation(gsc[:, r, 8:12], gsc[:, r, 0:4], AF.Exp)
                    eng.drain().then_inc(s_gs, 1)

            def b_vector(eng):
                for tt in range(NTT):
                    r = tt % 2
                    lg = gsc[:, r, 0:4]
                    sm = self.smalls
                    eng.wait_ge(s_rt, tt + 1)
                    eng.tensor_copy(lg, rp[:, :])
                    eng.drain()
                    eng.tensor_reduce(out=sm[:, 16 + r:17 + r], in_=lg, axis=AX.X,
                                      op=ALU.max, negate=True)
                    eng.drain()
                    eng.tensor_scalar(out=lg, in0=lg, scalar1=sm[:, 16 + r:17 + r],
                                      scalar2=None, op0=ALU.add)
                    eng.drain().then_inc(s_gs, 1)
                    eng.wait_ge(s_gs, 4 * tt + 2)
                    eng.tensor_scalar(out=gsc[:, r, 4:8], in0=gsc[:, r, 4:8],
                                      scalar1=1.0, scalar2=-1e30,
                                      op0=ALU.add, op1=ALU.mult)
                    eng.drain()
                    eng.tensor_add(gsc[:, r, 12:16], lg, gsc[:, r, 4:8])
                    eng.drain()
                    eng.tensor_reduce(out=sm[:, 18 + r:19 + r], in_=gsc[:, r, 12:16],
                                      axis=AX.X, op=ALU.max)
                    eng.drain()
                    eng.tensor_scalar(out=gsc[:, r, 12:16], in0=lg,
                                      scalar1=sm[:, 18 + r:19 + r], scalar2=None,
                                      op0=ALU.is_ge)
                    eng.drain().then_inc(s_gs, 1)
                    eng.wait_ge(s_gs, 4 * tt + 4)
                    eng.tensor_mul(gsc[:, r, 4:8], gsc[:, r, 8:12],
                                   gsc[:, r, 12:16])
                    eng.drain()
                    eng.tensor_reduce(out=sm[:, 20 + r:21 + r], in_=gsc[:, r, 4:8],
                                      axis=AX.X, op=ALU.add)
                    eng.drain()
                    eng.reciprocal(sm[:, 22 + r:23 + r], sm[:, 20 + r:21 + r])
                    eng.drain()
                    eng.tensor_scalar(out=gsc[:, r, 8:12], in0=gsc[:, r, 4:8],
                                      scalar1=sm[:, 22 + r:23 + r], scalar2=SCALE,
                                      op0=ALU.mult, op1=ALU.mult)
                    eng.drain().then_inc(s_gt, 1)
                    eng.wait_ge(s_mi, tt + 1)
                    if tt >= 1:
                        eng.wait_ge(s_mt, tt)
                    for e in range(4):
                        eng.tensor_scalar(out=mid_sb[:, r, e * 16:(e + 1) * 16],
                                          in0=midp[:, e * 16:(e + 1) * 16],
                                          scalar1=gsc[:, r, 8 + e:9 + e],
                                          scalar2=None, op0=ALU.mult)
                    eng.drain().then_inc(s_me, 1)
                    eng.wait_ge(s_mt, tt + 1)
                    eng.tensor_copy(midT_sb[0:64, r, :], tpb[1][0:64, :])
                    eng.drain().then_inc(s_mte, 1)
                    for og in range(NOG):
                        mi = tt * NOG + og
                        eng.wait_ge(s_dm, mi + 1)
                        if mi >= 2:
                            eng.wait_ge(s_dst[mi % 2], 16 * (mi // 2))
                        eng.tensor_copy(dsb[:, mi % 2, :], dp[:, mi % 2, :])
                        eng.drain().then_inc(s_dev, 1)

            self.emit_phase(dict(sync=b_sync, gpsimd=b_gpsimd, tensor=b_tensor,
                                 scalar=b_scalar, vector=b_vector))


    def phase_pc(self, L, dump_o=False):
        nc = self.nc
        qt_s = self.dram_tmp("qt_s", [D, S], BF16, inp=True)
        kt_s = self.dram_tmp("kt_s", [D, S], BF16, inp=True)
        v_s = self.dram_tmp("v_s", [S, D], BF16, inp=True)
        n_q = L["n_q"]
        G = n_q // L["n_kv"]
        win = L["window"]
        NT = n_q * NTT

        qh = self.carve(self.arenaA, 0, [128, 2, S], BF16)
        kh = self.carve(self.arenaA, 8192, [128, 2, S], BF16)
        vh = self.carve(self.arenaA, 16384, [128, 2, 16, 132], BF16)
        probs = self.carve(self.arenaA, 24832, [128, 2, S], BF16)
        pt_sb = self.carve(self.arenaA, 33024, [128, 2, 16, 128], BF16)
        sm = self.smalls

        def geom(i):
            if win is None:
                lo = 0
            else:
                lo = max(0, i - 8)
            b0 = lo // 4
            nb = i // 4 - b0 + 1
            sbl = list(range(lo, i + 1))
            return lo, b0, nb, sbl

        packs_cum = [0]
        mask_cum = [0]
        sc_cum = [0]
        ex_cum = [0]
        nb_of = {}
        for t in range(NT):
            i = t % 16
            lo, b0, nb, sbl = geom(i)
            nb_of[t] = nb
            packs_cum.append(packs_cum[-1] + (len(sbl) + 3) // 4)
            sc_cum.append(sc_cum[-1] + nb)
            ex_cum.append(ex_cum[-1] + nb)
            nm = 1 + (1 if (win is not None and i >= 8) else 0)
            mask_cum.append(mask_cum[-1] + nm)
        def masked_blocks(i):
            lo, b0, nb, sbl = geom(i)
            mb = {sbl.index(i)}
            if win is not None and i >= 8:
                mb.add(sbl.index(i - 8))
            return mb

        def pack_engine(i, pk):
            lo, b0, nb, sbl = geom(i)
            mb = masked_blocks(i)
            if any((j in mb) for j in range(pk * 4, min(pk * 4 + 4, len(sbl)))):
                return "B"
            return "A" if pk % 2 == 1 else "B"

        packsA_cum = [0]
        packsB_cum = [0]
        packsA_cum_g = [0]
        packsB_cum_g = [0]
        for t in range(NT):
            i = t % 16
            lo, b0, nb, sbl = geom(i)
            npk = (len(sbl) + 3) // 4
            a = sum(1 for pk in range(npk) if pack_engine(i, pk) == "A")
            b = npk - a
            packsA_cum.append(packsA_cum[-1] + a)
            packsB_cum.append(packsB_cum[-1] + b)
            for pk in range(npk):
                isA = pack_engine(i, pk) == "A"
                packsA_cum_g.append(packsA_cum_g[-1] + (1 if isA else 0))
                packsB_cum_g.append(packsB_cum_g[-1] + (0 if isA else 1))

        s_q = [self.sem("pc_q"), self.sem("pc_q")]
        s_k = [self.sem("pc_k"), self.sem("pc_k")]
        s_hd = self.sem("pc_hd")
        s_kv = self.sem("pc_kv")
        s_sc = self.sem("pc_sc")
        s_mx = self.sem("pc_mx")
        s_ex = self.sem("pc_ex")
        s_ptk = self.sem("pc_ptk")
        s_pteA = self.sem("pc_pteA")
        s_pteB = self.sem("pc_pteB")
        s_ptwA = self.sem("pc_ptwA")
        s_ptwB = self.sem("pc_ptwB")
        s_ptm = self.sem("pc_ptm")
        s_pv = self.sem("pc_pv")
        s_oev = self.sem("pc_oev")
        s_ost = self.sem("pc_ost")
        s_vset = self.sem("pc_vset")

        def kv_of(h):
            return h // G

        def kvslot(h):
            return (kv_of(h) % 2) if win is not None else (h % 2)

        qrounds = {h: 16 * (h // 2 + 1) for h in range(n_q)}
        kvrounds = {}
        for h in range(n_q):
            kvh = kv_of(h) if win is not None else h
            kvrounds[h] = 16 * 17 * (kvh // 2 + 1)

        with nc.psum_tensor(self.uname("pc_sc"), [128, 4, 512], F32) as sc_t, \
             nc.psum_tensor(self.uname("pc_ptA"), [128, 4, 128], BF16) as ptA, \
             nc.psum_tensor(self.uname("pc_ptB"), [128, 4, 128], BF16) as ptB, \
             nc.psum_tensor(self.uname("pc_oA"), [128, 132], F32) as oA, \
             nc.psum_tensor(self.uname("pc_oB"), [128, 132], F32) as oB:
            scp = sc_t.ap().rearrange("p a b -> p (a b)")
            ptp = [ptA.ap(), ptB.ap()]
            op = [oA.ap(), oB.ap()]

            def b_sync(eng):
                for h in range(n_q):
                    if h >= 2:
                        eng.wait_ge(s_hd, h - 1)
                    eng.dma_start(out=qh[:, h % 2, :],
                                  in_=qt_s[h * 128:(h + 1) * 128, :]
                                  ).then_inc(s_q[h % 2], 16)
                    if win is None or h % G == 0:
                        kvh = kv_of(h)
                        slot = kvslot(h)
                        if win is not None and kvh >= 2:
                            eng.wait_ge(s_kv, kvh - 1)
                        krow = (kvh if win is not None else h) * 128
                        dv0 = kvh * 128 if win is not None else h * 128
                        eng.dma_start(out=kh[:, slot, :],
                                      in_=kt_s[krow:krow + 128, :]
                                      ).then_inc(s_k[slot], 16)
                        eng.wait_ge(s_vset, 1)
                        for sb in range(16):
                            eng.dma_start(out=vh[:, slot, sb, 0:128],
                                          in_=v_s[sb * 128:(sb + 1) * 128,
                                                  dv0:dv0 + 128]
                                          ).then_inc(s_k[slot], 16)

            def b_gpsimd(eng):
                eng.memset(vh[:, 0, :, 128:129], 1.0)
                eng.memset(vh[:, 1, :, 128:129], 1.0)
                eng.drain().then_inc(s_vset, 1)

            def b_tensor(eng):
                gpk = 0
                for t in range(NT + 2):
                    if t < NT:
                        h, i = divmod(t, 16)
                        lo, b0, nb, sbl = geom(i)
                        if i == 0:
                            eng.wait_ge(s_q[h % 2], qrounds[h])
                            eng.wait_ge(s_k[kvslot(h)], kvrounds[h])
                        for b in range(nb):
                            if t >= 1:
                                eng.wait_ge(s_ex, ex_cum[t - 1] + min(b + 1, nb_of[t - 1]))
                            eng.matmul(
                                scp[:, b * 512:(b + 1) * 512],
                                qh[:, h % 2, i * 128:(i + 1) * 128],
                                kh[:, kvslot(h), (b0 + b) * 512:(b0 + b + 1) * 512],
                                start=True, stop=True).then_inc(s_sc, 1)
                    u = t - 1
                    if 0 <= u < NT:
                        h1, i1 = divmod(u, 16)
                        lo1, b01, nb1, sbl1 = geom(i1)
                        npk = (len(sbl1) + 3) // 4
                        eng.wait_ge(s_ex, u + 1)
                        for pk in range(npk):
                            g = packs_cum[u] + pk
                            if g >= 2:
                                eng.wait_ge(s_ptwA, packsA_cum_g[g - 1])
                                eng.wait_ge(s_ptwB, packsB_cum_g[g - 1])
                            blocks = sbl1[pk * 4:pk * 4 + 4]
                            needed_exp = max((sb - b01 * 4) // 4 for sb in blocks)
                            eng.wait_ge(s_ex, ex_cum[u] + needed_exp + 1)
                            for j, sb in enumerate(blocks):
                                lb = sb - b01 * 4
                                ii = self.tpose(
                                    eng, ptp[g % 2][:, j, :],
                                    probs[:, u % 2, lb * 128:(lb + 1) * 128])
                            ii.then_inc(s_ptk, 1)
                            self.tpose_end(eng)
                        eng.wait_ge(s_pteA, packsA_cum[u + 1])
                        eng.wait_ge(s_pteB, packsB_cum[u + 1])
                        if u >= 2:
                            eng.wait_ge(s_oev, u - 1)
                        for j, sb in enumerate(sbl1):
                            ii = eng.matmul(op[u % 2][:, 0:129],
                                            pt_sb[:, u % 2, j, :],
                                            vh[:, kvslot(h1), sb, 0:129],
                                            start=(j == 0),
                                            stop=(j == len(sbl1) - 1))
                        ii.then_inc(s_pv, 1)
                        if i1 == 15:
                            eng.drain().then_inc(s_hd, 1)
                            if win is not None and (h1 % G) == G - 1:
                                eng.nop().then_inc(s_kv, 1)

            def b_vector(eng):
                for t in range(NT + 2):
                    if t < NT:
                        h, i = divmod(t, 16)
                        lo, b0, nb, sbl = geom(i)
                        for b in range(nb):
                            eng.wait_ge(s_sc, sc_cum[t] + b + 1)
                            eng.tensor_reduce(out=sm[:, 32 + b:33 + b],
                                              in_=scp[:, b * 512:(b + 1) * 512],
                                              axis=AX.X, op=ALU.max, negate=True)
                        eng.drain()
                        eng.tensor_reduce(out=sm[:, 26 + t % 2:27 + t % 2],
                                          in_=sm[:, 32:32 + nb], axis=AX.X,
                                          op=ALU.min)
                        eng.drain().then_inc(s_mx, 1)
                    u2 = t - 2
                    if 0 <= u2 < NT:
                        eng.wait_ge(s_pv, u2 + 1)
                        eng.tensor_copy(sm[:, 28 + t % 2:29 + t % 2],
                                        op[u2 % 2][:, 128:129])
                    u = t - 1
                    if 0 <= u < NT:
                        h1, i1 = divmod(u, 16)
                        lo1, b01, nb1, sbl1 = geom(i1)
                        npk = (len(sbl1) + 3) // 4
                        mb1 = masked_blocks(i1)
                        nmine = 0
                        for pk in range(npk):
                            if pack_engine(i1, pk) != "B":
                                continue
                            g = packs_cum[u] + pk
                            eng.wait_ge(s_ptk, g + 1)
                            cntb = len(sbl1[pk * 4:pk * 4 + 4])
                            pkmb = [j for j in range(pk * 4, pk * 4 + cntb)
                                    if j in mb1]
                            if not pkmb:
                                eng.tensor_copy(
                                    pt_sb[:, u % 2, pk * 4:pk * 4 + cntb, :],
                                    ptp[g % 2][:, 0:cntb, :]).then_inc(s_ptwB, 1)
                            else:
                                ii = None
                                for j in range(pk * 4, pk * 4 + cntb):
                                    jj = j - pk * 4
                                    if j in pkmb:
                                        mask = (self.mask_diag
                                                if sbl1[j] == i1 else self.mask_left)
                                        ii = eng.tensor_mul(
                                            pt_sb[:, u % 2, j, :],
                                            ptp[g % 2][:, jj, :], mask)
                                    else:
                                        ii = eng.tensor_copy(
                                            pt_sb[:, u % 2, j, :],
                                            ptp[g % 2][:, jj, :])
                                ii.then_inc(s_ptwB, 1)
                            nmine += 1
                            if nmine == 1:
                                eng.drain().then_inc(s_pteB, 1)
                        if nmine > 1:
                            eng.drain().then_inc(s_pteB, nmine - 1)

                    if 0 <= u2 < NT:
                        h2, i2 = divmod(u2, 16)
                        eng.reciprocal(sm[:, 30 + t % 2:31 + t % 2],
                                       sm[:, 28 + t % 2:29 + t % 2])
                        eng.drain()
                        eng.tensor_scalar(out=self.big[:, h2, i2 * 128:(i2 + 1) * 128],
                                          in0=op[u2 % 2][:, 0:128],
                                          scalar1=sm[:, 30 + t % 2:31 + t % 2],
                                          scalar2=None, op0=ALU.mult)
                        eng.drain().then_inc(s_oev, 1)

            def b_scalar(eng):
                for t in range(NT + 1):
                    if t < NT:
                        h, i = divmod(t, 16)
                        lo, b0, nb, sbl = geom(i)
                        eng.wait_ge(s_mx, t + 1)
                        eng.activation(probs[:, t % 2, 0:nb * 512],
                                       scp[:, 0:nb * 512], AF.Exp,
                                       bias=sm[:, 26 + t % 2:27 + t % 2])
                        eng.drain().then_inc(s_ex, nb)
                    u = t - 1
                    if 0 <= u < NT:
                        h1, i1 = divmod(u, 16)
                        lo1, b01, nb1, sbl1 = geom(i1)
                        npk = (len(sbl1) + 3) // 4
                        nmine = 0
                        for pk in range(npk):
                            if pack_engine(i1, pk) != "A":
                                continue
                            g = packs_cum[u] + pk
                            eng.wait_ge(s_ptk, g + 1)
                            cntb = len(sbl1[pk * 4:pk * 4 + 4])
                            eng.activation(pt_sb[:, u % 2, pk * 4:pk * 4 + cntb, :],
                                           ptp[g % 2][:, 0:cntb, :], AF.Copy
                                           ).then_inc(s_ptwA, 1)
                            nmine += 1
                            if nmine == 1:
                                eng.drain().then_inc(s_pteA, 1)
                        if nmine > 1:
                            eng.drain().then_inc(s_pteA, nmine - 1)
                if dump_o and not self.fused:
                    o_s = self.dram_tmp("o_s", [D, S], BF16, out=True)
                    eng.wait_ge(s_oev, NT)
                    for ic in range(NCH):
                        eng.dma_start(out=o_s[ic * 128:(ic + 1) * 128, :],
                                      in_=self.big[:, ic, :]).then_inc(s_ost, 16)
                    eng.wait_ge(s_ost, 16 * NCH)

            self.emit_phase(dict(sync=b_sync, gpsimd=b_gpsimd, tensor=b_tensor,
                                 vector=b_vector, scalar=b_scalar))


    def phase_pd(self, L, resid_name, out_name, out_kind, load_o=False):
        nc = self.nc
        ln = L["name"]
        wo = self.dram_in(f"{ln}_wo", [D, D], F32)
        delta_s = self.dram_tmp("delta_s", [S, D], F32, inp=True)
        resid = (self.dram_in(resid_name, [S, D], F32) if resid_name == "xin"
                 else self.dram_tmp(resid_name, [S, D], F32))
        if out_kind == "out":
            yout = self.dram_out(out_name, [S, D], F32)
        else:
            yout = self.dram_tmp(out_name, [S, D], F32, out=True)

        GW = 256
        wbuf = self.carve(self.arenaA, 0, [128, 2, NCH, GW], BF16)
        xd = self.carve(self.arenaA, 32768, [128, 4, 2, GW], F32)
        y_sb = self.carve(self.arenaA, 40960, [128, 2, GW], F32)
        ytmp = self.carve(self.arenaA, 43008, [128, GW], F32)
        stage = self.carve(self.arenaB, 0, [128, 4, GW], F32)

        s_wl = [self.sem("pd_wl") for _ in range(4)]
        s_wc = self.sem("pd_wc")
        s_wcf = self.sem("pd_wcf")
        s_peg = self.sem("pd_peg")
        s_xd = [self.sem("pd_xd") for _ in range(4)]
        s_mm = self.sem("pd_mm")
        s_ye = self.sem("pd_ye")
        s_yst = [self.sem("pd_yst"), self.sem("pd_yst")]
        s_tp = self.sem("pd_tp")
        s_pte = self.sem("pd_pte")
        s_ldo = self.sem("pd_ldo")

        NG = D // GW

        with nc.psum_tensor(self.uname("pd_tpA"), [128, 4, 128], BF16) as tpA, \
             nc.psum_tensor(self.uname("pd_tpB"), [128, 4, 128], BF16) as tpB, \
             nc.psum_tensor(self.uname("pd_mm"), [128, 4, 512], F32) as mm_t:
            tp = [tpA.ap(), tpB.ap()]
            mmp = mm_t.ap()

            def b_sync(eng):
                if load_o and not self.fused:
                    srcb = self.dram_tmp("o_s", [D, S], BF16, inp=True)
                    for ic in range(NCH):
                        eng.dma_start(out=self.big[:, ic, :],
                                      in_=srcb[ic * 128:(ic + 1) * 128, :]
                                      ).then_inc(s_ldo, 16)
                for og in range(NG):
                    for ic in range(NCH):
                        j = og * NCH + ic
                        if j >= 4:
                            eng.wait_ge(s_wcf, j - 3)
                        eng.dma_start(out=stage[:, j % 4, :],
                                      in_=wo[ic * 128:(ic + 1) * 128,
                                             og * GW:(og + 1) * GW]
                                      ).then_inc(s_wl[j % 4], 16)

            def b_gpsimd(eng):
                for j in range(NG * NCH):
                    gi, ic = divmod(j, NCH)
                    if gi >= 2 and ic == 0:
                        eng.wait_ge(s_peg, gi - 1)
                    eng.wait_ge(s_wl[j % 4], 16 * (j // 4 + 1))
                    eng.tensor_copy(wbuf[:, gi % 2, ic, :],
                                    stage[:, j % 4, :]).then_inc(s_wcf, 1)
                    if ic == NCH - 1:
                        eng.drain().then_inc(s_wc, 1)

            def b_tensor(eng):
                if load_o and not self.fused:
                    eng.wait_ge(s_ldo, 16 * NCH)
                for tt in range(NTT):
                    for p in range(8):
                        g = tt * 8 + p
                        if g >= 2:
                            eng.wait_ge(s_pte, g - 1)
                        for j in range(4):
                            ic = p * 4 + j
                            ii = self.tpose(eng, tp[p % 2][:, j, :],
                                            self.big[:, ic, tt * 128:(tt + 1) * 128])
                        ii.then_inc(s_tp, 1)
                        self.tpose_end(eng)
                for og in range(NG):
                    eng.wait_ge(s_wc, og + 1)
                    eng.wait_ge(s_pte, NTT * 8)
                    for tt in range(NTT):
                        mi = og * NTT + tt
                        if mi >= 4:
                            eng.wait_ge(s_ye, mi - 3)
                        p = mmp[:, mi % 4, 0:GW]
                        for ic in range(NCH):
                            ii = eng.matmul(p,
                                            self.big[:, ic, tt * 128:(tt + 1) * 128],
                                            wbuf[:, og % 2, ic, :],
                                            start=(ic == 0), stop=(ic == NCH - 1))
                        ii.then_inc(s_mm, 1)
                    eng.drain().then_inc(s_peg, 1)

            def b_scalar(eng):
                for tt in range(NTT):
                    for p in range(8):
                        g = tt * 8 + p
                        eng.wait_ge(s_tp, g + 1)
                        eng.activation(
                            self.big[:, p * 4:p * 4 + 4, tt * 128:(tt + 1) * 128],
                            tp[p % 2][:, :, :], AF.Copy)
                        eng.drain().then_inc(s_pte, 1)
                nst = [0, 0]
                for mi in range(NG * NTT):
                    og, tt = divmod(mi, NTT)
                    if mi >= 4:
                        eng.wait_ge(s_ye, mi - 3)
                    eng.dma_start(out=xd[:, mi % 4, 0, :],
                                  in_=resid[tt * 128:(tt + 1) * 128,
                                            og * GW:(og + 1) * GW]
                                  ).then_inc(s_xd[mi % 4], 16)
                    eng.dma_start(out=xd[:, mi % 4, 1, :],
                                  in_=delta_s[tt * 128:(tt + 1) * 128,
                                              og * GW:(og + 1) * GW]
                                  ).then_inc(s_xd[mi % 4], 16)
                    if mi >= 2:
                        pm = mi - 2
                        pog, ptt = divmod(pm, NTT)
                        eng.wait_ge(s_ye, pm + 1)
                        eng.dma_start(out=yout[ptt * 128:(ptt + 1) * 128,
                                               pog * GW:(pog + 1) * GW],
                                      in_=y_sb[:, pm % 2, :]
                                      ).then_inc(s_yst[pm % 2], 16)
                        nst[pm % 2] += 1
                for pm in range(NG * NTT - 2, NG * NTT):
                    pog, ptt = divmod(pm, NTT)
                    eng.wait_ge(s_ye, pm + 1)
                    eng.dma_start(out=yout[ptt * 128:(ptt + 1) * 128,
                                           pog * GW:(pog + 1) * GW],
                                  in_=y_sb[:, pm % 2, :]
                                  ).then_inc(s_yst[pm % 2], 16)
                    nst[pm % 2] += 1
                eng.wait_ge(s_yst[0], 16 * nst[0])
                eng.wait_ge(s_yst[1], 16 * nst[1])

            def b_vector(eng):
                for mi in range(NG * NTT):
                    eng.wait_ge(s_mm, mi + 1)
                    eng.wait_ge(s_xd[mi % 4], 32 * (mi // 4 + 1))
                    if mi >= 2:
                        eng.wait_ge(s_yst[mi % 2], 16 * (mi // 2))
                    eng.scalar_tensor_tensor(out=ytmp[:, :], in0=mmp[:, mi % 4, 0:GW],
                                             scalar=1.0, in1=xd[:, mi % 4, 0, :],
                                             op0=ALU.mult, op1=ALU.add)
                    eng.drain()
                    eng.tensor_add(y_sb[:, mi % 2, :], ytmp[:, :],
                                   xd[:, mi % 4, 1, :])
                    eng.drain().then_inc(s_ye, 1)

            self.emit_phase(dict(sync=b_sync, gpsimd=b_gpsimd, tensor=b_tensor,
                                 scalar=b_scalar, vector=b_vector))



def build_prog(phases, fused):
    B = Builder(fused)
    B.alloc_fixed()
    with B.es:
        B.phase_consts()
        for (pname, L, kw) in phases:
            getattr(B, "phase_" + pname)(**kw) if L is None else \
                getattr(B, "phase_" + pname)(L, **kw)
    return B


_prog_cache = {}


def _layer_prog(layer):
    key = layer["name"]
    if key in _prog_cache:
        return _prog_cache[key]
    B = Builder(fused=True)
    B.alloc_fixed()
    with B.es:
        B.phase_consts()
        src = B.dram_in("xin", [S, D], F32)
        B.phase_pa(src)
        B.phase_pb(layer)
        if layer["ffn"] == "moe":
            B.phase_pf_moe(layer)
        else:
            B.phase_pf_bin(layer)
        B.phase_pc(layer)
        B.phase_pd(layer, "xin", "yout", "out")
    _prog_cache[key] = B.nc
    return B.nc


def _full_prog():
    if "full" in _prog_cache:
        return _prog_cache["full"]
    B = Builder(fused=True)
    B.alloc_fixed()
    with B.es:
        B.phase_consts()
        src = B.dram_in("xin", [S, D], F32)
        B.phase_pa(src)
        B.phase_pb(L0)
        B.phase_pf_moe(L0)
        B.phase_pc(L0)
        B.phase_pd(L0, "xin", "resid_s", "tmpout")
        B.phase_pa(B.dram["resid_s"])
        B.phase_pb(L1)
        B.phase_pf_bin(L1)
        B.phase_pc(L1)
        B.phase_pd(L1, "resid_s", "yout", "out")
    _prog_cache["full"] = B.nc
    return B.nc


def _weight_maps(inputs):
    consts = host_consts()
    perm = perm_interleave()
    f32c = lambda a: np.ascontiguousarray(np.asarray(a), dtype=np.float32)
    w0 = {
        **consts,
        "l0_wq": f32c(inputs["L0_attn_wq"]),
        "l0_wk": f32c(inputs["L0_attn_wk"]),
        "l0_wv": f32c(inputs["L0_attn_wv"]),
        "l0_wo": f32c(inputs["L0_attn_wo"]),
        "l0_A": f32c(inputs["L0_ffn_A"]),
        "l0_B": f32c(inputs["L0_ffn_B"]),
        "l0_Wr": f32c(inputs["L0_ffn_Wr"]),
    }
    w1 = {
        **consts,
        "l1_wq": f32c(np.asarray(inputs["L1_attn_wq"])[:, perm]),
        "l1_wk": f32c(np.asarray(inputs["L1_attn_wk"])[:, perm]),
        "l1_wv": f32c(inputs["L1_attn_wv"]),
        "l1_wo": f32c(inputs["L1_attn_wo"]),
        "l1_wreal": f32c(inputs["L1_ffn_weight_real"]),
        "l1_bias": f32c(np.asarray(inputs["L1_ffn_bias"]).reshape(1, D)),
    }
    return w0, w1


def kernel(**inputs):
    x = np.ascontiguousarray(np.asarray(inputs["L0_attn_x"], dtype=np.float32))
    w0, w1 = _weight_maps(inputs)
    w = dict(w0)
    w.update({k: v for k, v in w1.items()})
    p = _full_prog()
    im = [dict(w, xin=np.ascontiguousarray(x[c])) for c in range(8)]
    r = run_bass_kernel_spmd(p, im, core_ids=list(range(8)))
    return np.stack([r.results[c]["yout"] for c in range(8)], axis=0
                    ).astype(np.float32)
